# version 15
# baseline (speedup 1.0000x reference)
import numpy as np
import ml_dtypes
from contextlib import ExitStack

import concourse.bass as bass
import concourse.mybir as mybir
from concourse.bass_utils import run_bass_kernel_spmd

F32 = mybir.dt.float32
BF16 = mybir.dt.bfloat16
AF = mybir.ActivationFunctionType
ALU = mybir.AluOpType
AX = mybir.AxisListType

N_CORES = 8
B = 16
S = 2048
D = 2048
B_LOC = B // N_CORES
NT = S // 128
GT = 8
NG = NT // GT
EXP_SHIFT_A = -4.0
SCALE_A = 0.125
SCALE_B = float((128 + 64) ** -0.5)
LN_EPS = 1e-5
RMS_EPS = 1e-6



ENGS = ("pe", "dve", "act", "gpsimd", "sp")


class Prog:

    def __init__(self):
        self.ops = []
        self.last_writer = {}
        self.readers = {}
        self.last_reader = {}
        self.fence_deps = []

    def op(self, eng, fn, reads=(), writes=(), kind="c", deps=(),
           dsem=None, dmode="slot"):
        opid = len(self.ops)
        d = set(deps)
        d.update(self.fence_deps)
        for r in reads:
            w = self.last_writer.get(r)
            if w is not None:
                d.add(w)
            if r.startswith("ps"):
                lr = self.last_reader.get(r)
                if lr is not None and lr[1] != eng:
                    d.add(lr[0])
                self.last_reader[r] = (opid, eng)
        for w in writes:
            lw = self.last_writer.get(w)
            if lw is not None:
                d.add(lw)
            d.update(self.readers.get(w, ()))
        d.discard(opid)
        self.ops.append(dict(eng=eng, kind=kind, fn=fn, deps=d, id=opid,
                             dsem=dsem, dmode=dmode))
        for r in reads:
            self.readers.setdefault(r, set()).add(opid)
        for w in writes:
            self.last_writer[w] = opid
            self.readers[w] = set()
            if w.startswith("ps"):
                self.last_reader.pop(w, None)
        return opid

    def dma(self, q, out, in_, reads=(), writes=(), dsem=None, dmode="slot"):
        if dsem is None:
            assert writes, "dma needs dsem or a written slot resource"
            dsem = "D_" + writes[0]
        def fn(h, out=out, in_=in_):
            return h.dma_start(out=out, in_=in_)
        return self.op(q, fn, reads=reads, writes=writes, kind="d",
                       dsem=dsem, dmode=dmode)

    def fence(self):
        last = {}
        for o in self.ops:
            if o["kind"] == "d":
                last[("dsem", o["dsem"])] = o["id"]
            else:
                last[(o["eng"], o["kind"])] = o["id"]
        self.fence_deps = list(last.values())
        self.last_writer.clear()
        self.readers.clear()
        self.last_reader.clear()

    @staticmethod
    def _fifo_safe(prod, cons):
        return (prod["eng"] == cons["eng"] == "pe"
                and prod["kind"] == "c" and cons["kind"] == "c")

    def emit(self, nc, strict=False):
        ops = self.ops
        needs = [False] * len(ops)
        for o in ops:
            for d in o["deps"]:
                dd = ops[d]
                if strict or not self._fifo_safe(dd, o):
                    needs[d] = True
        last = {}
        for o in ops:
            last[(o["eng"], o["kind"])] = o["id"]
        for opid in last.values():
            needs[opid] = True

        csem_names = sorted({(o["eng"], o["kind"]) for o in ops
                             if o["kind"] == "c"})
        dsem_info = {}
        dmaval = {}
        for o in ops:
            if o["kind"] == "d":
                di = dsem_info.setdefault(o["dsem"],
                                          dict(count=0, mode=o["dmode"],
                                               ids=[]))
                assert di["mode"] == o["dmode"], o["dsem"]
                di["count"] += 1
                di["ids"].append(o["id"])
                dmaval[o["id"]] = 16 * di["count"]

        with ExitStack() as ctx:
            sems = {sn: ctx.enter_context(nc.semaphore(f"s_{sn[0]}_{sn[1]}"))
                    for sn in csem_names}
            dsems = {dn: ctx.enter_context(nc.semaphore(f"d_{dn}"))
                     for dn in dsem_info}
            counts = {sn: 0 for sn in csem_names}
            cum = {}
            for o in ops:
                if o["kind"] == "c" and needs[o["id"]]:
                    sn = (o["eng"], o["kind"])
                    counts[sn] += 1
                    cum[o["id"]] = (sn, counts[sn])
            finals = dict(counts)

            def dma_wait_val(dep_id, waiter_id):
                o = ops[dep_id]
                di = dsem_info[o["dsem"]]
                if di["mode"] == "slot":
                    return dmaval[dep_id]
                n = sum(1 for i in di["ids"] if i < waiter_id)
                return 16 * n

            block = ctx.enter_context(nc.Block())
            eng_ops = {e: [o for o in ops if o["eng"] == e] for e in ENGS}
            all_sem_keys = list(csem_names) + list(dsems)
            vclock = {e: {k: 0 for k in all_sem_keys} for e in ENGS}

            def run_engine(e, handle):
                vc = vclock[e]
                for o in eng_ops[e]:
                    waits = {}
                    for d in o["deps"]:
                        dd = ops[d]
                        if not strict and self._fifo_safe(dd, o):
                            continue
                        if dd["kind"] == "d":
                            k = dd["dsem"]
                            v = dma_wait_val(d, o["id"])
                        else:
                            if d not in cum:
                                continue
                            k, v = cum[d]
                        if vc[k] < v:
                            waits[k] = max(waits.get(k, 0), v)
                    for k, v in sorted(waits.items(), key=str):
                        handle.wait_ge(sems[k] if k in sems else dsems[k], v)
                        vc[k] = v
                    inst = o["fn"](handle)
                    if o["kind"] == "d":
                        inst.then_inc(dsems[o["dsem"]], 16)
                    elif o["id"] in cum:
                        sn, _ = cum[o["id"]]
                        inst.then_inc(sems[sn], 1)

            @block.tensor
            def _(te):
                run_engine("pe", te)

            @block.vector
            def _(v):
                run_engine("dve", v)

            @block.scalar
            def _(sc):
                run_engine("act", sc)

            @block.gpsimd
            def _(gp):
                run_engine("gpsimd", gp)

            @block.sync
            def _(sy):
                run_engine("sp", sy)
                for sn in csem_names:
                    if finals[sn] > vclock["sp"][sn]:
                        sy.wait_ge(sems[sn], finals[sn])
                for dn, di in dsem_info.items():
                    if 16 * di["count"] > vclock["sp"][dn]:
                        sy.wait_ge(dsems[dn], 16 * di["count"])



class Builder:
    def __init__(self, n_seq=B_LOC, subs=(0, 1, 2, 3), debug_taps=False):
        self.n_seq = n_seq
        self.subs = tuple(subs)
        self.debug_taps = debug_taps
        self.nc = bass.Bass("TRN2", target_bir_lowering=False)
        self.p = Prog()
        self._build()

    def _build(self):
        nc = self.nc
        p = self.p
        ns = self.n_seq

        def din(name, shape):
            return nc.dram_tensor(name, list(shape), F32,
                                  kind="ExternalInput").ap()

        x_in = din("x_in", [ns, S, D])
        y_out = nc.dram_tensor("y", [ns, S, D], F32, kind="ExternalOutput").ap()
        xA = nc.dram_tensor("xA", [ns, S, D], F32, kind="Internal").ap()
        xB = nc.dram_tensor("xB", [ns, S, D], F32, kind="Internal").ap()
        tapkind = "ExternalOutput" if self.debug_taps else "Internal"
        qspill = nc.dram_tensor("qspill", [ns, 16, 128, S], BF16,
                                kind="Internal").ap()
        qpespill = nc.dram_tensor("qpespill", [8, 128, S], BF16,
                                  kind=tapkind).ap()
        aofspill = nc.dram_tensor("aofspill", [16, 128, S], BF16,
                                  kind=tapkind).ap()
        if self.debug_taps:
            qantap = nc.dram_tensor("qantap", [12, 128, S], BF16,
                                    kind="ExternalOutput").ap()
            ckvtap = nc.dram_tensor("ckvtap", [4, 128, S], BF16,
                                    kind="ExternalOutput").ap()
            kpetap = nc.dram_tensor("kpetap", [128, S], BF16,
                                    kind="ExternalOutput").ap()
            self._taps = (qantap, ckvtap, kpetap)

        wq = din("wq", [4, 16, 128, 512])
        wk = din("wk", [16, 128, 512])
        wv = din("wv", [16, 128, 512])
        memw_d = din("memw", [1, 1])
        mem_d = din("mem", [4, 128, 512])
        memn_d = din("memn", [4, 128, 1])
        wqa = din("wqa", [16, 128, 1536])
        lnwq_d = din("lnwq", [128, 12])
        wqbn = din("wqbn", [16, 128, 12, 128])
        wqbp = din("wqbp", [12, 128, 1024])
        wkva5 = din("wkva5", [16, 128, 512])
        wkvap = din("wkvap", [16, 128, 64])
        lnwk_d = din("lnwk", [128, 4])
        wkvbn = din("wkvbn", [16, 4, 128, 128])
        wkvbv = din("wkvbv", [4, 128, 2048])
        wo_d = din("wo", [16, 128, 2048])
        wb_d = din("wb", [16, 128, 2048])
        ha_d = din("ha", [16, 128, 72])
        bb_d = din("bb", [1, 2048])
        bflat_d = din("bflat", [64, 2048])
        cos_d = din("cos", [16, 128, 64])
        sin_d = din("sin", [16, 128, 64])

        ctx = ExitStack()
        sb = lambda name, shape, dt: ctx.enter_context(
            nc.sbuf_tensor(name, shape, dt))
        AR1 = sb("AR1", [128, 32768], BF16)
        AR2 = sb("AR2", [128, 32768], BF16)
        AR3 = sb("AR3", [128, 10240], F32)
        AR4 = sb("AR4", [128, 8192], BF16)
        AR5 = sb("AR5", [128, 3072], F32)
        IDENT = sb("IDENT", [128, 128], BF16)
        ONES = sb("ONES", [128, 128], BF16)
        ONESF = sb("ONESF", [1, 128], F32)
        COS = sb("COS", [128, 16, 64], F32)
        SIN = sb("SIN", [128, 16, 64], F32)
        STAT = sb("STAT", [128, 64], F32)
        PSD = [ctx.enter_context(nc.psum_tensor(f"psd{i}", [128, 1024], F32))
               for i in range(2)]
        PS_hi = [ctx.enter_context(nc.psum_tensor(f"ps{i}", [128, 512], F32))
                 for i in range(4, 8)]

        class _PSView:
            def __init__(self, ap_):
                self._ap = ap_

            def ap(self):
                return self._ap

        PS = ([_PSView(PSD[i // 2].ap()[:, (i % 2) * 512:(i % 2 + 1) * 512])
               for i in range(4)] + PS_hi)

        def psb(i):
            if i < 4:
                return PSD[i // 2].ap().bitcast(BF16)[
                    :, (i % 2) * 1024:(i % 2 + 1) * 1024]
            return PS[i].ap().bitcast(BF16)

        st = STAT.ap()
        xt_ring = [AR3.ap()[:, i * 2048:(i + 1) * 2048] for i in range(3)]

        p.op("dve", lambda h: h.memset(ONES.ap(), 1.0), writes=["ONES"])
        p.op("dve", lambda h: h.memset(ONESF.ap(), 1.0), writes=["ONESF"])
        p.op("gpsimd", lambda h: h.affine_select(
            IDENT.ap(), ONES.ap(), [[1, 128]], ALU.is_equal, 0.0, base=0,
            channel_multiplier=-1), reads=["ONES"], writes=["IDENT"])
        p.dma("sp", COS.ap(), cos_d.rearrange("t p f -> p t f"), writes=["COS"], dsem="pre", dmode="coll")
        p.dma("sp", SIN.ap(), sin_d.rearrange("t p f -> p t f"), writes=["SIN"], dsem="pre", dmode="coll")
        p.dma("sp", AR5.ap()[0:1, 0:1], memw_d, writes=["mwraw"], dsem="pre", dmode="coll")
        p.op("act", lambda h: h.activation(AR5.ap()[0:1, 1:2],
                                           AR5.ap()[0:1, 0:1], AF.Exp,
                                           scale=-1.0),
             reads=["mwraw"], writes=["mwexp"])
        p.op("dve", lambda h: h.tensor_scalar_add(
            AR5.ap()[0:1, 2:3], AR5.ap()[0:1, 1:2], 1.0),
            reads=["mwexp"], writes=["mwp1"])
        p.op("dve", lambda h: h.reciprocal(AR5.ap()[0:1, 3:4],
                                           AR5.ap()[0:1, 2:3]),
             reads=["mwp1"], writes=["mww"])
        p.op("pe", lambda h: h.matmul(PS[0].ap()[:, 0:1], ONESF.ap()[0:1, :],
                                      AR5.ap()[0:1, 3:4], start=True,
                                      stop=True),
             reads=["mww", "ONESF"], writes=["ps0"])
        p.op("dve", lambda h: h.tensor_copy(st[:, 0:1], PS[0].ap()[:, 0:1]),
             reads=["ps0"], writes=["wrep"])
        p.op("dve", lambda h: h.tensor_scalar(st[:, 1:2], PS[0].ap()[:, 0:1],
                                              -1.0, 1.0, ALU.mult, ALU.add),
             reads=["ps0"], writes=["omwrep"])
        W_REP = st[:, 0:1]
        OMW_REP = st[:, 1:2]
        for i, v in enumerate([LN_EPS, RMS_EPS, EXP_SHIFT_A]):
            col = 60 + i
            p.op("dve", lambda h, col=col, v=v: h.memset(
                st[:, col:col + 1], v), writes=[f"cconst{i}"])
            nc.const_aps.register(F32, v, st[:, col:col + 1])
        p.fence()

        def ln_tile(xin, s, t, xt_slot, xn_fm_dst, dst_res, sq_scr, xn_scr,
                    xn_res, par):
            xt = xt_ring[xt_slot]
            rxt = f"xt{xt_slot}"
            o = 8 + 8 * par
            p.dma("sp", xt, xin[s, t * 128:(t + 1) * 128, :], writes=[rxt])
            p.op("dve", lambda h: h.reduce_sum(st[:, o:o + 1], xt, axis=AX.X),
                 reads=[rxt], writes=[f"lnsum{par}"])
            p.op("act", lambda h: h.activation(sq_scr, xt, AF.Square,
                                               accum_out=st[:, o + 1:o + 2]),
                 reads=[rxt], writes=[f"lnsq{par}", "SQSCR"])
            p.op("dve", lambda h: h.tensor_scalar_mul(st[:, o + 2:o + 3],
                                                      st[:, o:o + 1], 1.0 / D),
                 reads=[f"lnsum{par}"], writes=[f"lnmean{par}"])
            p.op("dve", lambda h: h.tensor_mul(st[:, o + 3:o + 4],
                                               st[:, o + 2:o + 3],
                                               st[:, o + 2:o + 3]),
                 reads=[f"lnmean{par}"], writes=[f"lnmsq{par}"])
            p.op("dve", lambda h: h.tensor_scalar(
                st[:, o + 4:o + 5], st[:, o + 1:o + 2], 1.0 / D,
                st[:, o + 3:o + 4], ALU.mult, ALU.subtract),
                reads=[f"lnsq{par}", f"lnmsq{par}"], writes=[f"lnvar{par}"])
            p.op("act", lambda h: h.activation(st[:, o + 5:o + 6],
                                               st[:, o + 4:o + 5], AF.Ln,
                                               bias=LN_EPS),
                 reads=[f"lnvar{par}"], writes=[f"lnln{par}"])
            p.op("act", lambda h: h.activation(st[:, o + 6:o + 7],
                                               st[:, o + 5:o + 6], AF.Exp,
                                               scale=-0.5),
                 reads=[f"lnln{par}"], writes=[f"lnrstd{par}"])
            p.op("dve", lambda h: h.tensor_scalar(
                xn_scr, xt, st[:, o + 2:o + 3], st[:, o + 6:o + 7],
                ALU.subtract, ALU.mult),
                reads=[rxt, f"lnmean{par}", f"lnrstd{par}"], writes=[xn_res])
            for half in range(2):
                bank = 6 + half
                for j in range(8):
                    c = half * 8 + j
                    p.op("pe", lambda h, c=c, j=j, bank=bank: h.transpose(
                        psb(bank)[:, j * 128:(j + 1) * 128],
                        xn_scr[:, c * 128:(c + 1) * 128], IDENT.ap()),
                        reads=[xn_res, "IDENT"], writes=[f"ps{bank}"])
                dst = xn_fm_dst(half * 8)
                p.op("dve", lambda h, bank=bank, dst=dst: h.tensor_copy(
                    dst, psb(bank).rearrange("p (c t) -> p c t", c=8)),
                    reads=[f"ps{bank}"], writes=[dst_res])
            return rxt

        def emit_attn_a(xin, xout):
            XNF = AR1.ap()[:, 0:16384].rearrange("p (c t) -> p c t", c=16)
            SIG = AR1.ap()[:, 16384:32768].rearrange("p (c t) -> p c t", c=16)
            QTMP = AR2.ap()[:, 0:8192].rearrange("p (c t) -> p c t", c=8)
            KC = AR2.ap()[:, 8192:16384].rearrange("p (c t) -> p c t", c=4)
            VT = AR2.ap()[:, 16384:25088].rearrange("p (t c) -> p t c", t=16)
            XN_SCR = [AR2.ap()[:, 25088 + i * 2048:25088 + (i + 1) * 2048]
                      for i in range(2)]
            WQ_ST = [AR4.ap()[:, i * 512:(i + 1) * 512] for i in range(4)]
            WK_ST = [AR4.ap()[:, 2048 + i * 512:2048 + (i + 1) * 512]
                     for i in range(2)]
            MEM_SB = AR4.ap()[:, 3072:5120].rearrange("p (c f) -> p c f", c=4)
            MEMN_SB = AR4.ap()[:, 5120:5184].rearrange("p (c f) -> p c f", c=4)
            SQ_SCR = AR4.ap()[:, 6144:8192]
            WVB = AR3.ap()[:, 6144:10240].bitcast(BF16).rearrange(
                "p (c f) -> p c f", c=16)

            for c in range(4):
                p.dma("gpsimd", MEM_SB[:, c, :], mem_d[c], writes=[f"mem{c}"], dsem="awt", dmode="coll")
                p.dma("gpsimd", MEMN_SB[:, c, 0:1], memn_d[c],
                      writes=[f"memn{c}"], dsem="awt", dmode="coll")
            for c in range(16):
                p.dma("gpsimd", WVB[:, c, :], wv[c], writes=[f"wv{c}"], dsem="awt", dmode="coll")

            for s in range(ns):
                for g in range(NG):
                    for tt in range(GT):
                        t = g * GT + tt
                        def dstf(c0, tt=tt):
                            return XNF[:, c0:c0 + 8, tt * 128:(tt + 1) * 128]
                        ln_tile(xin, s, t, tt % 2, dstf, f"xnf{tt//4}",
                                SQ_SCR, XN_SCR[tt % 2], f"xnscr{tt%2}",
                                tt % 2)
                    for fq in range(4):
                        for c in range(16):
                            ws = c % 4
                            p.dma("gpsimd", WQ_ST[ws], wq[fq, c],
                                  writes=[f"wqld{ws}"])
                            for fb in range(4):
                                for nh in range(2):
                                    bk = fb * 2 + nh
                                    p.op("pe", lambda h, c=c, fb=fb, nh=nh,
                                         bk=bk, ws=ws: h.matmul(
                                             PS[bk].ap(),
                                             WQ_ST[ws][:, fb * 128:(fb + 1) * 128],
                                             XNF[:, c, nh * 512:(nh + 1) * 512],
                                             start=(c == 0), stop=(c == 15)),
                                         reads=[f"wqld{ws}", f"xnf{nh}"],
                                         writes=[f"ps{bk}"])
                        for fb in range(4):
                            cg = fq * 4 + fb
                            qs = (fq % 2) * 4 + fb
                            for nh in range(2):
                                bk = fb * 2 + nh
                                p.op("dve", lambda h, bk=bk, qs=qs, nh=nh:
                                     h.tensor_copy(
                                         QTMP[:, qs, nh * 512:(nh + 1) * 512],
                                         PS[bk].ap()),
                                     reads=[f"ps{bk}"], writes=[f"qtmp{qs}"])
                            p.dma("sp",
                                  qspill[s, cg, :, g * 1024:(g + 1) * 1024],
                                  QTMP[:, qs, :], reads=[f"qtmp{qs}"],
                                  dsem=f"qsps{qs}")
                            p.op("dve", lambda h, qs=qs, cg=cg:
                                 h.tensor_scalar_min(SIG[:, cg, :],
                                                     QTMP[:, qs, :], 0.0),
                                 reads=[f"qtmp{qs}"], writes=[f"sig{cg}"])
                            p.op("act", lambda h, cg=cg: h.activation(
                                SIG[:, cg, :], SIG[:, cg, :], AF.Exp),
                                reads=[f"sig{cg}"], writes=[f"sig{cg}"])
                            p.op("dve", lambda h, qs=qs, cg=cg:
                                 h.scalar_tensor_tensor(
                                     SIG[:, cg, :], SIG[:, cg, :], -1.0,
                                     QTMP[:, qs, :], ALU.add, ALU.max),
                                 reads=[f"qtmp{qs}"], writes=[f"sig{cg}"])
                    for c in range(16):
                        ws = c % 2
                        p.dma("gpsimd", WK_ST[ws], wk[c], writes=[f"wkld{ws}"])
                        for fb in range(4):
                            for nh in range(2):
                                bk = fb * 2 + nh
                                p.op("pe", lambda h, c=c, fb=fb, nh=nh, bk=bk,
                                     ws=ws: h.matmul(
                                         PS[bk].ap(),
                                         WK_ST[ws][:, fb * 128:(fb + 1) * 128],
                                         XNF[:, c, nh * 512:(nh + 1) * 512],
                                         start=(c == 0), stop=(c == 15)),
                                     reads=[f"wkld{ws}", "xnf"],
                                     writes=[f"ps{bk}"])
                    for fb in range(4):
                        for nh in range(2):
                            bk = fb * 2 + nh
                            p.op("dve", lambda h, bk=bk, fb=fb, nh=nh, g=g:
                                 h.tensor_copy(
                                     KC[:, fb, g * 1024 + nh * 512:
                                        g * 1024 + (nh + 1) * 512],
                                     PS[bk].ap()),
                                 reads=[f"ps{bk}"], writes=["kc"])
                    for tt in range(GT):
                        t = g * GT + tt
                        bk = tt % 2
                        for c in range(16):
                            p.op("pe", lambda h, c=c, tt=tt, bk=bk: h.matmul(
                                PS[bk].ap(),
                                XNF[:, c, tt * 128:(tt + 1) * 128],
                                WVB[:, c, :], start=(c == 0), stop=(c == 15)),
                                reads=[f"xnf{tt//4}", f"wv{c}"],
                                writes=[f"ps{bk}"])
                        p.op("dve", lambda h, bk=bk, t=t: h.tensor_copy(
                            VT[:, t, 0:520].rearrange(
                                "p (hh c) -> p hh c", hh=8)[:, :, 0:64],
                            PS[bk].ap().rearrange("p (hh c) -> p hh c", hh=8)),
                            reads=[f"ps{bk}"], writes=[f"vt{t}"])
                        p.op("dve", lambda h, t=t: h.memset(
                            VT[:, t, 0:520].rearrange(
                                "p (hh c) -> p hh c", hh=8)[:, :, 64:65],
                            1.0), writes=[f"vt{t}"])
                    for gq in range(4):
                        for tt in range(GT):
                            t = g * GT + tt
                            bk = tt % 2
                            bkd = 2 + tt % 2
                            for cp in range(4):
                                p.op("pe", lambda h, gq=gq, tt=tt, cp=cp,
                                     bk=bk: h.matmul(
                                         PS[bk].ap(),
                                         SIG[:, 4 * gq + cp,
                                             tt * 128:(tt + 1) * 128],
                                         MEM_SB[:, cp, :],
                                         start=(cp == 0), stop=(cp == 3)),
                                     reads=[f"sig{4*gq+cp}", f"mem{cp}"],
                                     writes=[f"ps{bk}"])
                                p.op("pe", lambda h, gq=gq, tt=tt, cp=cp,
                                     bkd=bkd: h.matmul(
                                         PS[bkd].ap()[:, 0:1],
                                         SIG[:, 4 * gq + cp,
                                             tt * 128:(tt + 1) * 128],
                                         MEMN_SB[:, cp, 0:1],
                                         start=(cp == 0), stop=(cp == 3)),
                                     reads=[f"sig{4*gq+cp}", f"memn{cp}"],
                                     writes=[f"ps{bkd}"])
                            rc = 24 + tt % 2
                            p.op("dve", lambda h, bkd=bkd, rc=rc:
                                 h.reciprocal(st[:, rc:rc + 1],
                                              PS[bkd].ap()[:, 0:1]),
                                 reads=[f"ps{bkd}"], writes=[f"amrec{tt%2}"])
                            amw = AR5.ap()[:, 1024 + (tt % 2) * 512:
                                           1024 + (tt % 2 + 1) * 512]
                            p.op("dve", lambda h, bk=bk, rc=rc, amw=amw:
                                 h.tensor_scalar(amw, PS[bk].ap(),
                                                 st[:, rc:rc + 1], W_REP,
                                                 ALU.mult, ALU.mult),
                                 reads=[f"ps{bk}", f"amrec{tt%2}", "wrep"],
                                 writes=[f"amw{tt%2}"])
                            p.dma("sp",
                                  xout[s, t * 128:(t + 1) * 128,
                                       gq * 512:(gq + 1) * 512],
                                  amw, reads=[f"amw{tt%2}"],
                                  writes=[f"dxo_{t}"], dsem=f"amws{tt%2}")
                p.fence()
                QH = [AR1.ap()[:, i * 512:(i + 1) * 512] for i in range(2)]
                PBW = [AR1.ap()[:, 1024 + i * 1024:1024 + (i + 1) * 1024]
                       for i in range(3)]
                ADN = AR1.ap()[:, 4096:12288].rearrange(
                    "p (c t) -> p c t", c=16)
                RECB = AR5.ap()[:, 0:512]
                REC1S = [AR5.ap()[:, 512:1024], AR5.ap()[:, 1024:1536]]
                YTC = AR1.ap()[:, 12288:16384].bitcast(F32)
                def make_comb(s, sbk, sub):
                    def emit():
                        t = sbk * 4 + sub
                        xslot = sub % 2
                        xtile = xt_ring[xslot]
                        ptile = xt_ring[2]
                        p.dma("sp", xtile, xin[s, t * 128:(t + 1) * 128, :],
                              writes=[f"xt{xslot}"])
                        p.dma("sp", ptile, xout[s, t * 128:(t + 1) * 128, :],
                              reads=[f"dxo_{t}"], writes=["xt2"])
                        p.op("dve", lambda h: h.tensor_add(YTC, xtile,
                                                           ptile),
                             reads=["xt2", f"xt{xslot}"],
                             writes=["ytc"])
                        for half in range(2):
                            bank = 6 + half
                            for j in range(8):
                                fc = half * 8 + j
                                p.op("pe", lambda h, fc=fc, j=j, bank=bank:
                                     h.transpose(
                                         psb(bank)[:, j * 128:(j + 1) * 128],
                                         ADN[:, fc, sub * 128:(sub + 1) * 128],
                                         IDENT.ap()),
                                     reads=[f"adn{fc}", "IDENT"],
                                     writes=[f"ps{bank}"])
                            p.op("dve", lambda h, bank=bank, half=half:
                                 h.scalar_tensor_tensor(
                                     YTC[:, half * 1024:(half + 1) * 1024],
                                     psb(bank), OMW_REP,
                                     YTC[:, half * 1024:(half + 1) * 1024],
                                     ALU.mult, ALU.add),
                                 reads=[f"ps{bank}", "omwrep"],
                                 writes=["ytc"])
                        p.dma("sp", xout[s, t * 128:(t + 1) * 128, :], YTC,
                              reads=["ytc"], writes=[f"dxo_{t}"],
                              dsem="ytcs")
                    return emit

                self._acomb = []
                for sbk in range(4):
                    LAG = 2

                    def norm_closure(cq, bks):
                        def emit():
                            for h2 in range(2):
                                p.op("dve", lambda h, h2=h2: h.reciprocal(
                                    REC1S[h2][0:1, :],
                                    PS[bks[h2]].ap()[64:65, :]),
                                    reads=[f"ps{bks[h2]}"],
                                    writes=[f"rec1_{h2}"])
                            for h2 in range(2):
                                bk = bks[h2]
                                p.op("pe", lambda h, h2=h2, bk=bk: h.matmul(
                                    PS[bk].ap()[64:128, :],
                                    ONESF.ap()[0:1, 0:64],
                                    REC1S[h2][0:1, :], start=True, stop=True,
                                    tile_position=(0, 64)),
                                    reads=[f"rec1_{h2}", "ONESF"],
                                    writes=[f"ps{bk}"])
                                p.op("act", lambda h, h2=h2, bk=bk: h.copy(
                                    RECB[0:64, :],
                                    PS[bk].ap()[64:128, :]),
                                    reads=[f"ps{bk}"], writes=["recb"])
                                p.op("dve", lambda h, h2=h2, bk=bk:
                                     h.tensor_tensor(
                                         ADN[64 * h2:64 * h2 + 64, cq, :],
                                         PS[bk].ap()[0:64, :],
                                         RECB[0:64, :], ALU.mult),
                                     reads=[f"ps{bk}", "recb"],
                                     writes=[f"adn{cq}"])
                        return emit

                    pend = None
                    exp_hi_par = {}
                    exp_hi_kt = {}
                    for pit in range(16):
                        gq, hp = pit // 4, pit % 4
                        cq = 4 * gq + hp
                        qslot = pit % 2
                        bks = (4, 5) if pit % 2 == 0 else (6, 7)
                        p.dma("sp", QH[qslot],
                              qspill[s, cq, :, sbk * 512:(sbk + 1) * 512],
                              writes=[f"qh{qslot}"])
                        for step in range(16 + LAG):
                            if step == 6 and pend is not None:
                                pend()
                                pend = None
                            if (pit == 0 and step % 4 == 3 and self._acomb):
                                self._acomb.pop(0)()
                            if step < 16:
                                kt = step
                                xd = ([exp_hi_par[kt % 2]]
                                      if kt % 2 in exp_hi_par else [])
                                for h2 in range(2):
                                    scb = 2 * (kt % 2) + h2
                                    p.op("pe", lambda h, hp=hp, h2=h2, kt=kt,
                                         scb=scb, qslot=qslot: h.matmul(
                                             PS[scb].ap(),
                                             KC[64 * h2:64 * h2 + 64, hp,
                                                kt * 128:(kt + 1) * 128],
                                             QH[qslot][64 * h2:64 * h2 + 64, :],
                                             start=True, stop=True),
                                         reads=["kc", f"qh{qslot}"],
                                         writes=[f"ps{scb}"], deps=xd)
                                pb = kt % 3
                                eid = p.op("act", lambda h, kt=kt, pb=pb:
                                           h.activation(
                                               PBW[pb],
                                               PSD[kt % 2].ap(),
                                               AF.Exp, bias=EXP_SHIFT_A,
                                               scale=SCALE_A),
                                           reads=[f"ps{2*(kt%2)}",
                                                  f"ps{2*(kt%2)+1}"],
                                           writes=[f"pb{pb}"])
                                exp_hi_par[kt % 2] = eid
                                exp_hi_kt[kt] = eid
                            if step >= LAG:
                                kt = step - LAG
                                xd = [exp_hi_kt[kt]]
                                pb = kt % 3
                                for h2 in range(2):
                                    hh = 2 * hp + h2
                                    bk = bks[h2]
                                    p.op("pe", lambda h, kt=kt, hh=hh, pb=pb,
                                         h2=h2, bk=bk: h.matmul(
                                             PS[bk].ap()[0:65, :],
                                             VT[:, kt, 65 * hh:65 * hh + 65],
                                             PBW[pb][:, 512 * h2:
                                                     512 * h2 + 512],
                                             start=(kt == 0),
                                             stop=(kt == 15)),
                                         reads=[f"pb{pb}", f"vt{kt}"],
                                         writes=[f"ps{bk}"], deps=xd)
                        pend = norm_closure(cq, bks)
                    pend()
                    self._acomb.extend(
                        make_comb(s, sbk, sub) for sub in range(4))
                for cb in self._acomb:
                    cb()
                self._acomb = []
                p.fence()

        def emit_ffn(xin, xout, tag):
            WB = AR1.ap().rearrange("p (c f) -> p c f", c=16)
            BB2 = AR2.ap()[:, 0:2048]
            SQ_SCR = AR2.ap()[:, 2048:4096]
            XN_SCR = [AR2.ap()[:, 4096 + i * 2048:4096 + (i + 1) * 2048]
                      for i in range(3)]
            XNT = [AR2.ap()[:, 10240 + i * 2048:10240 + (i + 1) * 2048]
                   .rearrange("p (c t) -> p c t", c=16) for i in range(3)]
            YTS = [AR3.ap()[:, 6144 + i * 2048:6144 + (i + 1) * 2048]
                   for i in range(2)]
            HA = AR4.ap()[:, 4096:5248].rearrange("p (c f) -> p c f", c=16)
            BF = AR4.ap()[:, 5248:7296]
            HG = AR4.ap()[:, 7424:7488]
            HGF = AR4.ap()[:, 7488:7616]
            for c in range(16):
                p.dma("gpsimd", WB[:, c, :], wb_d[c], writes=[f"wb{c}"], dsem="fwt", dmode="coll")
                p.dma("gpsimd", HA[:, c, :], ha_d[c], writes=[f"ha{c}"], dsem="fwt", dmode="coll")
            p.dma("gpsimd", BF[0:64, :], bflat_d, writes=["bflat"], dsem="fwt", dmode="coll")
            p.dma("gpsimd", BB2[0:1, :], bb_d, writes=["bb"], dsem="fwt", dmode="coll")

            for s in range(ns):
                def ffn_A(t, s=s):
                    xslot = t % 3
                    xn = XNT[xslot]
                    def dstf(c0, xn=xn):
                        return xn[:, c0:c0 + 8, :]
                    return ln_tile(xin, s, t, t % 3, dstf, f"xnt{xslot}",
                                   SQ_SCR, XN_SCR[t % 3], f"xnscr{t%3}",
                                   t % 2)

                def ffn_B1(t, rxt, s=s):
                    xslot = t % 3
                    xn = XNT[xslot]
                    for c in range(16):
                        p.op("pe", lambda h, c=c, xn=xn: h.matmul(
                            PS[4].ap()[:, 0:72], xn[:, c, :], HA[:, c, :],
                            start=(c == 0), stop=(c == 15)),
                            reads=[f"xnt{xslot}", f"ha{c}"], writes=["ps4"])
                    for c in range(16):
                        for j in range(4):
                            p.op("pe", lambda h, c=c, j=j, xn=xn: h.matmul(
                                PS[j].ap(), xn[:, c, :],
                                WB[:, c, j * 512:(j + 1) * 512],
                                start=(c == 0), stop=False),
                                reads=[f"xnt{xslot}", f"wb{c}"],
                                writes=[f"ps{j}"])
                    for j in range(4):
                        p.op("pe", lambda h, j=j: h.matmul(
                            PS[j].ap(), ONES.ap()[0:1, :],
                            BB2[0:1, j * 512:(j + 1) * 512],
                            start=False, stop=False),
                            reads=["bb", "ONES"], writes=[f"ps{j}"])

                def ffn_B2(t, rxt, s=s):
                    xslot = t % 3
                    xn = XNT[xslot]
                    lg = PS[4].ap()[:, 64:72]
                    p.op("dve", lambda h: h.reduce_max(st[:, 26:27], lg,
                                                       axis=AX.X),
                         reads=["ps4"], writes=["rm1"])
                    p.op("dve", lambda h: h.tensor_scalar_mul(
                        st[:, 27:28], st[:, 26:27], -1.0),
                        reads=["rm1"], writes=["rnm1"])
                    p.op("dve", lambda h: h.tensor_scalar(
                        st[:, 32:40], lg, st[:, 26:27], None, ALU.is_equal),
                        reads=["ps4", "rm1"], writes=["req"])
                    p.op("dve", lambda h: h.scalar_tensor_tensor(
                        st[:, 32:40], st[:, 32:40], -1e30, lg,
                        ALU.mult, ALU.add),
                        reads=["req", "ps4"], writes=["rmsk"])
                    p.op("dve", lambda h: h.reduce_max(st[:, 28:29],
                                                       st[:, 32:40], axis=AX.X),
                         reads=["rmsk"], writes=["rm2"])
                    p.op("act", lambda h: h.activation(st[:, 40:48], lg,
                                                       AF.Exp,
                                                       bias=st[:, 27:28]),
                         reads=["ps4", "rnm1"], writes=["ru"])
                    p.op("dve", lambda h: h.tensor_scalar(
                        st[:, 48:56], lg, st[:, 28:29], None, ALU.is_ge),
                        reads=["ps4", "rm2"], writes=["rkeep"])
                    p.op("dve", lambda h: h.tensor_mul(
                        st[:, 40:48], st[:, 40:48], st[:, 48:56]),
                        reads=["ru", "rkeep"], writes=["rw8"])
                    p.op("dve", lambda h: h.reduce_sum(st[:, 29:30],
                                                       st[:, 40:48], axis=AX.X),
                         reads=["rw8"], writes=["rsg"])
                    p.op("dve", lambda h: h.reciprocal(st[:, 30:31],
                                                       st[:, 29:30]),
                         reads=["rsg"], writes=["rrec"])
                    p.op("dve", lambda h: h.tensor_scalar_mul(
                        st[:, 30:31], st[:, 30:31], 2.0),
                        reads=["rrec"], writes=["rrec"])
                    p.op("dve", lambda h: h.tensor_scalar_mul(
                        st[:, 40:48], st[:, 40:48], st[:, 30:31]),
                        reads=["rw8", "rrec"], writes=["rg8"])
                    p.op("dve", lambda h: h.tensor_tensor(
                        HG.rearrange("p (e r) -> p e r", e=8),
                        PS[4].ap()[:, 0:64].rearrange("p (e r) -> p e r", e=8),
                        st[:, 40:48].unsqueeze(2).broadcast_to([128, 8, 8]),
                        ALU.mult),
                        reads=["ps4", "rg8"], writes=["hg"])
                    p.op("pe", lambda h: h.transpose(psb(5)[0:64, 0:128], HG,
                                                     IDENT.ap()),
                         reads=["hg", "IDENT"], writes=["ps5"])
                    p.op("dve", lambda h: h.tensor_copy(
                        HGF[0:64, :], psb(5)[0:64, 0:128]),
                        reads=["ps5"], writes=["hgf"])
                    for j in range(4):
                        p.op("pe", lambda h, j=j: h.matmul(
                            PS[j].ap(), HGF[0:64, :],
                            BF[0:64, j * 512:(j + 1) * 512],
                            start=False, stop=True),
                            reads=["hgf", "bflat"], writes=[f"ps{j}"])
                    xt = xt_ring[t % 3]
                    yt = YTS[t % 2]
                    for j in range(4):
                        p.op("dve", lambda h, j=j, yt=yt, xt=xt: h.tensor_add(
                            yt[:, j * 512:(j + 1) * 512], PS[j].ap(),
                            xt[:, j * 512:(j + 1) * 512]),
                            reads=[f"ps{j}", rxt], writes=[f"yts{t%2}"])
                    p.dma("sp", xout[s, t * 128:(t + 1) * 128, :], yt,
                          reads=[f"yts{t%2}"], dsem=f"yso{t%2}")

                rxts = {}
                rxts[0] = ffn_A(0)
                rxts[1] = ffn_A(1)
                for t in range(NT):
                    ffn_B1(t, rxts[t])
                    if t + 2 < NT:
                        rxts[t + 2] = ffn_A(t + 2)
                    ffn_B2(t, rxts[t])
            p.fence()

        def emit_attn_b(xin, xout):
            WQA = AR1.ap()[:, 0:24576].rearrange("p (c f) -> p c f", c=16)
            CKV = AR1.ap()[:, 24576:32768].rearrange("p (c t) -> p c t", c=4)
            QAN = AR2.ap()[:, 0:24576].rearrange("p (c t) -> p c t", c=12)
            KPE = AR2.ap()[:, 24576:26624]
            QASC = AR2.ap()[:, 26624:28160]
            CKSC = AR2.ap()[:, 28160:28800]
            AOSC = [AR2.ap()[:, 28800 + i * 512:28800 + (i + 1) * 512]
                    for i in range(2)]
            XNT = [AR4.ap()[:, i * 2048:(i + 1) * 2048].rearrange(
                "p (c t) -> p c t", c=16) for i in range(2)]
            SQ_SCR = AR4.ap()[:, 4096:6144]
            XN_SCRB = AR4.ap()[:, 6144:8192]
            WKVA = AR3.ap()[:, 4096:10240].bitcast(BF16)
            WKVA5 = WKVA[:, 0:8192].rearrange("p (c f) -> p c f", c=16)
            WKVAP = WKVA[:, 8192:9216].rearrange("p (c f) -> p c f", c=16)
            KPSC = WKVA[:, 9216:9280]
            KPSC2 = WKVA[:, 9280:9344]
            LNWQ = st[:, 32:44]
            LNWK = st[:, 44:48]
            p.dma("sp", LNWQ, lnwq_d, writes=["lnwq"], dsem="bw1s", dmode="coll")
            p.dma("sp", LNWK, lnwk_d, writes=["lnwk"], dsem="bw1s", dmode="coll")

            for s in range(ns):
                for c in range(16):
                    p.dma("gpsimd", WQA[:, c, :], wqa[c], writes=[f"wqa{c}"], dsem="bw1", dmode="coll")
                    p.dma("gpsimd", WKVA5[:, c, :], wkva5[c],
                          writes=[f"wkva{c}"], dsem="bw1", dmode="coll")
                    p.dma("gpsimd", WKVAP[:, c, :], wkvap[c],
                          writes=[f"wkvap{c}"], dsem="bw1", dmode="coll")
                def b1_A(t, s=s):
                    xslot = t % 2
                    xn = XNT[xslot]
                    def dstf(c0, xn=xn):
                        return xn[:, c0:c0 + 8, :]
                    ln_tile(xin, s, t, t % 2, dstf, f"xnt{xslot}", SQ_SCR,
                            XN_SCRB, "xnscrB", t % 2)

                def b1_B1(t, s=s):
                    xslot = t % 2
                    xn = XNT[xslot]
                    for c in range(16):
                        for j in range(3):
                            p.op("pe", lambda h, c=c, j=j, xn=xn: h.matmul(
                                PS[j].ap(), xn[:, c, :],
                                WQA[:, c, j * 512:(j + 1) * 512],
                                start=(c == 0), stop=(c == 15)),
                                reads=[f"xnt{xslot}", f"wqa{c}"],
                                writes=[f"ps{j}"])
                        p.op("pe", lambda h, c=c, xn=xn: h.matmul(
                            PS[3].ap(), xn[:, c, :], WKVA5[:, c, :],
                            start=(c == 0), stop=(c == 15)),
                            reads=[f"xnt{xslot}", f"wkva{c}"], writes=["ps3"])
                        p.op("pe", lambda h, c=c, xn=xn: h.matmul(
                            PS[4].ap()[:, 0:64], xn[:, c, :], WKVAP[:, c, :],
                            start=(c == 0), stop=(c == 15)),
                            reads=[f"xnt{xslot}", f"wkvap{c}"], writes=["ps4"])

                def b1_B2(t, s=s):
                    xslot = t % 2
                    xn = XNT[xslot]
                    for j in range(3):
                        p.op("act", lambda h, j=j: h.activation(
                            SQ_SCR[:, j * 512:(j + 1) * 512], PS[j].ap(),
                            AF.Square, accum_out=st[:, 24 + j:25 + j]),
                            reads=[f"ps{j}"], writes=[f"qsq{j}", "SQSCR"])
                    p.op("dve", lambda h: h.tensor_add(
                        st[:, 27:28], st[:, 24:25], st[:, 25:26]),
                        reads=["qsq0", "qsq1"], writes=["qss"])
                    p.op("dve", lambda h: h.tensor_add(
                        st[:, 27:28], st[:, 27:28], st[:, 26:27]),
                        reads=["qss", "qsq2"], writes=["qss"])
                    p.op("act", lambda h: h.activation(
                        st[:, 28:29], st[:, 27:28], AF.Ln, bias=RMS_EPS,
                        scale=1.0 / 1536), reads=["qss"], writes=["qln"])
                    p.op("act", lambda h: h.activation(
                        st[:, 29:30], st[:, 28:29], AF.Exp, scale=-0.5),
                        reads=["qln"], writes=["qrstd"])
                    for j in range(3):
                        p.op("dve", lambda h, j=j: h.tensor_scalar_mul(
                            QASC[:, j * 512:(j + 1) * 512], PS[j].ap(),
                            st[:, 29:30]),
                            reads=[f"ps{j}", "qrstd"], writes=["qasc"])
                    p.op("act", lambda h: h.activation(
                        SQ_SCR[:, 1536:2048], PS[3].ap(), AF.Square,
                        accum_out=st[:, 30:31]), reads=["ps3"],
                        writes=["csq", "SQSCR"])
                    p.op("act", lambda h: h.activation(
                        st[:, 31:32], st[:, 30:31], AF.Ln, bias=RMS_EPS,
                        scale=1.0 / 512), reads=["csq"], writes=["cln"])
                    p.op("act", lambda h: h.activation(
                        st[:, 48:49], st[:, 31:32], AF.Exp, scale=-0.5),
                        reads=["cln"], writes=["crstd"])
                    p.op("dve", lambda h: h.tensor_scalar_mul(
                        CKSC[:, 0:512], PS[3].ap(), st[:, 48:49]),
                        reads=["ps3", "crstd"], writes=["cksc"])
                    p.op("act", lambda h: h.copy(KPSC, PS[4].ap()[:, 0:64]),
                         reads=["ps4"], writes=["kpsc"])
                    cosb = COS.ap()[:, t, :]
                    sinb = SIN.ap()[:, t, :]
                    p.op("dve", lambda h, cosb=cosb: h.tensor_tensor(
                        KPSC2, KPSC, cosb, ALU.mult),
                        reads=["kpsc", "COS"], writes=["kpsc2"])
                    p.op("dve", lambda h, sinb=sinb: h.tensor_tensor(
                        CKSC[:, 512:544], KPSC[:, 32:64], sinb[:, 0:32],
                        ALU.mult), reads=["kpsc", "SIN"], writes=["kpT1"])
                    p.op("dve", lambda h, sinb=sinb: h.tensor_tensor(
                        CKSC[:, 544:576], KPSC[:, 0:32], sinb[:, 32:64],
                        ALU.mult), reads=["kpsc", "SIN"], writes=["kpT2"])
                    p.op("dve", lambda h: h.tensor_sub(
                        KPSC2[:, 0:32], KPSC2[:, 0:32], CKSC[:, 512:544]),
                        reads=["kpT1", "kpsc2"], writes=["kpsc2"])
                    p.op("dve", lambda h: h.tensor_add(
                        KPSC2[:, 32:64], KPSC2[:, 32:64], CKSC[:, 544:576]),
                        reads=["kpT2", "kpsc2"], writes=["kpsc2"])
                    for half in range(2):
                        bank = 6 + half
                        for j in range(8):
                            c = half * 8 + j
                            if c < 12:
                                src = QASC[:, c * 128:(c + 1) * 128]
                                rr = "qasc"
                            elif c < 16:
                                src = CKSC[:, (c - 12) * 128:(c - 11) * 128]
                                rr = "cksc"
                            p.op("pe", lambda h, src=src, j=j, bank=bank:
                                 h.transpose(psb(bank)[:, j * 128:(j + 1) * 128],
                                             src, IDENT.ap()),
                                 reads=[rr, "IDENT"], writes=[f"ps{bank}"])
                        if half == 0:
                            p.op("dve", lambda h, t=t: h.tensor_tensor(
                                QAN[:, 0:8, t * 128:(t + 1) * 128],
                                psb(6).rearrange("p (c u) -> p c u", c=8),
                                LNWQ[:, 0:8].unsqueeze(2).broadcast_to(
                                    [128, 8, 128]), ALU.mult),
                                reads=["ps6", "lnwq"], writes=["qan"])
                        else:
                            p.op("dve", lambda h, t=t: h.tensor_tensor(
                                QAN[:, 8:12, t * 128:(t + 1) * 128],
                                psb(7).rearrange("p (c u) -> p c u", c=8)[:, 0:4, :],
                                LNWQ[:, 8:12].unsqueeze(2).broadcast_to(
                                    [128, 4, 128]), ALU.mult),
                                reads=["ps7", "lnwq"], writes=["qan"])
                            p.op("dve", lambda h, t=t: h.tensor_tensor(
                                CKV[:, 0:4, t * 128:(t + 1) * 128],
                                psb(7).rearrange("p (c u) -> p c u", c=8)[:, 4:8, :],
                                LNWK[:, 0:4].unsqueeze(2).broadcast_to(
                                    [128, 4, 128]), ALU.mult),
                                reads=["ps7", "lnwk"], writes=["ckv"])
                    p.op("pe", lambda h: h.transpose(psb(5)[0:64, 0:128],
                                                     KPSC2, IDENT.ap()),
                         reads=["kpsc2", "IDENT"], writes=["ps5"])
                    p.op("dve", lambda h, t=t: h.tensor_copy(
                        KPE[0:64, t * 128:(t + 1) * 128],
                        psb(5)[0:64, 0:128]),
                        reads=["ps5"], writes=["kpe"])
                    p.op("dve", lambda h, t=t: h.tensor_copy(
                        KPE[64:128, t * 128:(t + 1) * 128],
                        psb(5)[0:64, 0:128]),
                        reads=["ps5"], writes=["kpe"])

                b1_A(0)
                for t in range(NT):
                    b1_B1(t)
                    if t + 1 < NT:
                        b1_A(t + 1)
                    b1_B2(t)
                p.fence()
                if self.debug_taps and s == 0:
                    qantap, ckvtap, kpetap = self._taps
                    for c in range(12):
                        p.dma("sp", qantap[c], QAN[:, c, :], dsem="tapq",
                              dmode="coll")
                    for c in range(4):
                        p.dma("sp", ckvtap[c], CKV[:, c, :], dsem="tapq",
                              dmode="coll")
                    p.dma("sp", kpetap, KPE, dsem="tapq", dmode="coll")
                    p.fence()
                WQBP = AR1.ap()[:, 0:12288].rearrange("p (c f) -> p c f", c=12)
                QPT = AR1.ap()[:, 12288:13312]
                QPR = AR1.ap()[:, 13312:14336]
                TSC = AR1.ap()[:, 14336:14848]
                QPF4 = AR1.ap()[:, 14848:18944].rearrange(
                    "p (c t) -> p c t", c=8)
                for c in range(12):
                    p.dma("gpsimd", WQBP[:, c, :], wqbp[c], writes=[f"wqbp{c}"], dsem="bw2", dmode="coll")
                for t in range(NT):
                    for c in range(12):
                        for j in range(2):
                            p.op("pe", lambda h, c=c, j=j, t=t: h.matmul(
                                PS[j].ap(), QAN[:, c, t * 128:(t + 1) * 128],
                                WQBP[:, c, j * 512:(j + 1) * 512],
                                start=(c == 0), stop=(c == 11)),
                                reads=["qan", f"wqbp{c}"], writes=[f"ps{j}"])
                    for j in range(2):
                        p.op("act", lambda h, j=j: h.copy(
                            QPT[:, j * 512:(j + 1) * 512], PS[j].ap()),
                            reads=[f"ps{j}"], writes=["qpt"])
                    cosb = COS.ap()[:, t, :].unsqueeze(1)
                    sinb = SIN.ap()[:, t, :].unsqueeze(1)
                    qpt3 = QPT.rearrange("p (hh u) -> p hh u", hh=16)
                    qpr3 = QPR.rearrange("p (hh u) -> p hh u", hh=16)
                    tsc3 = TSC.rearrange("p (hh u) -> p hh u", hh=16)
                    p.op("dve", lambda h, cosb=cosb: h.tensor_tensor(
                        qpr3, qpt3, cosb.broadcast_to([128, 16, 64]),
                        ALU.mult), reads=["qpt", "COS"], writes=["qpr"])
                    p.op("dve", lambda h, sinb=sinb: h.tensor_tensor(
                        tsc3[:, :, 0:32], qpt3[:, :, 32:64],
                        sinb.broadcast_to([128, 16, 64])[:, :, 0:32],
                        ALU.mult), reads=["qpt", "SIN"], writes=["qtsc"])
                    p.op("dve", lambda h: h.tensor_sub(
                        qpr3[:, :, 0:32], qpr3[:, :, 0:32], tsc3[:, :, 0:32]),
                        reads=["qtsc", "qpr"], writes=["qpr"])
                    p.op("dve", lambda h, sinb=sinb: h.tensor_tensor(
                        tsc3[:, :, 0:32], qpt3[:, :, 0:32],
                        sinb.broadcast_to([128, 16, 64])[:, :, 32:64],
                        ALU.mult), reads=["qpt", "SIN"], writes=["qtsc"])
                    p.op("dve", lambda h: h.tensor_add(
                        qpr3[:, :, 32:64], qpr3[:, :, 32:64], tsc3[:, :, 0:32]),
                        reads=["qtsc", "qpr"], writes=["qpr"])
                    bank = 5
                    for j in range(8):
                        p.op("pe", lambda h, j=j: h.transpose(
                            psb(bank)[:, j * 128:(j + 1) * 128],
                            QPR[:, j * 128:(j + 1) * 128], IDENT.ap()),
                            reads=["qpr", "IDENT"], writes=[f"ps{bank}"])
                    p.op("dve", lambda h, t=t: h.tensor_copy(
                        QPF4[:, :, (t % 4) * 128:(t % 4 + 1) * 128],
                        psb(bank).rearrange("p (c u) -> p c u", c=8)),
                        reads=[f"ps{bank}"], writes=[f"qpf4_{t%4}"])
                    if t % 4 == 3:
                        p.dma("sp",
                              qpespill[:, :, (t - 3) * 128:(t + 1) * 128]
                              .rearrange("c p u -> p c u"),
                              QPF4[:, :, :],
                              reads=["qpf4_0", "qpf4_1", "qpf4_2", "qpf4_3"],
                              writes=[f"qpesp_{t//4}"], dsem="qpst")
                p.fence()
                WKVBN_SB = AR1.ap()[:, 0:8192].rearrange(
                    "p (hh c u) -> p hh c u", hh=16, c=4)
                WKVBV_SB = AR1.ap()[:, 8192:16384].rearrange(
                    "p (c f) -> p c f", c=4)
                VT4 = AR1.ap()[:, 16384:24576].rearrange(
                    "p (t f) -> p t f", t=16)
                WQBN_ST = [AR4.ap()[:, i * 1536:(i + 1) * 1536].rearrange(
                    "p (c u) -> p c u", c=12) for i in range(2)]
                PBW2 = [AR4.ap()[:, 3072 + i * 1024:3072 + (i + 1) * 1024]
                        for i in range(2)]
                QPEH = [AR4.ap()[:, 5120 + i * 512:5120 + (i + 1) * 512]
                        for i in range(2)]
                QNH = [AR3.ap()[:, i * 1024:(i + 1) * 1024].bitcast(BF16)
                       for i in range(2)]
                KNH = [AR3.ap()[:, 2048 + i * 1024:2048 + (i + 1) * 1024]
                       .bitcast(BF16) for i in range(2)]
                RECB = AR5.ap()[:, 0:512]
                REC1S = [AR5.ap()[:, 512:1024], AR5.ap()[:, 1024:1536]]
                for hh in range(16):
                    p.dma("gpsimd", WKVBN_SB[:, hh, :, :],
                          wkvbn[hh].rearrange("c p u -> p c u"),
                          writes=[f"wkvbn{hh}"], dsem="bw3", dmode="coll")
                for c in range(4):
                    p.dma("gpsimd", WKVBV_SB[:, c, :], wkvbv[c],
                          writes=[f"wkvbv{c}"], dsem="bw3", dmode="coll")
                self._bpend = None
                for hg in range(4):
                    for t in range(NT):
                        bk = t % 2
                        for c in range(4):
                            p.op("pe", lambda h, c=c, t=t, bk=bk, hg=hg:
                                 h.matmul(PS[bk].ap(),
                                          CKV[:, c, t * 128:(t + 1) * 128],
                                          WKVBV_SB[:, c,
                                                   hg * 512:(hg + 1) * 512],
                                          start=(c == 0), stop=(c == 3)),
                                 reads=["ckv", f"wkvbv{c}"],
                                 writes=[f"ps{bk}"])
                        p.op("dve", lambda h, bk=bk, t=t: h.tensor_copy(
                            VT4[:, t, :], PS[bk].ap()),
                            reads=[f"ps{bk}"], writes=[f"vt4_{t}"])
                    for hl in range(4):
                        hh = hg * 4 + hl
                        hp = hh // 2
                        h2 = hh % 2
                        ws = hh % 2
                        p.dma("gpsimd", WQBN_ST[ws], wqbn[hh],
                              writes=[f"wqbnld{ws}"])
                        for tb in range(4):
                            bk = 4
                            for c in range(12):
                                p.op("pe", lambda h, c=c, tb=tb, bk=bk, ws=ws:
                                     h.matmul(
                                         PS[bk].ap(),
                                         WQBN_ST[ws][:, c, :],
                                         QAN[:, c, tb * 512:(tb + 1) * 512],
                                         start=(c == 0), stop=(c == 11)),
                                     reads=[f"wqbnld{ws}", "qan"],
                                     writes=[f"ps{bk}"])
                            p.op("dve", lambda h, bk=bk, tb=tb, ws=ws:
                                 h.tensor_copy(
                                     QNH[ws][:, tb * 512:(tb + 1) * 512],
                                     PS[bk].ap()),
                                 reads=[f"ps{bk}"], writes=[f"qnh{ws}"])
                        for tb in range(4):
                            bk = 4
                            for c in range(4):
                                p.op("pe", lambda h, c=c, tb=tb, bk=bk, hh=hh:
                                     h.matmul(
                                         PS[bk].ap(),
                                         WKVBN_SB[:, hh, c, :],
                                         CKV[:, c, tb * 512:(tb + 1) * 512],
                                         start=(c == 0), stop=(c == 3)),
                                     reads=[f"wkvbn{hh}", "ckv"],
                                     writes=[f"ps{bk}"])
                            p.op("dve", lambda h, bk=bk, tb=tb, ws=ws:
                                 h.tensor_copy(
                                     KNH[ws][:, tb * 512:(tb + 1) * 512],
                                     PS[bk].ap()),
                                 reads=[f"ps{bk}"], writes=[f"knh{ws}"])
                        LAGB = 1
                        for sbk in range(4):
                            qslot = sbk % 2
                            obk = 6 if sbk % 2 == 0 else 7
                            rsb = 5
                            p.dma("sp", QPEH[qslot][0:64, :],
                                  qpespill[hp, 64 * h2:64 * h2 + 64,
                                           sbk * 512:(sbk + 1) * 512],
                                  reads=[f"qpesp_{sbk}"],
                                  writes=[f"qpeh{qslot}"])
                            p.dma("sp", QPEH[qslot][64:128, :],
                                  qpespill[hp, 64 * h2:64 * h2 + 64,
                                           sbk * 512:(sbk + 1) * 512],
                                  reads=[f"qpesp_{sbk}"],
                                  writes=[f"qpeh{qslot}"])
                            nkt = (sbk + 1) * 4
                            exp_last = {}
                            for step in range(nkt // 2 + LAGB):
                                if step < nkt // 2:
                                    k0 = 2 * step
                                    p2 = (k0 // 2) % 2
                                    xd = ([exp_last[k0 - 4]]
                                          if k0 - 4 in exp_last else [])
                                    for dk in range(2):
                                        kt = k0 + dk
                                        scb = 2 * p2 + dk
                                        p.op("pe", lambda h, kt=kt, scb=scb,
                                             ws=ws, sbk=sbk: h.matmul(
                                                 PS[scb].ap(),
                                                 KNH[ws][:, kt * 128:(kt + 1) * 128],
                                                 QNH[ws][:, sbk * 512:(sbk + 1) * 512],
                                                 start=True, stop=False),
                                             reads=[f"knh{ws}", f"qnh{ws}"],
                                             writes=[f"ps{scb}"], deps=xd)
                                    for dk in range(2):
                                        kt = k0 + dk
                                        scb = 2 * p2 + dk
                                        p.op("pe", lambda h, kt=kt, scb=scb,
                                             dk=dk, qslot=qslot: h.matmul(
                                                 PS[scb].ap(),
                                                 KPE[64 * dk:64 * dk + 64,
                                                     kt * 128:(kt + 1) * 128],
                                                 QPEH[qslot][64 * dk:64 * dk + 64, :],
                                                 start=False, stop=True),
                                             reads=["kpe", f"qpeh{qslot}"],
                                             writes=[f"ps{scb}"])
                                    pb = (k0 // 2) % 2
                                    eid = p.op("act", lambda h, p2=p2, pb=pb:
                                               h.activation(
                                                   PBW2[pb],
                                                   PSD[p2].ap(),
                                                   AF.Exp, scale=SCALE_B),
                                               reads=[f"ps{2 * p2}",
                                                      f"ps{2 * p2 + 1}"],
                                               writes=[f"pb{pb}"])
                                    exp_last[k0] = eid
                                    exp_last[k0 + 1] = eid
                                    for dk in range(2):
                                        kt = k0 + dk
                                        if kt >= nkt - 4:
                                            base = sbk * 512 - kt * 128
                                            p.op("gpsimd",
                                                 lambda h, pb=pb, dk=dk,
                                                 base=base:
                                                 h.affine_select(
                                                     PBW2[pb][:, 512 * dk:
                                                              512 * dk + 512],
                                                     PBW2[pb][:, 512 * dk:
                                                              512 * dk + 512],
                                                     [[1, 512]],
                                                     ALU.is_ge, 0.0,
                                                     base=base,
                                                     channel_multiplier=-1),
                                                 reads=[f"pb{pb}"],
                                                 writes=[f"pb{pb}"])
                                if step == 0 and self._bpend is not None:
                                    self._bpend()
                                    self._bpend = None
                                if step >= LAGB:
                                    k0 = 2 * (step - LAGB)
                                    xd = [exp_last[k0 + 1]]
                                    pb = (k0 // 2) % 2
                                    for dk in range(2):
                                        kt = k0 + dk
                                        p.op("pe", lambda h, kt=kt, pb=pb,
                                             dk=dk, hl=hl, nkt=nkt, obk=obk:
                                             h.matmul(
                                                 PS[obk].ap(),
                                                 VT4[:, kt,
                                                     hl * 128:(hl + 1) * 128],
                                                 PBW2[pb][:, 512 * dk:
                                                          512 * dk + 512],
                                                 start=(kt == 0),
                                                 stop=(kt == nkt - 1)),
                                             reads=[f"pb{pb}", f"vt4_{kt}"],
                                             writes=[f"ps{obk}"], deps=xd)
                                        p.op("pe", lambda h, kt=kt, pb=pb,
                                             dk=dk, nkt=nkt, rsb=rsb:
                                             h.matmul(
                                                 PS[rsb].ap()[0:1, :],
                                                 ONES.ap()[:, 0:1],
                                                 PBW2[pb][:, 512 * dk:
                                                          512 * dk + 512],
                                                 start=(kt == 0),
                                                 stop=(kt == nkt - 1)),
                                             reads=[f"pb{pb}", "ONES"],
                                             writes=[f"ps{rsb}"])

                            def bnorm(hh=hh, sbk=sbk, rsb=rsb, obk=obk,
                                      rslt=sbk % 2, asl=sbk % 2):
                                def emit():
                                    p.op("act", lambda h: h.activation(
                                        REC1S[rslt][0:1, :],
                                        PS[rsb].ap()[0:1, :], AF.Ln),
                                        reads=[f"ps{rsb}"],
                                        writes=[f"rec1_{rslt}"])
                                    p.op("act", lambda h: h.activation(
                                        REC1S[rslt][0:1, :],
                                        REC1S[rslt][0:1, :], AF.Exp,
                                        scale=-1.0),
                                        reads=[f"rec1_{rslt}"],
                                        writes=[f"rec1_{rslt}"])
                                    p.op("pe", lambda h: h.matmul(
                                        PS[rsb].ap(), ONESF.ap()[0:1, :],
                                        REC1S[rslt][0:1, :],
                                        start=True, stop=True),
                                        reads=[f"rec1_{rslt}", "ONESF"],
                                        writes=[f"ps{rsb}"])
                                    p.op("act", lambda h: h.copy(
                                        RECB, PS[rsb].ap()),
                                        reads=[f"ps{rsb}"], writes=["recb"])
                                    p.op("dve", lambda h: h.tensor_tensor(
                                        AOSC[asl], PS[obk].ap(), RECB,
                                        ALU.mult),
                                        reads=[f"ps{obk}", "recb"],
                                        writes=[f"aosc{asl}"])
                                    p.dma("sp",
                                          aofspill[hh, :,
                                                   sbk * 512:(sbk + 1) * 512],
                                          AOSC[asl], reads=[f"aosc{asl}"],
                                          writes=[f"aofsp_{hh}_{sbk}"],
                                          dsem=f"aos{asl}")
                                return emit
                            self._bpend = bnorm()
                if self._bpend is not None:
                    self._bpend()
                    self._bpend = None
                p.fence()
                WO_SB = AR1.ap().rearrange("p (c f) -> p c f", c=16)
                AOF_SB = AR2.ap()[:, 0:16384].rearrange(
                    "p (r c t) -> p r c t", r=2, c=16)
                for c in range(16):
                    p.dma("gpsimd", WO_SB[:, c, :], wo_d[c], writes=[f"wo{c}"], dsem="bwo", dmode="coll")
                for sbk in range(4):
                    rr = sbk % 2
                    for c in range(16):
                        p.dma("sp", AOF_SB[:, rr, c, :],
                              aofspill[c, :, sbk * 512:(sbk + 1) * 512],
                              reads=[f"aofsp_{c}_{sbk}"],
                              writes=[f"aofsb{rr}_{c}"],
                              dsem=f"aofl{rr}", dmode="coll")
                    for sub in range(4):
                        t = sbk * 4 + sub
                        xslot = t % 3
                        xtile = xt_ring[xslot]
                        ytile = AR3.ap()[:, 6144 + (t % 2) * 2048:
                                         6144 + (t % 2 + 1) * 2048]
                        p.dma("sp", xtile, xin[s, t * 128:(t + 1) * 128, :],
                              writes=[f"xt{xslot}"])
                        for c in range(16):
                            for j in range(4):
                                p.op("pe", lambda h, c=c, j=j, sub=sub,
                                     rr=rr: h.matmul(
                                         PS[j].ap(),
                                         AOF_SB[:, rr, c,
                                                sub * 128:(sub + 1) * 128],
                                         WO_SB[:, c, j * 512:(j + 1) * 512],
                                         start=(c == 0), stop=(c == 15)),
                                     reads=[f"aofsb{rr}_{c}", f"wo{c}"],
                                     writes=[f"ps{j}"])
                        for j in range(4):
                            p.op("dve", lambda h, j=j, xtile=xtile,
                                 ytile=ytile: h.tensor_add(
                                     ytile[:, j * 512:(j + 1) * 512],
                                     PS[j].ap(),
                                     xtile[:, j * 512:(j + 1) * 512]),
                                 reads=[f"ps{j}", f"xt{xslot}"],
                                 writes=[f"yts{t%2}"])
                        p.dma("sp", xout[s, t * 128:(t + 1) * 128, :], ytile,
                              reads=[f"yts{t%2}"], dsem=f"yso{t%2}")
                p.fence()

        subs = list(self.subs)
        chain_in = {0: x_in, 1: xA, 2: xB, 3: xA}
        chain_out = {0: xA, 1: xB, 2: xA, 3: y_out}
        for i in subs:
            xi = x_in if i == subs[0] else chain_in[i]
            if i == subs[-1] and i != 0:
                xo = y_out
            else:
                xo = chain_out[i]
            if i == 0:
                emit_attn_a(xi, xo)
            elif i == 2:
                emit_attn_b(xi, xo)
            else:
                emit_ffn(xi, xo, f"ffn{i}")
        if subs[-1] == 0:
            for s in range(ns):
                for t in range(NT):
                    xt = xt_ring[t % 2]
                    p.dma("sp", xt, xA[s, t * 128:(t + 1) * 128, :],
                          writes=[f"cp{t%2}"])
                    p.dma("sp", y_out[s, t * 128:(t + 1) * 128, :], xt,
                          reads=[f"cp{t%2}"], writes=[f"cpd{t%2}"],
                          dsem=f"cps{t%2}")

        p.emit(nc)
        ctx.close()



def host_layouts(I):
    f32 = np.float32
    out = {}
    out["wq"] = np.ascontiguousarray(
        I["L0_attn_a_Wq"].T.reshape(16, 128, 4, 512).transpose(2, 0, 1, 3))
    out["wk"] = np.ascontiguousarray(I["L0_attn_a_Wk"].T.reshape(16, 128, 512))
    out["wv"] = np.ascontiguousarray(I["L0_attn_a_Wv"].T.reshape(16, 128, 512))
    out["mem"] = np.ascontiguousarray(
        I["L0_attn_a_memory"].reshape(4, 128, 512))
    out["memn"] = np.ascontiguousarray(
        I["L0_attn_a_memory_norm"].reshape(4, 128, 1))
    out["memw"] = I["L0_attn_a_memory_weight"].reshape(1, 1).astype(f32)
    out["wqa"] = np.ascontiguousarray(I["L0_attn_b_Wqa"].reshape(16, 128, 1536))
    out["lnwq"] = np.ascontiguousarray(
        I["L0_attn_b_q_a_ln_w"].reshape(12, 128).T)
    Wqb = I["L0_attn_b_Wqb"].reshape(12, 128, 16, 192)
    out["wqbn"] = np.ascontiguousarray(Wqb[:, :, :, :128].transpose(2, 1, 0, 3))
    out["wqbp"] = np.ascontiguousarray(
        Wqb[:, :, :, 128:].reshape(12, 128, 16 * 64))
    Wkva = I["L0_attn_b_Wkva"].reshape(16, 128, 576)
    out["wkva5"] = np.ascontiguousarray(Wkva[:, :, 0:512])
    out["wkvap"] = np.ascontiguousarray(Wkva[:, :, 512:576])
    out["lnwk"] = np.ascontiguousarray(
        I["L0_attn_b_kv_a_ln_w"].reshape(4, 128).T)
    Wkvb = I["L0_attn_b_Wkvb"].reshape(4, 128, 16, 256)
    out["wkvbn"] = np.ascontiguousarray(Wkvb[:, :, :, :128].transpose(2, 0, 1, 3))
    out["wkvbv"] = np.ascontiguousarray(
        Wkvb[:, :, :, 128:].reshape(4, 128, 16 * 128))
    out["wo"] = np.ascontiguousarray(I["L0_attn_b_Wo"].reshape(16, 128, 2048))
    out["wb"] = np.ascontiguousarray(I["L0_ffn_W_base"].T.reshape(16, 128, 2048))
    ha = np.concatenate([
        I["L0_ffn_A_w"].transpose(2, 0, 1).reshape(2048, 64),
        I["L0_ffn_W_router"].T], axis=1)
    out["ha"] = np.ascontiguousarray(ha.reshape(16, 128, 72))
    out["bb"] = I["L0_ffn_b_base"].reshape(1, 2048).astype(f32)
    out["bflat"] = np.ascontiguousarray(
        I["L0_ffn_B_w"].transpose(0, 2, 1).reshape(64, 2048))
    inv = 1.0 / 10000.0 ** (np.arange(0, 64, 2, dtype=np.float64) / 64)
    tpos = np.arange(S, dtype=np.float64)
    fr = np.outer(tpos, inv)
    emb = np.concatenate([fr, fr], axis=1)
    out["cos"] = np.ascontiguousarray(np.cos(emb).astype(f32).reshape(16, 128, 64))
    out["sin"] = np.ascontiguousarray(np.sin(emb).astype(f32).reshape(16, 128, 64))
    for k in out:
        out[k] = out[k].astype(f32)
    return out


_BUILDER_CACHE = {}


def get_builder(n_seq=B_LOC, subs=(0, 1, 2, 3)):
    key = (n_seq, tuple(subs))
    if key not in _BUILDER_CACHE:
        _BUILDER_CACHE[key] = Builder(n_seq, subs)
    return _BUILDER_CACHE[key]


def kernel(**inputs):
    w = host_layouts(inputs)
    x = np.ascontiguousarray(inputs["L0_attn_a_x"], dtype=np.float32)
    b = get_builder()
    in_maps = []
    for core in range(N_CORES):
        m = dict(w)
        m["x_in"] = np.ascontiguousarray(x[core * B_LOC:(core + 1) * B_LOC])
        in_maps.append(m)
    res = run_bass_kernel_spmd(b.nc, in_maps, core_ids=list(range(N_CORES)))
    return np.concatenate([r["y"] for r in res.results], axis=0)


# version 16
# speedup vs baseline: 1.0392x; 1.0392x over previous
import numpy as np
import ml_dtypes
from contextlib import ExitStack

import concourse.bass as bass
import concourse.mybir as mybir
from concourse.bass_utils import run_bass_kernel_spmd

F32 = mybir.dt.float32
BF16 = mybir.dt.bfloat16
AF = mybir.ActivationFunctionType
ALU = mybir.AluOpType
AX = mybir.AxisListType

N_CORES = 8
B = 16
S = 2048
D = 2048
B_LOC = B // N_CORES
NT = S // 128
GT = 8
NG = NT // GT
EXP_SHIFT_A = -4.0
SCALE_A = 0.125
SCALE_B = float((128 + 64) ** -0.5)
LN_EPS = 1e-5
RMS_EPS = 1e-6



ENGS = ("pe", "dve", "act", "gpsimd", "sp")


class Prog:

    def __init__(self):
        self.ops = []
        self.last_writer = {}
        self.readers = {}
        self.last_reader = {}
        self.fence_deps = []

    def op(self, eng, fn, reads=(), writes=(), kind="c", deps=(),
           dsem=None, dmode="slot"):
        opid = len(self.ops)
        d = set(deps)
        d.update(self.fence_deps)
        for r in reads:
            w = self.last_writer.get(r)
            if w is not None:
                d.add(w)
            if r.startswith("ps"):
                lr = self.last_reader.get(r)
                if lr is not None and lr[1] != eng:
                    d.add(lr[0])
                self.last_reader[r] = (opid, eng)
        for w in writes:
            lw = self.last_writer.get(w)
            if lw is not None:
                d.add(lw)
            d.update(self.readers.get(w, ()))
        d.discard(opid)
        self.ops.append(dict(eng=eng, kind=kind, fn=fn, deps=d, id=opid,
                             dsem=dsem, dmode=dmode))
        for r in reads:
            self.readers.setdefault(r, set()).add(opid)
        for w in writes:
            self.last_writer[w] = opid
            self.readers[w] = set()
            if w.startswith("ps"):
                self.last_reader.pop(w, None)
        return opid

    def dma(self, q, out, in_, reads=(), writes=(), dsem=None, dmode="slot"):
        if dsem is None:
            assert writes, "dma needs dsem or a written slot resource"
            dsem = "D_" + writes[0]
        def fn(h, out=out, in_=in_):
            return h.dma_start(out=out, in_=in_)
        return self.op(q, fn, reads=reads, writes=writes, kind="d",
                       dsem=dsem, dmode=dmode)

    def fence(self):
        last = {}
        for o in self.ops:
            if o["kind"] == "d":
                last[("dsem", o["dsem"])] = o["id"]
            else:
                last[(o["eng"], o["kind"])] = o["id"]
        self.fence_deps = list(last.values())
        self.last_writer.clear()
        self.readers.clear()
        self.last_reader.clear()

    @staticmethod
    def _fifo_safe(prod, cons):
        return (prod["eng"] == cons["eng"] == "pe"
                and prod["kind"] == "c" and cons["kind"] == "c")

    def emit(self, nc, strict=False):
        ops = self.ops
        needs = [False] * len(ops)
        for o in ops:
            for d in o["deps"]:
                dd = ops[d]
                if strict or not self._fifo_safe(dd, o):
                    needs[d] = True
        last = {}
        for o in ops:
            last[(o["eng"], o["kind"])] = o["id"]
        for opid in last.values():
            needs[opid] = True

        csem_names = sorted({(o["eng"], o["kind"]) for o in ops
                             if o["kind"] == "c"})
        dsem_info = {}
        dmaval = {}
        for o in ops:
            if o["kind"] == "d":
                di = dsem_info.setdefault(o["dsem"],
                                          dict(count=0, mode=o["dmode"],
                                               ids=[]))
                assert di["mode"] == o["dmode"], o["dsem"]
                di["count"] += 1
                di["ids"].append(o["id"])
                dmaval[o["id"]] = 16 * di["count"]

        with ExitStack() as ctx:
            sems = {sn: ctx.enter_context(nc.semaphore(f"s_{sn[0]}_{sn[1]}"))
                    for sn in csem_names}
            dsems = {dn: ctx.enter_context(nc.semaphore(f"d_{dn}"))
                     for dn in dsem_info}
            counts = {sn: 0 for sn in csem_names}
            cum = {}
            for o in ops:
                if o["kind"] == "c" and needs[o["id"]]:
                    sn = (o["eng"], o["kind"])
                    counts[sn] += 1
                    cum[o["id"]] = (sn, counts[sn])
            finals = dict(counts)

            def dma_wait_val(dep_id, waiter_id):
                o = ops[dep_id]
                di = dsem_info[o["dsem"]]
                if di["mode"] == "slot":
                    return dmaval[dep_id]
                n = sum(1 for i in di["ids"] if i < waiter_id)
                return 16 * n

            block = ctx.enter_context(nc.Block())
            eng_ops = {e: [o for o in ops if o["eng"] == e] for e in ENGS}
            all_sem_keys = list(csem_names) + list(dsems)
            vclock = {e: {k: 0 for k in all_sem_keys} for e in ENGS}

            def run_engine(e, handle):
                vc = vclock[e]
                for o in eng_ops[e]:
                    waits = {}
                    for d in o["deps"]:
                        dd = ops[d]
                        if not strict and self._fifo_safe(dd, o):
                            continue
                        if dd["kind"] == "d":
                            k = dd["dsem"]
                            v = dma_wait_val(d, o["id"])
                        else:
                            if d not in cum:
                                continue
                            k, v = cum[d]
                        if vc[k] < v:
                            waits[k] = max(waits.get(k, 0), v)
                    for k, v in sorted(waits.items(), key=str):
                        handle.wait_ge(sems[k] if k in sems else dsems[k], v)
                        vc[k] = v
                    inst = o["fn"](handle)
                    if o["kind"] == "d":
                        inst.then_inc(dsems[o["dsem"]], 16)
                    elif o["id"] in cum:
                        sn, _ = cum[o["id"]]
                        inst.then_inc(sems[sn], 1)

            @block.tensor
            def _(te):
                run_engine("pe", te)

            @block.vector
            def _(v):
                run_engine("dve", v)

            @block.scalar
            def _(sc):
                run_engine("act", sc)

            @block.gpsimd
            def _(gp):
                run_engine("gpsimd", gp)

            @block.sync
            def _(sy):
                run_engine("sp", sy)
                for sn in csem_names:
                    if finals[sn] > vclock["sp"][sn]:
                        sy.wait_ge(sems[sn], finals[sn])
                for dn, di in dsem_info.items():
                    if 16 * di["count"] > vclock["sp"][dn]:
                        sy.wait_ge(dsems[dn], 16 * di["count"])



class Builder:
    def __init__(self, n_seq=B_LOC, subs=(0, 1, 2, 3), debug_taps=False):
        self.n_seq = n_seq
        self.subs = tuple(subs)
        self.debug_taps = debug_taps
        self.nc = bass.Bass("TRN2", target_bir_lowering=False)
        self.p = Prog()
        self._build()

    def _build(self):
        nc = self.nc
        p = self.p
        ns = self.n_seq

        def din(name, shape):
            return nc.dram_tensor(name, list(shape), F32,
                                  kind="ExternalInput").ap()

        x_in = din("x_in", [ns, S, D])
        y_out = nc.dram_tensor("y", [ns, S, D], F32, kind="ExternalOutput").ap()
        xA = nc.dram_tensor("xA", [ns, S, D], F32, kind="Internal").ap()
        xB = nc.dram_tensor("xB", [ns, S, D], F32, kind="Internal").ap()
        tapkind = "ExternalOutput" if self.debug_taps else "Internal"
        qspill = nc.dram_tensor("qspill", [ns, 16, 128, S], BF16,
                                kind="Internal").ap()
        qpespill = nc.dram_tensor("qpespill", [8, 128, S], BF16,
                                  kind=tapkind).ap()
        aofspill = nc.dram_tensor("aofspill", [16, 128, S], BF16,
                                  kind=tapkind).ap()
        if self.debug_taps:
            qantap = nc.dram_tensor("qantap", [12, 128, S], BF16,
                                    kind="ExternalOutput").ap()
            ckvtap = nc.dram_tensor("ckvtap", [4, 128, S], BF16,
                                    kind="ExternalOutput").ap()
            kpetap = nc.dram_tensor("kpetap", [128, S], BF16,
                                    kind="ExternalOutput").ap()
            self._taps = (qantap, ckvtap, kpetap)

        wq = din("wq", [4, 16, 128, 512])
        wk = din("wk", [16, 128, 512])
        wv = din("wv", [16, 128, 512])
        memw_d = din("memw", [1, 1])
        mem_d = din("mem", [4, 128, 512])
        memn_d = din("memn", [4, 128, 1])
        wqa = din("wqa", [16, 128, 1536])
        lnwq_d = din("lnwq", [128, 12])
        wqbn = din("wqbn", [16, 128, 12, 128])
        wqbp = din("wqbp", [12, 128, 1024])
        wkva5 = din("wkva5", [16, 128, 512])
        wkvap = din("wkvap", [16, 128, 64])
        lnwk_d = din("lnwk", [128, 4])
        wkvbn = din("wkvbn", [16, 4, 128, 128])
        wkvbv = din("wkvbv", [4, 128, 2048])
        wo_d = din("wo", [16, 128, 2048])
        wb_d = din("wb", [16, 128, 2048])
        ha_d = din("ha", [16, 128, 72])
        bb_d = din("bb", [1, 2048])
        bflat_d = din("bflat", [64, 2048])
        cos_d = din("cos", [16, 128, 64])
        sin_d = din("sin", [16, 128, 64])

        ctx = ExitStack()
        sb = lambda name, shape, dt: ctx.enter_context(
            nc.sbuf_tensor(name, shape, dt))
        AR1 = sb("AR1", [128, 32768], BF16)
        AR2 = sb("AR2", [128, 32768], BF16)
        AR3 = sb("AR3", [128, 10240], F32)
        AR4 = sb("AR4", [128, 8192], BF16)
        AR5 = sb("AR5", [128, 3072], F32)
        IDENT = sb("IDENT", [128, 128], BF16)
        ONES = sb("ONES", [128, 128], BF16)
        ONESF = sb("ONESF", [1, 128], F32)
        COS = sb("COS", [128, 16, 64], F32)
        SIN = sb("SIN", [128, 16, 64], F32)
        STAT = sb("STAT", [128, 64], F32)
        PSD = [ctx.enter_context(nc.psum_tensor(f"psd{i}", [128, 1024], F32))
               for i in range(2)]
        PS_hi = [ctx.enter_context(nc.psum_tensor(f"ps{i}", [128, 512], F32))
                 for i in range(4, 8)]

        class _PSView:
            def __init__(self, ap_):
                self._ap = ap_

            def ap(self):
                return self._ap

        PS = ([_PSView(PSD[i // 2].ap()[:, (i % 2) * 512:(i % 2 + 1) * 512])
               for i in range(4)] + PS_hi)

        def psb(i):
            if i < 4:
                return PSD[i // 2].ap().bitcast(BF16)[
                    :, (i % 2) * 1024:(i % 2 + 1) * 1024]
            return PS[i].ap().bitcast(BF16)

        st = STAT.ap()
        xt_ring = [AR3.ap()[:, i * 2048:(i + 1) * 2048] for i in range(3)]

        p.op("dve", lambda h: h.memset(ONES.ap(), 1.0), writes=["ONES"])
        p.op("dve", lambda h: h.memset(ONESF.ap(), 1.0), writes=["ONESF"])
        p.op("gpsimd", lambda h: h.affine_select(
            IDENT.ap(), ONES.ap(), [[1, 128]], ALU.is_equal, 0.0, base=0,
            channel_multiplier=-1), reads=["ONES"], writes=["IDENT"])
        p.dma("sp", COS.ap(), cos_d.rearrange("t p f -> p t f"), writes=["COS"], dsem="pre", dmode="coll")
        p.dma("sp", SIN.ap(), sin_d.rearrange("t p f -> p t f"), writes=["SIN"], dsem="pre", dmode="coll")
        p.dma("sp", AR5.ap()[0:1, 0:1], memw_d, writes=["mwraw"], dsem="pre", dmode="coll")
        p.op("act", lambda h: h.activation(AR5.ap()[0:1, 1:2],
                                           AR5.ap()[0:1, 0:1], AF.Exp,
                                           scale=-1.0),
             reads=["mwraw"], writes=["mwexp"])
        p.op("dve", lambda h: h.tensor_scalar_add(
            AR5.ap()[0:1, 2:3], AR5.ap()[0:1, 1:2], 1.0),
            reads=["mwexp"], writes=["mwp1"])
        p.op("dve", lambda h: h.reciprocal(AR5.ap()[0:1, 3:4],
                                           AR5.ap()[0:1, 2:3]),
             reads=["mwp1"], writes=["mww"])
        p.op("pe", lambda h: h.matmul(PS[0].ap()[:, 0:1], ONESF.ap()[0:1, :],
                                      AR5.ap()[0:1, 3:4], start=True,
                                      stop=True),
             reads=["mww", "ONESF"], writes=["ps0"])
        p.op("dve", lambda h: h.tensor_copy(st[:, 0:1], PS[0].ap()[:, 0:1]),
             reads=["ps0"], writes=["wrep"])
        p.op("dve", lambda h: h.tensor_scalar(st[:, 1:2], PS[0].ap()[:, 0:1],
                                              -1.0, 1.0, ALU.mult, ALU.add),
             reads=["ps0"], writes=["omwrep"])
        W_REP = st[:, 0:1]
        OMW_REP = st[:, 1:2]
        for i, v in enumerate([LN_EPS, RMS_EPS, EXP_SHIFT_A]):
            col = 60 + i
            p.op("dve", lambda h, col=col, v=v: h.memset(
                st[:, col:col + 1], v), writes=[f"cconst{i}"])
            nc.const_aps.register(F32, v, st[:, col:col + 1])
        p.fence()

        def ln_tile(xin, s, t, xt_slot, xn_fm_dst, dst_res, sq_scr, xn_scr,
                    xn_res, par):
            xt = xt_ring[xt_slot]
            rxt = f"xt{xt_slot}"
            o = 8 + 8 * par
            p.dma("sp", xt, xin[s, t * 128:(t + 1) * 128, :], writes=[rxt])
            p.op("dve", lambda h: h.reduce_sum(st[:, o:o + 1], xt, axis=AX.X),
                 reads=[rxt], writes=[f"lnsum{par}"])
            p.op("act", lambda h: h.activation(sq_scr, xt, AF.Square,
                                               accum_out=st[:, o + 1:o + 2]),
                 reads=[rxt], writes=[f"lnsq{par}", "SQSCR"])
            p.op("dve", lambda h: h.tensor_scalar_mul(st[:, o + 2:o + 3],
                                                      st[:, o:o + 1], 1.0 / D),
                 reads=[f"lnsum{par}"], writes=[f"lnmean{par}"])
            p.op("dve", lambda h: h.tensor_mul(st[:, o + 3:o + 4],
                                               st[:, o + 2:o + 3],
                                               st[:, o + 2:o + 3]),
                 reads=[f"lnmean{par}"], writes=[f"lnmsq{par}"])
            p.op("dve", lambda h: h.tensor_scalar(
                st[:, o + 4:o + 5], st[:, o + 1:o + 2], 1.0 / D,
                st[:, o + 3:o + 4], ALU.mult, ALU.subtract),
                reads=[f"lnsq{par}", f"lnmsq{par}"], writes=[f"lnvar{par}"])
            p.op("act", lambda h: h.activation(st[:, o + 5:o + 6],
                                               st[:, o + 4:o + 5], AF.Ln,
                                               bias=LN_EPS),
                 reads=[f"lnvar{par}"], writes=[f"lnln{par}"])
            p.op("act", lambda h: h.activation(st[:, o + 6:o + 7],
                                               st[:, o + 5:o + 6], AF.Exp,
                                               scale=-0.5),
                 reads=[f"lnln{par}"], writes=[f"lnrstd{par}"])
            p.op("dve", lambda h: h.tensor_scalar(
                xn_scr, xt, st[:, o + 2:o + 3], st[:, o + 6:o + 7],
                ALU.subtract, ALU.mult),
                reads=[rxt, f"lnmean{par}", f"lnrstd{par}"], writes=[xn_res])
            for half in range(2):
                bank = 6 + half
                for j in range(8):
                    c = half * 8 + j
                    p.op("pe", lambda h, c=c, j=j, bank=bank: h.transpose(
                        psb(bank)[:, j * 128:(j + 1) * 128],
                        xn_scr[:, c * 128:(c + 1) * 128], IDENT.ap()),
                        reads=[xn_res, "IDENT"], writes=[f"ps{bank}"])
                dst = xn_fm_dst(half * 8)
                p.op("dve", lambda h, bank=bank, dst=dst: h.tensor_copy(
                    dst, psb(bank).rearrange("p (c t) -> p c t", c=8)),
                    reads=[f"ps{bank}"], writes=[dst_res])
            return rxt

        def emit_attn_a(xin, xout):
            XNF = AR1.ap()[:, 0:16384].rearrange("p (c t) -> p c t", c=16)
            SIG = AR1.ap()[:, 16384:32768].rearrange("p (c t) -> p c t", c=16)
            QTMP = AR2.ap()[:, 0:8192].rearrange("p (c t) -> p c t", c=8)
            KC = AR2.ap()[:, 8192:16384].rearrange("p (c t) -> p c t", c=4)
            VT = AR2.ap()[:, 16384:25088].rearrange("p (t c) -> p t c", t=16)
            XN_SCR = [AR2.ap()[:, 25088 + i * 2048:25088 + (i + 1) * 2048]
                      for i in range(2)]
            WQ_ST = [AR4.ap()[:, i * 512:(i + 1) * 512] for i in range(4)]
            WK_ST = [AR4.ap()[:, 2048 + i * 512:2048 + (i + 1) * 512]
                     for i in range(2)]
            MEM_SB = AR4.ap()[:, 3072:5120].rearrange("p (c f) -> p c f", c=4)
            MEMN_SB = AR4.ap()[:, 5120:5184].rearrange("p (c f) -> p c f", c=4)
            SQ_SCR = AR4.ap()[:, 6144:8192]
            WVB = AR3.ap()[:, 6144:10240].bitcast(BF16).rearrange(
                "p (c f) -> p c f", c=16)

            for c in range(4):
                p.dma("gpsimd", MEM_SB[:, c, :], mem_d[c], writes=[f"mem{c}"], dsem="awt", dmode="coll")
                p.dma("gpsimd", MEMN_SB[:, c, 0:1], memn_d[c],
                      writes=[f"memn{c}"], dsem="awt", dmode="coll")
            for c in range(16):
                p.dma("gpsimd", WVB[:, c, :], wv[c], writes=[f"wv{c}"], dsem="awt", dmode="coll")

            for s in range(ns):
                for g in range(NG):
                    for tt in range(GT):
                        t = g * GT + tt
                        def dstf(c0, tt=tt):
                            return XNF[:, c0:c0 + 8, tt * 128:(tt + 1) * 128]
                        ln_tile(xin, s, t, tt % 2, dstf, f"xnf{tt//4}",
                                SQ_SCR, XN_SCR[tt % 2], f"xnscr{tt%2}",
                                tt % 2)
                    for fq in range(4):
                        for c in range(16):
                            ws = c % 4
                            p.dma("gpsimd", WQ_ST[ws], wq[fq, c],
                                  writes=[f"wqld{ws}"])
                            for fb in range(4):
                                for nh in range(2):
                                    bk = fb * 2 + nh
                                    p.op("pe", lambda h, c=c, fb=fb, nh=nh,
                                         bk=bk, ws=ws: h.matmul(
                                             PS[bk].ap(),
                                             WQ_ST[ws][:, fb * 128:(fb + 1) * 128],
                                             XNF[:, c, nh * 512:(nh + 1) * 512],
                                             start=(c == 0), stop=(c == 15)),
                                         reads=[f"wqld{ws}", f"xnf{nh}"],
                                         writes=[f"ps{bk}"])
                        for fb in range(4):
                            cg = fq * 4 + fb
                            qs = (fq % 2) * 4 + fb
                            for nh in range(2):
                                bk = fb * 2 + nh
                                p.op("dve", lambda h, bk=bk, qs=qs, nh=nh:
                                     h.tensor_copy(
                                         QTMP[:, qs, nh * 512:(nh + 1) * 512],
                                         PS[bk].ap()),
                                     reads=[f"ps{bk}"], writes=[f"qtmp{qs}"])
                            p.dma("sp",
                                  qspill[s, cg, :, g * 1024:(g + 1) * 1024],
                                  QTMP[:, qs, :], reads=[f"qtmp{qs}"],
                                  dsem=f"qsps{qs}")
                            p.op("dve", lambda h, qs=qs, cg=cg:
                                 h.tensor_scalar_min(SIG[:, cg, :],
                                                     QTMP[:, qs, :], 0.0),
                                 reads=[f"qtmp{qs}"], writes=[f"sig{cg}"])
                            p.op("act", lambda h, cg=cg: h.activation(
                                SIG[:, cg, :], SIG[:, cg, :], AF.Exp),
                                reads=[f"sig{cg}"], writes=[f"sig{cg}"])
                            p.op("dve", lambda h, qs=qs, cg=cg:
                                 h.scalar_tensor_tensor(
                                     SIG[:, cg, :], SIG[:, cg, :], -1.0,
                                     QTMP[:, qs, :], ALU.add, ALU.max),
                                 reads=[f"qtmp{qs}"], writes=[f"sig{cg}"])
                    for c in range(16):
                        ws = c % 2
                        p.dma("gpsimd", WK_ST[ws], wk[c], writes=[f"wkld{ws}"])
                        for fb in range(4):
                            for nh in range(2):
                                bk = fb * 2 + nh
                                p.op("pe", lambda h, c=c, fb=fb, nh=nh, bk=bk,
                                     ws=ws: h.matmul(
                                         PS[bk].ap(),
                                         WK_ST[ws][:, fb * 128:(fb + 1) * 128],
                                         XNF[:, c, nh * 512:(nh + 1) * 512],
                                         start=(c == 0), stop=(c == 15)),
                                     reads=[f"wkld{ws}", "xnf"],
                                     writes=[f"ps{bk}"])
                    for fb in range(4):
                        for nh in range(2):
                            bk = fb * 2 + nh
                            p.op("dve", lambda h, bk=bk, fb=fb, nh=nh, g=g:
                                 h.tensor_copy(
                                     KC[:, fb, g * 1024 + nh * 512:
                                        g * 1024 + (nh + 1) * 512],
                                     PS[bk].ap()),
                                 reads=[f"ps{bk}"], writes=["kc"])
                    for tt in range(GT):
                        t = g * GT + tt
                        bk = tt % 2
                        for c in range(16):
                            p.op("pe", lambda h, c=c, tt=tt, bk=bk: h.matmul(
                                PS[bk].ap(),
                                XNF[:, c, tt * 128:(tt + 1) * 128],
                                WVB[:, c, :], start=(c == 0), stop=(c == 15)),
                                reads=[f"xnf{tt//4}", f"wv{c}"],
                                writes=[f"ps{bk}"])
                        p.op("dve", lambda h, bk=bk, t=t: h.tensor_copy(
                            VT[:, t, 0:520].rearrange(
                                "p (hh c) -> p hh c", hh=8)[:, :, 0:64],
                            PS[bk].ap().rearrange("p (hh c) -> p hh c", hh=8)),
                            reads=[f"ps{bk}"], writes=[f"vt{t}"])
                        p.op("dve", lambda h, t=t: h.memset(
                            VT[:, t, 0:520].rearrange(
                                "p (hh c) -> p hh c", hh=8)[:, :, 64:65],
                            1.0), writes=[f"vt{t}"])
                    for gq in range(4):
                        for tt in range(GT):
                            t = g * GT + tt
                            bk = tt % 2
                            bkd = 2 + tt % 2
                            for cp in range(4):
                                p.op("pe", lambda h, gq=gq, tt=tt, cp=cp,
                                     bk=bk: h.matmul(
                                         PS[bk].ap(),
                                         SIG[:, 4 * gq + cp,
                                             tt * 128:(tt + 1) * 128],
                                         MEM_SB[:, cp, :],
                                         start=(cp == 0), stop=(cp == 3)),
                                     reads=[f"sig{4*gq+cp}", f"mem{cp}"],
                                     writes=[f"ps{bk}"])
                                p.op("pe", lambda h, gq=gq, tt=tt, cp=cp,
                                     bkd=bkd: h.matmul(
                                         PS[bkd].ap()[:, 0:1],
                                         SIG[:, 4 * gq + cp,
                                             tt * 128:(tt + 1) * 128],
                                         MEMN_SB[:, cp, 0:1],
                                         start=(cp == 0), stop=(cp == 3)),
                                     reads=[f"sig{4*gq+cp}", f"memn{cp}"],
                                     writes=[f"ps{bkd}"])
                            rc = 24 + tt % 2
                            p.op("dve", lambda h, bkd=bkd, rc=rc:
                                 h.reciprocal(st[:, rc:rc + 1],
                                              PS[bkd].ap()[:, 0:1]),
                                 reads=[f"ps{bkd}"], writes=[f"amrec{tt%2}"])
                            amw = AR5.ap()[:, 1024 + (tt % 2) * 512:
                                           1024 + (tt % 2 + 1) * 512]
                            p.op("dve", lambda h, bk=bk, rc=rc, amw=amw:
                                 h.tensor_scalar(amw, PS[bk].ap(),
                                                 st[:, rc:rc + 1], W_REP,
                                                 ALU.mult, ALU.mult),
                                 reads=[f"ps{bk}", f"amrec{tt%2}", "wrep"],
                                 writes=[f"amw{tt%2}"])
                            p.dma("sp",
                                  xout[s, t * 128:(t + 1) * 128,
                                       gq * 512:(gq + 1) * 512],
                                  amw, reads=[f"amw{tt%2}"],
                                  writes=[f"dxo_{t}"], dsem=f"amws{tt%2}")
                p.fence()
                QH = [AR1.ap()[:, i * 512:(i + 1) * 512] for i in range(2)]
                PBW = [AR1.ap()[:, 1024 + i * 1024:1024 + (i + 1) * 1024]
                       for i in range(3)]
                ADN = AR1.ap()[:, 4096:12288].rearrange(
                    "p (c t) -> p c t", c=16)
                RECB = AR5.ap()[:, 0:512]
                REC1S = [AR5.ap()[:, 512:1024], AR5.ap()[:, 1024:1536]]
                YTC = AR1.ap()[:, 12288:16384].bitcast(F32)
                def make_comb(s, sbk, sub):
                    def emit():
                        t = sbk * 4 + sub
                        xslot = sub % 2
                        xtile = xt_ring[xslot]
                        ptile = xt_ring[2]
                        p.dma("sp", xtile, xin[s, t * 128:(t + 1) * 128, :],
                              writes=[f"xt{xslot}"])
                        p.dma("sp", ptile, xout[s, t * 128:(t + 1) * 128, :],
                              reads=[f"dxo_{t}"], writes=["xt2"])
                        p.op("dve", lambda h: h.tensor_add(YTC, xtile,
                                                           ptile),
                             reads=["xt2", f"xt{xslot}"],
                             writes=["ytc"])
                        for half in range(2):
                            bank = 6 + half
                            for j in range(8):
                                fc = half * 8 + j
                                p.op("pe", lambda h, fc=fc, j=j, bank=bank:
                                     h.transpose(
                                         psb(bank)[:, j * 128:(j + 1) * 128],
                                         ADN[:, fc, sub * 128:(sub + 1) * 128],
                                         IDENT.ap()),
                                     reads=[f"adn{fc}", "IDENT"],
                                     writes=[f"ps{bank}"])
                            p.op("dve", lambda h, bank=bank, half=half:
                                 h.scalar_tensor_tensor(
                                     YTC[:, half * 1024:(half + 1) * 1024],
                                     psb(bank), OMW_REP,
                                     YTC[:, half * 1024:(half + 1) * 1024],
                                     ALU.mult, ALU.add),
                                 reads=[f"ps{bank}", "omwrep"],
                                 writes=["ytc"])
                        p.dma("sp", xout[s, t * 128:(t + 1) * 128, :], YTC,
                              reads=["ytc"], writes=[f"dxo_{t}"],
                              dsem="ytcs")
                    return emit

                self._acomb = []
                for sbk in range(4):
                    LAG = 2

                    def norm_closure(cq, bks):
                        def emit():
                            for h2 in range(2):
                                p.op("dve", lambda h, h2=h2: h.reciprocal(
                                    REC1S[h2][0:1, :],
                                    PS[bks[h2]].ap()[64:65, :]),
                                    reads=[f"ps{bks[h2]}"],
                                    writes=[f"rec1_{h2}"])
                            for h2 in range(2):
                                bk = bks[h2]
                                p.op("pe", lambda h, h2=h2, bk=bk: h.matmul(
                                    PS[bk].ap()[64:128, :],
                                    ONESF.ap()[0:1, 0:64],
                                    REC1S[h2][0:1, :], start=True, stop=True,
                                    tile_position=(0, 64)),
                                    reads=[f"rec1_{h2}", "ONESF"],
                                    writes=[f"ps{bk}"])
                                p.op("act", lambda h, h2=h2, bk=bk: h.copy(
                                    RECB[0:64, :],
                                    PS[bk].ap()[64:128, :]),
                                    reads=[f"ps{bk}"], writes=["recb"])
                                p.op("dve", lambda h, h2=h2, bk=bk:
                                     h.tensor_tensor(
                                         ADN[64 * h2:64 * h2 + 64, cq, :],
                                         PS[bk].ap()[0:64, :],
                                         RECB[0:64, :], ALU.mult),
                                     reads=[f"ps{bk}", "recb"],
                                     writes=[f"adn{cq}"])
                        return emit

                    pend = None
                    exp_hi_par = {}
                    exp_hi_kt = {}
                    for pit in range(16):
                        gq, hp = pit // 4, pit % 4
                        cq = 4 * gq + hp
                        qslot = pit % 2
                        bks = (4, 5) if pit % 2 == 0 else (6, 7)
                        p.dma("sp", QH[qslot],
                              qspill[s, cq, :, sbk * 512:(sbk + 1) * 512],
                              writes=[f"qh{qslot}"])
                        for step in range(16 + LAG):
                            if step == 6 and pend is not None:
                                pend()
                                pend = None
                            if (pit == 0 and step % 4 == 3 and self._acomb):
                                self._acomb.pop(0)()
                            if step < 16:
                                kt = step
                                xd = ([exp_hi_par[kt % 2]]
                                      if kt % 2 in exp_hi_par else [])
                                for h2 in range(2):
                                    scb = 2 * (kt % 2) + h2
                                    p.op("pe", lambda h, hp=hp, h2=h2, kt=kt,
                                         scb=scb, qslot=qslot: h.matmul(
                                             PS[scb].ap(),
                                             KC[64 * h2:64 * h2 + 64, hp,
                                                kt * 128:(kt + 1) * 128],
                                             QH[qslot][64 * h2:64 * h2 + 64, :],
                                             start=True, stop=True),
                                         reads=["kc", f"qh{qslot}"],
                                         writes=[f"ps{scb}"], deps=xd)
                                pb = kt % 3
                                eid = p.op("act", lambda h, kt=kt, pb=pb:
                                           h.activation(
                                               PBW[pb],
                                               PSD[kt % 2].ap(),
                                               AF.Exp, bias=EXP_SHIFT_A,
                                               scale=SCALE_A),
                                           reads=[f"ps{2*(kt%2)}",
                                                  f"ps{2*(kt%2)+1}"],
                                           writes=[f"pb{pb}"])
                                exp_hi_par[kt % 2] = eid
                                exp_hi_kt[kt] = eid
                            if step >= LAG:
                                kt = step - LAG
                                xd = [exp_hi_kt[kt]]
                                pb = kt % 3
                                for h2 in range(2):
                                    hh = 2 * hp + h2
                                    bk = bks[h2]
                                    p.op("pe", lambda h, kt=kt, hh=hh, pb=pb,
                                         h2=h2, bk=bk: h.matmul(
                                             PS[bk].ap()[0:65, :],
                                             VT[:, kt, 65 * hh:65 * hh + 65],
                                             PBW[pb][:, 512 * h2:
                                                     512 * h2 + 512],
                                             start=(kt == 0),
                                             stop=(kt == 15)),
                                         reads=[f"pb{pb}", f"vt{kt}"],
                                         writes=[f"ps{bk}"], deps=xd)
                        pend = norm_closure(cq, bks)
                    pend()
                    self._acomb.extend(
                        make_comb(s, sbk, sub) for sub in range(4))
                for cb in self._acomb:
                    cb()
                self._acomb = []
                p.fence()

        def emit_ffn(xin, xout, tag):
            WB = AR1.ap().rearrange("p (c f) -> p c f", c=16)
            BB2 = AR2.ap()[:, 0:2048]
            SQ_SCR = AR2.ap()[:, 2048:4096]
            XN_SCR = [AR2.ap()[:, 4096 + i * 2048:4096 + (i + 1) * 2048]
                      for i in range(3)]
            XNT = [AR2.ap()[:, 10240 + i * 2048:10240 + (i + 1) * 2048]
                   .rearrange("p (c t) -> p c t", c=16) for i in range(3)]
            YTS = [AR3.ap()[:, 6144 + i * 2048:6144 + (i + 1) * 2048]
                   for i in range(2)]
            HA = AR4.ap()[:, 4096:5248].rearrange("p (c f) -> p c f", c=16)
            BF = AR4.ap()[:, 5248:7296]
            HG = AR4.ap()[:, 7424:7488]
            HGF = AR4.ap()[:, 7488:7616]
            for c in range(16):
                p.dma("gpsimd", WB[:, c, :], wb_d[c], writes=[f"wb{c}"], dsem="fwt", dmode="coll")
                p.dma("gpsimd", HA[:, c, :], ha_d[c], writes=[f"ha{c}"], dsem="fwt", dmode="coll")
            p.dma("gpsimd", BF[0:64, :], bflat_d, writes=["bflat"], dsem="fwt", dmode="coll")
            p.dma("gpsimd", BB2[0:1, :], bb_d, writes=["bb"], dsem="fwt", dmode="coll")

            for s in range(ns):
                def ffn_A(t, s=s):
                    xslot = t % 3
                    xn = XNT[xslot]
                    def dstf(c0, xn=xn):
                        return xn[:, c0:c0 + 8, :]
                    return ln_tile(xin, s, t, t % 3, dstf, f"xnt{xslot}",
                                   SQ_SCR, XN_SCR[t % 3], f"xnscr{t%3}",
                                   t % 2)

                def ffn_B1(t, rxt, s=s):
                    xslot = t % 3
                    xn = XNT[xslot]
                    for c in range(16):
                        p.op("pe", lambda h, c=c, xn=xn: h.matmul(
                            PS[4].ap()[:, 0:72], xn[:, c, :], HA[:, c, :],
                            start=(c == 0), stop=(c == 15)),
                            reads=[f"xnt{xslot}", f"ha{c}"], writes=["ps4"])
                    for c in range(16):
                        for j in range(4):
                            p.op("pe", lambda h, c=c, j=j, xn=xn: h.matmul(
                                PS[j].ap(), xn[:, c, :],
                                WB[:, c, j * 512:(j + 1) * 512],
                                start=(c == 0), stop=False),
                                reads=[f"xnt{xslot}", f"wb{c}"],
                                writes=[f"ps{j}"])
                    for j in range(4):
                        p.op("pe", lambda h, j=j: h.matmul(
                            PS[j].ap(), ONES.ap()[0:1, :],
                            BB2[0:1, j * 512:(j + 1) * 512],
                            start=False, stop=False),
                            reads=["bb", "ONES"], writes=[f"ps{j}"])

                def ffn_B2(t, rxt, s=s):
                    xslot = t % 3
                    xn = XNT[xslot]
                    lg = PS[4].ap()[:, 64:72]
                    p.op("dve", lambda h: h.reduce_max(st[:, 26:27], lg,
                                                       axis=AX.X),
                         reads=["ps4"], writes=["rm1"])
                    p.op("dve", lambda h: h.tensor_scalar_mul(
                        st[:, 27:28], st[:, 26:27], -1.0),
                        reads=["rm1"], writes=["rnm1"])
                    p.op("dve", lambda h: h.tensor_scalar(
                        st[:, 32:40], lg, st[:, 26:27], None, ALU.is_equal),
                        reads=["ps4", "rm1"], writes=["req"])
                    p.op("dve", lambda h: h.scalar_tensor_tensor(
                        st[:, 32:40], st[:, 32:40], -1e30, lg,
                        ALU.mult, ALU.add),
                        reads=["req", "ps4"], writes=["rmsk"])
                    p.op("dve", lambda h: h.reduce_max(st[:, 28:29],
                                                       st[:, 32:40], axis=AX.X),
                         reads=["rmsk"], writes=["rm2"])
                    p.op("act", lambda h: h.activation(st[:, 40:48], lg,
                                                       AF.Exp,
                                                       bias=st[:, 27:28]),
                         reads=["ps4", "rnm1"], writes=["ru"])
                    p.op("dve", lambda h: h.tensor_scalar(
                        st[:, 48:56], lg, st[:, 28:29], None, ALU.is_ge),
                        reads=["ps4", "rm2"], writes=["rkeep"])
                    p.op("dve", lambda h: h.tensor_mul(
                        st[:, 40:48], st[:, 40:48], st[:, 48:56]),
                        reads=["ru", "rkeep"], writes=["rw8"])
                    p.op("dve", lambda h: h.reduce_sum(st[:, 29:30],
                                                       st[:, 40:48], axis=AX.X),
                         reads=["rw8"], writes=["rsg"])
                    p.op("dve", lambda h: h.reciprocal(st[:, 30:31],
                                                       st[:, 29:30]),
                         reads=["rsg"], writes=["rrec"])
                    p.op("dve", lambda h: h.tensor_scalar_mul(
                        st[:, 30:31], st[:, 30:31], 2.0),
                        reads=["rrec"], writes=["rrec"])
                    p.op("dve", lambda h: h.tensor_scalar_mul(
                        st[:, 40:48], st[:, 40:48], st[:, 30:31]),
                        reads=["rw8", "rrec"], writes=["rg8"])
                    p.op("dve", lambda h: h.tensor_tensor(
                        HG.rearrange("p (e r) -> p e r", e=8),
                        PS[4].ap()[:, 0:64].rearrange("p (e r) -> p e r", e=8),
                        st[:, 40:48].unsqueeze(2).broadcast_to([128, 8, 8]),
                        ALU.mult),
                        reads=["ps4", "rg8"], writes=["hg"])
                    p.op("pe", lambda h: h.transpose(psb(5)[0:64, 0:128], HG,
                                                     IDENT.ap()),
                         reads=["hg", "IDENT"], writes=["ps5"])
                    p.op("dve", lambda h: h.tensor_copy(
                        HGF[0:64, :], psb(5)[0:64, 0:128]),
                        reads=["ps5"], writes=["hgf"])
                    for j in range(4):
                        p.op("pe", lambda h, j=j: h.matmul(
                            PS[j].ap(), HGF[0:64, :],
                            BF[0:64, j * 512:(j + 1) * 512],
                            start=False, stop=True),
                            reads=["hgf", "bflat"], writes=[f"ps{j}"])
                    xt = xt_ring[t % 3]
                    yt = YTS[t % 2]
                    for j in range(4):
                        p.op("dve", lambda h, j=j, yt=yt, xt=xt: h.tensor_add(
                            yt[:, j * 512:(j + 1) * 512], PS[j].ap(),
                            xt[:, j * 512:(j + 1) * 512]),
                            reads=[f"ps{j}", rxt], writes=[f"yts{t%2}"])
                    p.dma("sp", xout[s, t * 128:(t + 1) * 128, :], yt,
                          reads=[f"yts{t%2}"], dsem=f"yso{t%2}")

                rxts = {}
                rxts[0] = ffn_A(0)
                rxts[1] = ffn_A(1)
                for t in range(NT):
                    ffn_B1(t, rxts[t])
                    if t + 2 < NT:
                        rxts[t + 2] = ffn_A(t + 2)
                    ffn_B2(t, rxts[t])
            p.fence()

        def emit_attn_b(xin, xout):
            WQA = AR1.ap()[:, 0:24576].rearrange("p (c f) -> p c f", c=16)
            CKV = AR1.ap()[:, 24576:32768].rearrange("p (c t) -> p c t", c=4)
            QAN = AR2.ap()[:, 0:24576].rearrange("p (c t) -> p c t", c=12)
            KPE = AR2.ap()[:, 24576:26624]
            QASC = AR2.ap()[:, 26624:28160]
            CKSC = AR2.ap()[:, 28160:28800]
            AOSC = [AR2.ap()[:, 28800 + i * 512:28800 + (i + 1) * 512]
                    for i in range(2)]
            XNT = [AR4.ap()[:, i * 2048:(i + 1) * 2048].rearrange(
                "p (c t) -> p c t", c=16) for i in range(2)]
            SQ_SCR = AR4.ap()[:, 4096:6144]
            XN_SCRB = AR4.ap()[:, 6144:8192]
            WKVA = AR3.ap()[:, 4096:10240].bitcast(BF16)
            WKVA5 = WKVA[:, 0:8192].rearrange("p (c f) -> p c f", c=16)
            WKVAP = WKVA[:, 8192:9216].rearrange("p (c f) -> p c f", c=16)
            KPSC = WKVA[:, 9216:9280]
            KPSC2 = WKVA[:, 9280:9344]
            LNWQ = st[:, 32:44]
            LNWK = st[:, 44:48]
            p.dma("sp", LNWQ, lnwq_d, writes=["lnwq"], dsem="bw1s", dmode="coll")
            p.dma("sp", LNWK, lnwk_d, writes=["lnwk"], dsem="bw1s", dmode="coll")

            for s in range(ns):
                for c in range(16):
                    p.dma("gpsimd", WQA[:, c, :], wqa[c], writes=[f"wqa{c}"], dsem="bw1", dmode="coll")
                    p.dma("gpsimd", WKVA5[:, c, :], wkva5[c],
                          writes=[f"wkva{c}"], dsem="bw1", dmode="coll")
                    p.dma("gpsimd", WKVAP[:, c, :], wkvap[c],
                          writes=[f"wkvap{c}"], dsem="bw1", dmode="coll")
                def b1_A(t, s=s):
                    xslot = t % 2
                    xn = XNT[xslot]
                    def dstf(c0, xn=xn):
                        return xn[:, c0:c0 + 8, :]
                    ln_tile(xin, s, t, t % 2, dstf, f"xnt{xslot}", SQ_SCR,
                            XN_SCRB, "xnscrB", t % 2)

                def b1_B1(t, s=s):
                    xslot = t % 2
                    xn = XNT[xslot]
                    for c in range(16):
                        for j in range(3):
                            p.op("pe", lambda h, c=c, j=j, xn=xn: h.matmul(
                                PS[j].ap(), xn[:, c, :],
                                WQA[:, c, j * 512:(j + 1) * 512],
                                start=(c == 0), stop=(c == 15)),
                                reads=[f"xnt{xslot}", f"wqa{c}"],
                                writes=[f"ps{j}"])
                        p.op("pe", lambda h, c=c, xn=xn: h.matmul(
                            PS[3].ap(), xn[:, c, :], WKVA5[:, c, :],
                            start=(c == 0), stop=(c == 15)),
                            reads=[f"xnt{xslot}", f"wkva{c}"], writes=["ps3"])
                        p.op("pe", lambda h, c=c, xn=xn: h.matmul(
                            PS[4].ap()[:, 0:64], xn[:, c, :], WKVAP[:, c, :],
                            start=(c == 0), stop=(c == 15)),
                            reads=[f"xnt{xslot}", f"wkvap{c}"], writes=["ps4"])

                def b1_B2(t, s=s):
                    xslot = t % 2
                    xn = XNT[xslot]
                    for j in range(3):
                        p.op("act", lambda h, j=j: h.activation(
                            SQ_SCR[:, j * 512:(j + 1) * 512], PS[j].ap(),
                            AF.Square, accum_out=st[:, 24 + j:25 + j]),
                            reads=[f"ps{j}"], writes=[f"qsq{j}", "SQSCR"])
                    p.op("dve", lambda h: h.tensor_add(
                        st[:, 27:28], st[:, 24:25], st[:, 25:26]),
                        reads=["qsq0", "qsq1"], writes=["qss"])
                    p.op("dve", lambda h: h.tensor_add(
                        st[:, 27:28], st[:, 27:28], st[:, 26:27]),
                        reads=["qss", "qsq2"], writes=["qss"])
                    p.op("act", lambda h: h.activation(
                        st[:, 28:29], st[:, 27:28], AF.Ln, bias=RMS_EPS,
                        scale=1.0 / 1536), reads=["qss"], writes=["qln"])
                    p.op("act", lambda h: h.activation(
                        st[:, 29:30], st[:, 28:29], AF.Exp, scale=-0.5),
                        reads=["qln"], writes=["qrstd"])
                    for j in range(3):
                        p.op("dve", lambda h, j=j: h.tensor_scalar_mul(
                            QASC[:, j * 512:(j + 1) * 512], PS[j].ap(),
                            st[:, 29:30]),
                            reads=[f"ps{j}", "qrstd"], writes=["qasc"])
                    p.op("act", lambda h: h.activation(
                        SQ_SCR[:, 1536:2048], PS[3].ap(), AF.Square,
                        accum_out=st[:, 30:31]), reads=["ps3"],
                        writes=["csq", "SQSCR"])
                    p.op("act", lambda h: h.activation(
                        st[:, 31:32], st[:, 30:31], AF.Ln, bias=RMS_EPS,
                        scale=1.0 / 512), reads=["csq"], writes=["cln"])
                    p.op("act", lambda h: h.activation(
                        st[:, 48:49], st[:, 31:32], AF.Exp, scale=-0.5),
                        reads=["cln"], writes=["crstd"])
                    p.op("dve", lambda h: h.tensor_scalar_mul(
                        CKSC[:, 0:512], PS[3].ap(), st[:, 48:49]),
                        reads=["ps3", "crstd"], writes=["cksc"])
                    p.op("act", lambda h: h.copy(KPSC, PS[4].ap()[:, 0:64]),
                         reads=["ps4"], writes=["kpsc"])
                    cosb = COS.ap()[:, t, :]
                    sinb = SIN.ap()[:, t, :]
                    p.op("dve", lambda h, cosb=cosb: h.tensor_tensor(
                        KPSC2, KPSC, cosb, ALU.mult),
                        reads=["kpsc", "COS"], writes=["kpsc2"])
                    p.op("dve", lambda h, sinb=sinb: h.tensor_tensor(
                        CKSC[:, 512:544], KPSC[:, 32:64], sinb[:, 0:32],
                        ALU.mult), reads=["kpsc", "SIN"], writes=["kpT1"])
                    p.op("dve", lambda h, sinb=sinb: h.tensor_tensor(
                        CKSC[:, 544:576], KPSC[:, 0:32], sinb[:, 32:64],
                        ALU.mult), reads=["kpsc", "SIN"], writes=["kpT2"])
                    p.op("dve", lambda h: h.tensor_sub(
                        KPSC2[:, 0:32], KPSC2[:, 0:32], CKSC[:, 512:544]),
                        reads=["kpT1", "kpsc2"], writes=["kpsc2"])
                    p.op("dve", lambda h: h.tensor_add(
                        KPSC2[:, 32:64], KPSC2[:, 32:64], CKSC[:, 544:576]),
                        reads=["kpT2", "kpsc2"], writes=["kpsc2"])
                    for half in range(2):
                        bank = 6 + half
                        for j in range(8):
                            c = half * 8 + j
                            if c < 12:
                                src = QASC[:, c * 128:(c + 1) * 128]
                                rr = "qasc"
                            elif c < 16:
                                src = CKSC[:, (c - 12) * 128:(c - 11) * 128]
                                rr = "cksc"
                            p.op("pe", lambda h, src=src, j=j, bank=bank:
                                 h.transpose(psb(bank)[:, j * 128:(j + 1) * 128],
                                             src, IDENT.ap()),
                                 reads=[rr, "IDENT"], writes=[f"ps{bank}"])
                        if half == 0:
                            p.op("dve", lambda h, t=t: h.tensor_tensor(
                                QAN[:, 0:8, t * 128:(t + 1) * 128],
                                psb(6).rearrange("p (c u) -> p c u", c=8),
                                LNWQ[:, 0:8].unsqueeze(2).broadcast_to(
                                    [128, 8, 128]), ALU.mult),
                                reads=["ps6", "lnwq"], writes=["qan"])
                        else:
                            p.op("dve", lambda h, t=t: h.tensor_tensor(
                                QAN[:, 8:12, t * 128:(t + 1) * 128],
                                psb(7).rearrange("p (c u) -> p c u", c=8)[:, 0:4, :],
                                LNWQ[:, 8:12].unsqueeze(2).broadcast_to(
                                    [128, 4, 128]), ALU.mult),
                                reads=["ps7", "lnwq"], writes=["qan"])
                            p.op("dve", lambda h, t=t: h.tensor_tensor(
                                CKV[:, 0:4, t * 128:(t + 1) * 128],
                                psb(7).rearrange("p (c u) -> p c u", c=8)[:, 4:8, :],
                                LNWK[:, 0:4].unsqueeze(2).broadcast_to(
                                    [128, 4, 128]), ALU.mult),
                                reads=["ps7", "lnwk"], writes=["ckv"])
                    p.op("pe", lambda h: h.transpose(psb(5)[0:64, 0:128],
                                                     KPSC2, IDENT.ap()),
                         reads=["kpsc2", "IDENT"], writes=["ps5"])
                    p.op("dve", lambda h, t=t: h.tensor_copy(
                        KPE[0:64, t * 128:(t + 1) * 128],
                        psb(5)[0:64, 0:128]),
                        reads=["ps5"], writes=["kpe"])
                    p.op("dve", lambda h, t=t: h.tensor_copy(
                        KPE[64:128, t * 128:(t + 1) * 128],
                        psb(5)[0:64, 0:128]),
                        reads=["ps5"], writes=["kpe"])

                b1_A(0)
                for t in range(NT):
                    b1_B1(t)
                    if t + 1 < NT:
                        b1_A(t + 1)
                    b1_B2(t)
                p.fence()
                if self.debug_taps and s == 0:
                    qantap, ckvtap, kpetap = self._taps
                    for c in range(12):
                        p.dma("sp", qantap[c], QAN[:, c, :], dsem="tapq",
                              dmode="coll")
                    for c in range(4):
                        p.dma("sp", ckvtap[c], CKV[:, c, :], dsem="tapq",
                              dmode="coll")
                    p.dma("sp", kpetap, KPE, dsem="tapq", dmode="coll")
                    p.fence()
                WQBP = AR1.ap()[:, 0:12288].rearrange("p (c f) -> p c f", c=12)
                QPT = AR1.ap()[:, 12288:13312]
                QPR = AR1.ap()[:, 13312:14336]
                TSC = AR1.ap()[:, 14336:14848]
                QPF4 = AR1.ap()[:, 14848:18944].rearrange(
                    "p (c t) -> p c t", c=8)
                for c in range(12):
                    p.dma("gpsimd", WQBP[:, c, :], wqbp[c], writes=[f"wqbp{c}"], dsem="bw2", dmode="coll")
                for t in range(NT):
                    for c in range(12):
                        for j in range(2):
                            p.op("pe", lambda h, c=c, j=j, t=t: h.matmul(
                                PS[j].ap(), QAN[:, c, t * 128:(t + 1) * 128],
                                WQBP[:, c, j * 512:(j + 1) * 512],
                                start=(c == 0), stop=(c == 11)),
                                reads=["qan", f"wqbp{c}"], writes=[f"ps{j}"])
                    for j in range(2):
                        p.op("act", lambda h, j=j: h.copy(
                            QPT[:, j * 512:(j + 1) * 512], PS[j].ap()),
                            reads=[f"ps{j}"], writes=["qpt"])
                    cosb = COS.ap()[:, t, :].unsqueeze(1)
                    sinb = SIN.ap()[:, t, :].unsqueeze(1)
                    qpt3 = QPT.rearrange("p (hh u) -> p hh u", hh=16)
                    qpr3 = QPR.rearrange("p (hh u) -> p hh u", hh=16)
                    tsc3 = TSC.rearrange("p (hh u) -> p hh u", hh=16)
                    p.op("dve", lambda h, cosb=cosb: h.tensor_tensor(
                        qpr3, qpt3, cosb.broadcast_to([128, 16, 64]),
                        ALU.mult), reads=["qpt", "COS"], writes=["qpr"])
                    p.op("dve", lambda h, sinb=sinb: h.tensor_tensor(
                        tsc3[:, :, 0:32], qpt3[:, :, 32:64],
                        sinb.broadcast_to([128, 16, 64])[:, :, 0:32],
                        ALU.mult), reads=["qpt", "SIN"], writes=["qtsc"])
                    p.op("dve", lambda h: h.tensor_sub(
                        qpr3[:, :, 0:32], qpr3[:, :, 0:32], tsc3[:, :, 0:32]),
                        reads=["qtsc", "qpr"], writes=["qpr"])
                    p.op("dve", lambda h, sinb=sinb: h.tensor_tensor(
                        tsc3[:, :, 0:32], qpt3[:, :, 0:32],
                        sinb.broadcast_to([128, 16, 64])[:, :, 32:64],
                        ALU.mult), reads=["qpt", "SIN"], writes=["qtsc"])
                    p.op("dve", lambda h: h.tensor_add(
                        qpr3[:, :, 32:64], qpr3[:, :, 32:64], tsc3[:, :, 0:32]),
                        reads=["qtsc", "qpr"], writes=["qpr"])
                    bank = 5
                    for j in range(8):
                        p.op("pe", lambda h, j=j: h.transpose(
                            psb(bank)[:, j * 128:(j + 1) * 128],
                            QPR[:, j * 128:(j + 1) * 128], IDENT.ap()),
                            reads=["qpr", "IDENT"], writes=[f"ps{bank}"])
                    p.op("dve", lambda h, t=t: h.tensor_copy(
                        QPF4[:, :, (t % 4) * 128:(t % 4 + 1) * 128],
                        psb(bank).rearrange("p (c u) -> p c u", c=8)),
                        reads=[f"ps{bank}"], writes=[f"qpf4_{t%4}"])
                    if t % 4 == 3:
                        p.dma("sp",
                              qpespill[:, :, (t - 3) * 128:(t + 1) * 128]
                              .rearrange("c p u -> p c u"),
                              QPF4[:, :, :],
                              reads=["qpf4_0", "qpf4_1", "qpf4_2", "qpf4_3"],
                              writes=[f"qpesp_{t//4}"], dsem="qpst")
                p.fence()
                WKVBN_SB = AR1.ap()[:, 0:8192].rearrange(
                    "p (hh c u) -> p hh c u", hh=16, c=4)
                WKVBV_SB = AR1.ap()[:, 8192:16384].rearrange(
                    "p (c f) -> p c f", c=4)
                VT4 = AR1.ap()[:, 16384:24576].rearrange(
                    "p (t f) -> p t f", t=16)
                WQBN_ST = [AR4.ap()[:, i * 1536:(i + 1) * 1536].rearrange(
                    "p (c u) -> p c u", c=12) for i in range(2)]
                PBW2 = [AR4.ap()[:, 3072 + i * 1024:3072 + (i + 1) * 1024]
                        for i in range(2)]
                QPEH = [AR4.ap()[:, 5120 + i * 512:5120 + (i + 1) * 512]
                        for i in range(2)]
                QNH = [AR3.ap()[:, i * 1024:(i + 1) * 1024].bitcast(BF16)
                       for i in range(2)]
                KNH = [AR3.ap()[:, 2048 + i * 1024:2048 + (i + 1) * 1024]
                       .bitcast(BF16) for i in range(2)]
                RECB = AR5.ap()[:, 0:512]
                REC1S = [AR5.ap()[:, 512:1024], AR5.ap()[:, 1024:1536]]
                for hh in range(16):
                    p.dma("gpsimd", WKVBN_SB[:, hh, :, :],
                          wkvbn[hh].rearrange("c p u -> p c u"),
                          writes=[f"wkvbn{hh}"], dsem="bw3", dmode="coll")
                for c in range(4):
                    p.dma("gpsimd", WKVBV_SB[:, c, :], wkvbv[c],
                          writes=[f"wkvbv{c}"], dsem="bw3", dmode="coll")
                self._bpend = None
                for hg in range(4):
                    for t in range(NT):
                        bk = t % 2
                        for c in range(4):
                            p.op("pe", lambda h, c=c, t=t, bk=bk, hg=hg:
                                 h.matmul(PS[bk].ap(),
                                          CKV[:, c, t * 128:(t + 1) * 128],
                                          WKVBV_SB[:, c,
                                                   hg * 512:(hg + 1) * 512],
                                          start=(c == 0), stop=(c == 3)),
                                 reads=["ckv", f"wkvbv{c}"],
                                 writes=[f"ps{bk}"])
                        p.op("dve", lambda h, bk=bk, t=t: h.tensor_copy(
                            VT4[:, t, :], PS[bk].ap()),
                            reads=[f"ps{bk}"], writes=[f"vt4_{t}"])
                    for hl in range(4):
                        hh = hg * 4 + hl
                        hp = hh // 2
                        h2 = hh % 2
                        ws = hh % 2
                        p.dma("gpsimd", WQBN_ST[ws], wqbn[hh],
                              writes=[f"wqbnld{ws}"])
                        for tb in range(4):
                            bk = 4 if tb % 2 == 0 else 2
                            for c in range(12):
                                p.op("pe", lambda h, c=c, tb=tb, bk=bk, ws=ws:
                                     h.matmul(
                                         PS[bk].ap(),
                                         WQBN_ST[ws][:, c, :],
                                         QAN[:, c, tb * 512:(tb + 1) * 512],
                                         start=(c == 0), stop=(c == 11)),
                                     reads=[f"wqbnld{ws}", "qan"],
                                     writes=[f"ps{bk}"])
                            p.op("dve", lambda h, bk=bk, tb=tb, ws=ws:
                                 h.tensor_copy(
                                     QNH[ws][:, tb * 512:(tb + 1) * 512],
                                     PS[bk].ap()),
                                 reads=[f"ps{bk}"], writes=[f"qnh{ws}"])
                        for tb in range(4):
                            bk = 4 if tb % 2 == 0 else 2
                            for c in range(4):
                                p.op("pe", lambda h, c=c, tb=tb, bk=bk, hh=hh:
                                     h.matmul(
                                         PS[bk].ap(),
                                         WKVBN_SB[:, hh, c, :],
                                         CKV[:, c, tb * 512:(tb + 1) * 512],
                                         start=(c == 0), stop=(c == 3)),
                                     reads=[f"wkvbn{hh}", "ckv"],
                                     writes=[f"ps{bk}"])
                            p.op("dve", lambda h, bk=bk, tb=tb, ws=ws:
                                 h.tensor_copy(
                                     KNH[ws][:, tb * 512:(tb + 1) * 512],
                                     PS[bk].ap()),
                                 reads=[f"ps{bk}"], writes=[f"knh{ws}"])
                        LAGB = 1
                        for sbk in range(4):
                            qslot = sbk % 2
                            obk = 6 if sbk % 2 == 0 else 7
                            rsb = 5
                            p.dma("sp", QPEH[qslot][0:64, :],
                                  qpespill[hp, 64 * h2:64 * h2 + 64,
                                           sbk * 512:(sbk + 1) * 512],
                                  reads=[f"qpesp_{sbk}"],
                                  writes=[f"qpeh{qslot}"])
                            p.dma("sp", QPEH[qslot][64:128, :],
                                  qpespill[hp, 64 * h2:64 * h2 + 64,
                                           sbk * 512:(sbk + 1) * 512],
                                  reads=[f"qpesp_{sbk}"],
                                  writes=[f"qpeh{qslot}"])
                            nkt = (sbk + 1) * 4
                            exp_last = {}
                            for step in range(nkt // 2 + LAGB):
                                if step < nkt // 2:
                                    k0 = 2 * step
                                    p2 = (k0 // 2) % 2
                                    xd = ([exp_last[k0 - 4]]
                                          if k0 - 4 in exp_last else [])
                                    for dk in range(2):
                                        kt = k0 + dk
                                        scb = 2 * p2 + dk
                                        p.op("pe", lambda h, kt=kt, scb=scb,
                                             ws=ws, sbk=sbk: h.matmul(
                                                 PS[scb].ap(),
                                                 KNH[ws][:, kt * 128:(kt + 1) * 128],
                                                 QNH[ws][:, sbk * 512:(sbk + 1) * 512],
                                                 start=True, stop=False),
                                             reads=[f"knh{ws}", f"qnh{ws}"],
                                             writes=[f"ps{scb}"], deps=xd)
                                    for dk in range(2):
                                        kt = k0 + dk
                                        scb = 2 * p2 + dk
                                        p.op("pe", lambda h, kt=kt, scb=scb,
                                             dk=dk, qslot=qslot: h.matmul(
                                                 PS[scb].ap(),
                                                 KPE[64 * dk:64 * dk + 64,
                                                     kt * 128:(kt + 1) * 128],
                                                 QPEH[qslot][64 * dk:64 * dk + 64, :],
                                                 start=False, stop=True),
                                             reads=["kpe", f"qpeh{qslot}"],
                                             writes=[f"ps{scb}"])
                                    pb = (k0 // 2) % 2
                                    eid = p.op("act", lambda h, p2=p2, pb=pb:
                                               h.activation(
                                                   PBW2[pb],
                                                   PSD[p2].ap(),
                                                   AF.Exp, scale=SCALE_B),
                                               reads=[f"ps{2 * p2}",
                                                      f"ps{2 * p2 + 1}"],
                                               writes=[f"pb{pb}"])
                                    exp_last[k0] = eid
                                    exp_last[k0 + 1] = eid
                                    for dk in range(2):
                                        kt = k0 + dk
                                        if kt >= nkt - 4:
                                            base = sbk * 512 - kt * 128
                                            p.op("gpsimd",
                                                 lambda h, pb=pb, dk=dk,
                                                 base=base:
                                                 h.affine_select(
                                                     PBW2[pb][:, 512 * dk:
                                                              512 * dk + 512],
                                                     PBW2[pb][:, 512 * dk:
                                                              512 * dk + 512],
                                                     [[1, 512]],
                                                     ALU.is_ge, 0.0,
                                                     base=base,
                                                     channel_multiplier=-1),
                                                 reads=[f"pb{pb}"],
                                                 writes=[f"pb{pb}"])
                                if step == 0 and self._bpend is not None:
                                    self._bpend()
                                    self._bpend = None
                                if step >= LAGB:
                                    k0 = 2 * (step - LAGB)
                                    xd = [exp_last[k0 + 1]]
                                    pb = (k0 // 2) % 2
                                    for dk in range(2):
                                        kt = k0 + dk
                                        p.op("pe", lambda h, kt=kt, pb=pb,
                                             dk=dk, hl=hl, nkt=nkt, obk=obk:
                                             h.matmul(
                                                 PS[obk].ap(),
                                                 VT4[:, kt,
                                                     hl * 128:(hl + 1) * 128],
                                                 PBW2[pb][:, 512 * dk:
                                                          512 * dk + 512],
                                                 start=(kt == 0),
                                                 stop=(kt == nkt - 1)),
                                             reads=[f"pb{pb}", f"vt4_{kt}"],
                                             writes=[f"ps{obk}"], deps=xd)
                                        p.op("pe", lambda h, kt=kt, pb=pb,
                                             dk=dk, nkt=nkt, rsb=rsb:
                                             h.matmul(
                                                 PS[rsb].ap()[0:1, :],
                                                 ONES.ap()[:, 0:1],
                                                 PBW2[pb][:, 512 * dk:
                                                          512 * dk + 512],
                                                 start=(kt == 0),
                                                 stop=(kt == nkt - 1)),
                                             reads=[f"pb{pb}", "ONES"],
                                             writes=[f"ps{rsb}"])

                            def bnorm(hh=hh, sbk=sbk, rsb=rsb, obk=obk,
                                      rslt=sbk % 2, asl=sbk % 2):
                                def emit():
                                    p.op("act", lambda h: h.activation(
                                        REC1S[rslt][0:1, :],
                                        PS[rsb].ap()[0:1, :], AF.Ln),
                                        reads=[f"ps{rsb}"],
                                        writes=[f"rec1_{rslt}"])
                                    p.op("act", lambda h: h.activation(
                                        REC1S[rslt][0:1, :],
                                        REC1S[rslt][0:1, :], AF.Exp,
                                        scale=-1.0),
                                        reads=[f"rec1_{rslt}"],
                                        writes=[f"rec1_{rslt}"])
                                    p.op("pe", lambda h: h.matmul(
                                        PS[rsb].ap(), ONESF.ap()[0:1, :],
                                        REC1S[rslt][0:1, :],
                                        start=True, stop=True),
                                        reads=[f"rec1_{rslt}", "ONESF"],
                                        writes=[f"ps{rsb}"])
                                    p.op("act", lambda h: h.copy(
                                        RECB, PS[rsb].ap()),
                                        reads=[f"ps{rsb}"], writes=["recb"])
                                    p.op("dve", lambda h: h.tensor_tensor(
                                        AOSC[asl], PS[obk].ap(), RECB,
                                        ALU.mult),
                                        reads=[f"ps{obk}", "recb"],
                                        writes=[f"aosc{asl}"])
                                    p.dma("sp",
                                          aofspill[hh, :,
                                                   sbk * 512:(sbk + 1) * 512],
                                          AOSC[asl], reads=[f"aosc{asl}"],
                                          writes=[f"aofsp_{hh}_{sbk}"],
                                          dsem=f"aos{asl}")
                                return emit
                            self._bpend = bnorm()
                if self._bpend is not None:
                    self._bpend()
                    self._bpend = None
                p.fence()
                WO_SB = AR1.ap().rearrange("p (c f) -> p c f", c=16)
                AOF_SB = AR2.ap()[:, 0:16384].rearrange(
                    "p (r c t) -> p r c t", r=2, c=16)
                for c in range(16):
                    p.dma("gpsimd", WO_SB[:, c, :], wo_d[c], writes=[f"wo{c}"], dsem="bwo", dmode="coll")
                for sbk in range(4):
                    rr = sbk % 2
                    for c in range(16):
                        p.dma("sp", AOF_SB[:, rr, c, :],
                              aofspill[c, :, sbk * 512:(sbk + 1) * 512],
                              reads=[f"aofsp_{c}_{sbk}"],
                              writes=[f"aofsb{rr}_{c}"],
                              dsem=f"aofl{rr}", dmode="coll")
                    for sub in range(4):
                        t = sbk * 4 + sub
                        xslot = t % 3
                        xtile = xt_ring[xslot]
                        ytile = AR3.ap()[:, 6144 + (t % 2) * 2048:
                                         6144 + (t % 2 + 1) * 2048]
                        p.dma("sp", xtile, xin[s, t * 128:(t + 1) * 128, :],
                              writes=[f"xt{xslot}"])
                        for c in range(16):
                            for j in range(4):
                                p.op("pe", lambda h, c=c, j=j, sub=sub,
                                     rr=rr: h.matmul(
                                         PS[j].ap(),
                                         AOF_SB[:, rr, c,
                                                sub * 128:(sub + 1) * 128],
                                         WO_SB[:, c, j * 512:(j + 1) * 512],
                                         start=(c == 0), stop=(c == 15)),
                                     reads=[f"aofsb{rr}_{c}", f"wo{c}"],
                                     writes=[f"ps{j}"])
                        for j in range(4):
                            p.op("dve", lambda h, j=j, xtile=xtile,
                                 ytile=ytile: h.tensor_add(
                                     ytile[:, j * 512:(j + 1) * 512],
                                     PS[j].ap(),
                                     xtile[:, j * 512:(j + 1) * 512]),
                                 reads=[f"ps{j}", f"xt{xslot}"],
                                 writes=[f"yts{t%2}"])
                        p.dma("sp", xout[s, t * 128:(t + 1) * 128, :], ytile,
                              reads=[f"yts{t%2}"], dsem=f"yso{t%2}")
                p.fence()

        subs = list(self.subs)
        chain_in = {0: x_in, 1: xA, 2: xB, 3: xA}
        chain_out = {0: xA, 1: xB, 2: xA, 3: y_out}
        for i in subs:
            xi = x_in if i == subs[0] else chain_in[i]
            if i == subs[-1] and i != 0:
                xo = y_out
            else:
                xo = chain_out[i]
            if i == 0:
                emit_attn_a(xi, xo)
            elif i == 2:
                emit_attn_b(xi, xo)
            else:
                emit_ffn(xi, xo, f"ffn{i}")
        if subs[-1] == 0:
            for s in range(ns):
                for t in range(NT):
                    xt = xt_ring[t % 2]
                    p.dma("sp", xt, xA[s, t * 128:(t + 1) * 128, :],
                          writes=[f"cp{t%2}"])
                    p.dma("sp", y_out[s, t * 128:(t + 1) * 128, :], xt,
                          reads=[f"cp{t%2}"], writes=[f"cpd{t%2}"],
                          dsem=f"cps{t%2}")

        p.emit(nc)
        ctx.close()



def host_layouts(I):
    f32 = np.float32
    out = {}
    out["wq"] = np.ascontiguousarray(
        I["L0_attn_a_Wq"].T.reshape(16, 128, 4, 512).transpose(2, 0, 1, 3))
    out["wk"] = np.ascontiguousarray(I["L0_attn_a_Wk"].T.reshape(16, 128, 512))
    out["wv"] = np.ascontiguousarray(I["L0_attn_a_Wv"].T.reshape(16, 128, 512))
    out["mem"] = np.ascontiguousarray(
        I["L0_attn_a_memory"].reshape(4, 128, 512))
    out["memn"] = np.ascontiguousarray(
        I["L0_attn_a_memory_norm"].reshape(4, 128, 1))
    out["memw"] = I["L0_attn_a_memory_weight"].reshape(1, 1).astype(f32)
    out["wqa"] = np.ascontiguousarray(I["L0_attn_b_Wqa"].reshape(16, 128, 1536))
    out["lnwq"] = np.ascontiguousarray(
        I["L0_attn_b_q_a_ln_w"].reshape(12, 128).T)
    Wqb = I["L0_attn_b_Wqb"].reshape(12, 128, 16, 192)
    out["wqbn"] = np.ascontiguousarray(Wqb[:, :, :, :128].transpose(2, 1, 0, 3))
    out["wqbp"] = np.ascontiguousarray(
        Wqb[:, :, :, 128:].reshape(12, 128, 16 * 64))
    Wkva = I["L0_attn_b_Wkva"].reshape(16, 128, 576)
    out["wkva5"] = np.ascontiguousarray(Wkva[:, :, 0:512])
    out["wkvap"] = np.ascontiguousarray(Wkva[:, :, 512:576])
    out["lnwk"] = np.ascontiguousarray(
        I["L0_attn_b_kv_a_ln_w"].reshape(4, 128).T)
    Wkvb = I["L0_attn_b_Wkvb"].reshape(4, 128, 16, 256)
    out["wkvbn"] = np.ascontiguousarray(Wkvb[:, :, :, :128].transpose(2, 0, 1, 3))
    out["wkvbv"] = np.ascontiguousarray(
        Wkvb[:, :, :, 128:].reshape(4, 128, 16 * 128))
    out["wo"] = np.ascontiguousarray(I["L0_attn_b_Wo"].reshape(16, 128, 2048))
    out["wb"] = np.ascontiguousarray(I["L0_ffn_W_base"].T.reshape(16, 128, 2048))
    ha = np.concatenate([
        I["L0_ffn_A_w"].transpose(2, 0, 1).reshape(2048, 64),
        I["L0_ffn_W_router"].T], axis=1)
    out["ha"] = np.ascontiguousarray(ha.reshape(16, 128, 72))
    out["bb"] = I["L0_ffn_b_base"].reshape(1, 2048).astype(f32)
    out["bflat"] = np.ascontiguousarray(
        I["L0_ffn_B_w"].transpose(0, 2, 1).reshape(64, 2048))
    inv = 1.0 / 10000.0 ** (np.arange(0, 64, 2, dtype=np.float64) / 64)
    tpos = np.arange(S, dtype=np.float64)
    fr = np.outer(tpos, inv)
    emb = np.concatenate([fr, fr], axis=1)
    out["cos"] = np.ascontiguousarray(np.cos(emb).astype(f32).reshape(16, 128, 64))
    out["sin"] = np.ascontiguousarray(np.sin(emb).astype(f32).reshape(16, 128, 64))
    for k in out:
        out[k] = out[k].astype(f32)
    return out


_BUILDER_CACHE = {}


def get_builder(n_seq=B_LOC, subs=(0, 1, 2, 3)):
    key = (n_seq, tuple(subs))
    if key not in _BUILDER_CACHE:
        _BUILDER_CACHE[key] = Builder(n_seq, subs)
    return _BUILDER_CACHE[key]


def kernel(**inputs):
    w = host_layouts(inputs)
    x = np.ascontiguousarray(inputs["L0_attn_a_x"], dtype=np.float32)
    b = get_builder()
    in_maps = []
    for core in range(N_CORES):
        m = dict(w)
        m["x_in"] = np.ascontiguousarray(x[core * B_LOC:(core + 1) * B_LOC])
        in_maps.append(m)
    res = run_bass_kernel_spmd(b.nc, in_maps, core_ids=list(range(N_CORES)))
    return np.concatenate([r["y"] for r in res.results], axis=0)


# version 18 (final)
# speedup vs baseline: 1.1043x; 1.0626x over previous
import numpy as np
import ml_dtypes
from contextlib import ExitStack

import concourse.bass as bass
import concourse.mybir as mybir
from concourse.bass_utils import run_bass_kernel_spmd

F32 = mybir.dt.float32
BF16 = mybir.dt.bfloat16
AF = mybir.ActivationFunctionType
ALU = mybir.AluOpType
AX = mybir.AxisListType

N_CORES = 8
B = 16
S = 2048
D = 2048
B_LOC = B // N_CORES
NT = S // 128
GT = 8
NG = NT // GT
EXP_SHIFT_A = -4.0
SCALE_A = 0.125
SCALE_B = float((128 + 64) ** -0.5)
LN_EPS = 1e-5
RMS_EPS = 1e-6



ENGS = ("pe", "dve", "act", "gpsimd", "sp")


class Prog:

    def __init__(self):
        self.ops = []
        self.last_writer = {}
        self.readers = {}
        self.last_reader = {}
        self.fence_deps = []

    def op(self, eng, fn, reads=(), writes=(), kind="c", deps=(),
           dsem=None, dmode="slot"):
        opid = len(self.ops)
        d = set(deps)
        d.update(self.fence_deps)
        for r in reads:
            w = self.last_writer.get(r)
            if w is not None:
                d.add(w)
            if r.startswith("ps"):
                lr = self.last_reader.get(r)
                if lr is not None and lr[1] != eng:
                    d.add(lr[0])
                self.last_reader[r] = (opid, eng)
        for w in writes:
            lw = self.last_writer.get(w)
            if lw is not None:
                d.add(lw)
            d.update(self.readers.get(w, ()))
        d.discard(opid)
        self.ops.append(dict(eng=eng, kind=kind, fn=fn, deps=d, id=opid,
                             dsem=dsem, dmode=dmode))
        for r in reads:
            self.readers.setdefault(r, set()).add(opid)
        for w in writes:
            self.last_writer[w] = opid
            self.readers[w] = set()
            if w.startswith("ps"):
                self.last_reader.pop(w, None)
        return opid

    def dma(self, q, out, in_, reads=(), writes=(), dsem=None, dmode="slot"):
        if dsem is None:
            assert writes, "dma needs dsem or a written slot resource"
            dsem = "D_" + writes[0]
        def fn(h, out=out, in_=in_):
            return h.dma_start(out=out, in_=in_)
        return self.op(q, fn, reads=reads, writes=writes, kind="d",
                       dsem=dsem, dmode=dmode)

    def fence(self):
        last = {}
        for o in self.ops:
            if o["kind"] == "d":
                last[("dsem", o["dsem"])] = o["id"]
            else:
                last[(o["eng"], o["kind"])] = o["id"]
        self.fence_deps = list(last.values())
        self.last_writer.clear()
        self.readers.clear()
        self.last_reader.clear()

    @staticmethod
    def _fifo_safe(prod, cons):
        return (prod["eng"] == cons["eng"] == "pe"
                and prod["kind"] == "c" and cons["kind"] == "c")

    def emit(self, nc, strict=False):
        ops = self.ops
        needs = [False] * len(ops)
        for o in ops:
            for d in o["deps"]:
                dd = ops[d]
                if strict or not self._fifo_safe(dd, o):
                    needs[d] = True
        last = {}
        for o in ops:
            last[(o["eng"], o["kind"])] = o["id"]
        for opid in last.values():
            needs[opid] = True

        csem_names = sorted({(o["eng"], o["kind"]) for o in ops
                             if o["kind"] == "c"})
        dsem_info = {}
        dmaval = {}
        for o in ops:
            if o["kind"] == "d":
                di = dsem_info.setdefault(o["dsem"],
                                          dict(count=0, mode=o["dmode"],
                                               ids=[]))
                assert di["mode"] == o["dmode"], o["dsem"]
                di["count"] += 1
                di["ids"].append(o["id"])
                dmaval[o["id"]] = 16 * di["count"]

        with ExitStack() as ctx:
            sems = {sn: ctx.enter_context(nc.semaphore(f"s_{sn[0]}_{sn[1]}"))
                    for sn in csem_names}
            dsems = {dn: ctx.enter_context(nc.semaphore(f"d_{dn}"))
                     for dn in dsem_info}
            counts = {sn: 0 for sn in csem_names}
            cum = {}
            for o in ops:
                if o["kind"] == "c" and needs[o["id"]]:
                    sn = (o["eng"], o["kind"])
                    counts[sn] += 1
                    cum[o["id"]] = (sn, counts[sn])
            finals = dict(counts)

            def dma_wait_val(dep_id, waiter_id):
                o = ops[dep_id]
                di = dsem_info[o["dsem"]]
                if di["mode"] == "slot":
                    return dmaval[dep_id]
                n = sum(1 for i in di["ids"] if i < waiter_id)
                return 16 * n

            block = ctx.enter_context(nc.Block())
            eng_ops = {e: [o for o in ops if o["eng"] == e] for e in ENGS}
            all_sem_keys = list(csem_names) + list(dsems)
            vclock = {e: {k: 0 for k in all_sem_keys} for e in ENGS}

            def run_engine(e, handle):
                vc = vclock[e]
                for o in eng_ops[e]:
                    waits = {}
                    for d in o["deps"]:
                        dd = ops[d]
                        if not strict and self._fifo_safe(dd, o):
                            continue
                        if dd["kind"] == "d":
                            k = dd["dsem"]
                            v = dma_wait_val(d, o["id"])
                        else:
                            if d not in cum:
                                continue
                            k, v = cum[d]
                        if vc[k] < v:
                            waits[k] = max(waits.get(k, 0), v)
                    for k, v in sorted(waits.items(), key=str):
                        handle.wait_ge(sems[k] if k in sems else dsems[k], v)
                        vc[k] = v
                    inst = o["fn"](handle)
                    if o["kind"] == "d":
                        inst.then_inc(dsems[o["dsem"]], 16)
                    elif o["id"] in cum:
                        sn, _ = cum[o["id"]]
                        inst.then_inc(sems[sn], 1)

            @block.tensor
            def _(te):
                run_engine("pe", te)

            @block.vector
            def _(v):
                run_engine("dve", v)

            @block.scalar
            def _(sc):
                run_engine("act", sc)

            @block.gpsimd
            def _(gp):
                run_engine("gpsimd", gp)

            @block.sync
            def _(sy):
                run_engine("sp", sy)
                for sn in csem_names:
                    if finals[sn] > vclock["sp"][sn]:
                        sy.wait_ge(sems[sn], finals[sn])
                for dn, di in dsem_info.items():
                    if 16 * di["count"] > vclock["sp"][dn]:
                        sy.wait_ge(dsems[dn], 16 * di["count"])



class Builder:
    def __init__(self, n_seq=B_LOC, subs=(0, 1, 2, 3), debug_taps=False):
        self.n_seq = n_seq
        self.subs = tuple(subs)
        self.debug_taps = debug_taps
        self.nc = bass.Bass("TRN2", target_bir_lowering=False)
        self.p = Prog()
        self._build()

    def _build(self):
        nc = self.nc
        p = self.p
        ns = self.n_seq

        def din(name, shape, dt=F32):
            return nc.dram_tensor(name, list(shape), dt,
                                  kind="ExternalInput").ap()

        x_in = din("x_in", [ns, S, D])
        y_out = nc.dram_tensor("y", [ns, S, D], F32, kind="ExternalOutput").ap()
        xA = nc.dram_tensor("xA", [ns, S, D], F32, kind="Internal").ap()
        xB = nc.dram_tensor("xB", [ns, S, D], F32, kind="Internal").ap()
        tapkind = "ExternalOutput" if self.debug_taps else "Internal"
        qspill = nc.dram_tensor("qspill", [ns, 16, 128, S], BF16,
                                kind="Internal").ap()
        qpespill = nc.dram_tensor("qpespill", [8, 128, S], BF16,
                                  kind=tapkind).ap()
        aofspill = nc.dram_tensor("aofspill", [16, 128, S], BF16,
                                  kind=tapkind).ap()
        if self.debug_taps:
            qantap = nc.dram_tensor("qantap", [12, 128, S], BF16,
                                    kind="ExternalOutput").ap()
            ckvtap = nc.dram_tensor("ckvtap", [4, 128, S], BF16,
                                    kind="ExternalOutput").ap()
            kpetap = nc.dram_tensor("kpetap", [128, S], BF16,
                                    kind="ExternalOutput").ap()
            self._taps = (qantap, ckvtap, kpetap)

        wq = din("wq", [4, 16, 128, 512], BF16)
        wk = din("wk", [16, 128, 512], BF16)
        wv = din("wv", [16, 128, 512], BF16)
        memw_d = din("memw", [1, 1])
        mem_d = din("mem", [4, 128, 512], BF16)
        memn_d = din("memn", [4, 128, 1])
        wqa = din("wqa", [16, 128, 1536], BF16)
        lnwq_d = din("lnwq", [128, 12])
        wqbn = din("wqbn", [16, 128, 12, 128], BF16)
        wqbp = din("wqbp", [12, 128, 1024], BF16)
        wkva5 = din("wkva5", [16, 128, 512], BF16)
        wkvap = din("wkvap", [16, 128, 64], BF16)
        lnwk_d = din("lnwk", [128, 4])
        wkvbn = din("wkvbn", [16, 4, 128, 128], BF16)
        wkvbv = din("wkvbv", [4, 128, 2048], BF16)
        wo_d = din("wo", [16, 128, 2048], BF16)
        wb_d = din("wb", [16, 128, 2048], BF16)
        ha_d = din("ha", [16, 128, 72], BF16)
        bb_d = din("bb", [1, 2048], BF16)
        bflat_d = din("bflat", [64, 2048], BF16)
        cos_d = din("cos", [16, 128, 64])
        sin_d = din("sin", [16, 128, 64])

        ctx = ExitStack()
        sb = lambda name, shape, dt: ctx.enter_context(
            nc.sbuf_tensor(name, shape, dt))
        AR1 = sb("AR1", [128, 32768], BF16)
        AR2 = sb("AR2", [128, 32768], BF16)
        AR3 = sb("AR3", [128, 10240], F32)
        AR4 = sb("AR4", [128, 8192], BF16)
        AR5 = sb("AR5", [128, 3072], F32)
        IDENT = sb("IDENT", [128, 128], BF16)
        ONES = sb("ONES", [128, 128], BF16)
        ONESF = sb("ONESF", [1, 128], F32)
        COS = sb("COS", [128, 16, 64], F32)
        SIN = sb("SIN", [128, 16, 64], F32)
        STAT = sb("STAT", [128, 64], F32)
        PSD = [ctx.enter_context(nc.psum_tensor(f"psd{i}", [128, 1024], F32))
               for i in range(2)]
        PS_hi = [ctx.enter_context(nc.psum_tensor(f"ps{i}", [128, 512], F32))
                 for i in range(4, 8)]

        class _PSView:
            def __init__(self, ap_):
                self._ap = ap_

            def ap(self):
                return self._ap

        PS = ([_PSView(PSD[i // 2].ap()[:, (i % 2) * 512:(i % 2 + 1) * 512])
               for i in range(4)] + PS_hi)

        def psb(i):
            if i < 4:
                return PSD[i // 2].ap().bitcast(BF16)[
                    :, (i % 2) * 1024:(i % 2 + 1) * 1024]
            return PS[i].ap().bitcast(BF16)

        st = STAT.ap()
        xt_ring = [AR3.ap()[:, i * 2048:(i + 1) * 2048] for i in range(3)]

        p.op("dve", lambda h: h.memset(ONES.ap(), 1.0), writes=["ONES"])
        p.op("dve", lambda h: h.memset(ONESF.ap(), 1.0), writes=["ONESF"])
        p.op("gpsimd", lambda h: h.affine_select(
            IDENT.ap(), ONES.ap(), [[1, 128]], ALU.is_equal, 0.0, base=0,
            channel_multiplier=-1), reads=["ONES"], writes=["IDENT"])
        p.dma("sp", COS.ap(), cos_d.rearrange("t p f -> p t f"), writes=["COS"], dsem="pre", dmode="coll")
        p.dma("sp", SIN.ap(), sin_d.rearrange("t p f -> p t f"), writes=["SIN"], dsem="pre", dmode="coll")
        p.dma("sp", AR5.ap()[0:1, 0:1], memw_d, writes=["mwraw"], dsem="pre", dmode="coll")
        p.op("act", lambda h: h.activation(AR5.ap()[0:1, 1:2],
                                           AR5.ap()[0:1, 0:1], AF.Exp,
                                           scale=-1.0),
             reads=["mwraw"], writes=["mwexp"])
        p.op("dve", lambda h: h.tensor_scalar_add(
            AR5.ap()[0:1, 2:3], AR5.ap()[0:1, 1:2], 1.0),
            reads=["mwexp"], writes=["mwp1"])
        p.op("dve", lambda h: h.reciprocal(AR5.ap()[0:1, 3:4],
                                           AR5.ap()[0:1, 2:3]),
             reads=["mwp1"], writes=["mww"])
        p.op("pe", lambda h: h.matmul(PS[0].ap()[:, 0:1], ONESF.ap()[0:1, :],
                                      AR5.ap()[0:1, 3:4], start=True,
                                      stop=True),
             reads=["mww", "ONESF"], writes=["ps0"])
        p.op("dve", lambda h: h.tensor_copy(st[:, 0:1], PS[0].ap()[:, 0:1]),
             reads=["ps0"], writes=["wrep"])
        p.op("dve", lambda h: h.tensor_scalar(st[:, 1:2], PS[0].ap()[:, 0:1],
                                              -1.0, 1.0, ALU.mult, ALU.add),
             reads=["ps0"], writes=["omwrep"])
        W_REP = st[:, 0:1]
        OMW_REP = st[:, 1:2]
        for i, v in enumerate([LN_EPS, RMS_EPS, EXP_SHIFT_A]):
            col = 60 + i
            p.op("dve", lambda h, col=col, v=v: h.memset(
                st[:, col:col + 1], v), writes=[f"cconst{i}"])
            nc.const_aps.register(F32, v, st[:, col:col + 1])
        p.fence()

        def ln_tile(xin, s, t, xt_slot, xn_fm_dst, dst_res, sq_scr, xn_scr,
                    xn_res, par):
            xt = xt_ring[xt_slot]
            rxt = f"xt{xt_slot}"
            o = 8 + 8 * par
            p.dma("sp", xt, xin[s, t * 128:(t + 1) * 128, :], writes=[rxt])
            p.op("dve", lambda h: h.reduce_sum(st[:, o:o + 1], xt, axis=AX.X),
                 reads=[rxt], writes=[f"lnsum{par}"])
            p.op("act", lambda h: h.activation(sq_scr, xt, AF.Square,
                                               accum_out=st[:, o + 1:o + 2]),
                 reads=[rxt], writes=[f"lnsq{par}", "SQSCR"])
            p.op("dve", lambda h: h.tensor_scalar_mul(st[:, o + 2:o + 3],
                                                      st[:, o:o + 1], 1.0 / D),
                 reads=[f"lnsum{par}"], writes=[f"lnmean{par}"])
            p.op("dve", lambda h: h.tensor_mul(st[:, o + 3:o + 4],
                                               st[:, o + 2:o + 3],
                                               st[:, o + 2:o + 3]),
                 reads=[f"lnmean{par}"], writes=[f"lnmsq{par}"])
            p.op("dve", lambda h: h.tensor_scalar(
                st[:, o + 4:o + 5], st[:, o + 1:o + 2], 1.0 / D,
                st[:, o + 3:o + 4], ALU.mult, ALU.subtract),
                reads=[f"lnsq{par}", f"lnmsq{par}"], writes=[f"lnvar{par}"])
            p.op("act", lambda h: h.activation(st[:, o + 5:o + 6],
                                               st[:, o + 4:o + 5], AF.Ln,
                                               bias=LN_EPS),
                 reads=[f"lnvar{par}"], writes=[f"lnln{par}"])
            p.op("act", lambda h: h.activation(st[:, o + 6:o + 7],
                                               st[:, o + 5:o + 6], AF.Exp,
                                               scale=-0.5),
                 reads=[f"lnln{par}"], writes=[f"lnrstd{par}"])
            p.op("dve", lambda h: h.tensor_scalar(
                xn_scr, xt, st[:, o + 2:o + 3], st[:, o + 6:o + 7],
                ALU.subtract, ALU.mult),
                reads=[rxt, f"lnmean{par}", f"lnrstd{par}"], writes=[xn_res])
            for half in range(2):
                bank = 6 + half
                for j in range(8):
                    c = half * 8 + j
                    p.op("pe", lambda h, c=c, j=j, bank=bank: h.transpose(
                        psb(bank)[:, j * 128:(j + 1) * 128],
                        xn_scr[:, c * 128:(c + 1) * 128], IDENT.ap()),
                        reads=[xn_res, "IDENT"], writes=[f"ps{bank}"])
                dst = xn_fm_dst(half * 8)
                p.op("dve", lambda h, bank=bank, dst=dst: h.tensor_copy(
                    dst, psb(bank).rearrange("p (c t) -> p c t", c=8)),
                    reads=[f"ps{bank}"], writes=[dst_res])
            return rxt

        def emit_attn_a(xin, xout):
            XNF = AR1.ap()[:, 0:16384].rearrange("p (c t) -> p c t", c=16)
            SIG = AR1.ap()[:, 16384:32768].rearrange("p (c t) -> p c t", c=16)
            QTMP = AR2.ap()[:, 0:8192].rearrange("p (c t) -> p c t", c=8)
            KC = AR2.ap()[:, 8192:16384].rearrange("p (c t) -> p c t", c=4)
            VT = AR2.ap()[:, 16384:25088].rearrange("p (t c) -> p t c", t=16)
            XN_SCR = [AR2.ap()[:, 25088 + i * 2048:25088 + (i + 1) * 2048]
                      for i in range(2)]
            WQ_ST = [AR4.ap()[:, i * 512:(i + 1) * 512] for i in range(4)]
            WK_ST = [AR4.ap()[:, 2048 + i * 512:2048 + (i + 1) * 512]
                     for i in range(2)]
            MEM_SB = AR4.ap()[:, 3072:5120].rearrange("p (c f) -> p c f", c=4)
            MEMN_SB = AR4.ap()[:, 5120:5184].rearrange("p (c f) -> p c f", c=4)
            SQ_SCR = AR4.ap()[:, 6144:8192]
            WVB = AR3.ap()[:, 6144:10240].bitcast(BF16).rearrange(
                "p (c f) -> p c f", c=16)

            ROT = ("gpsimd", "act")
            for c in range(4):
                p.dma(ROT[c % 2], MEM_SB[:, c, :], mem_d[c], writes=[f"mem{c}"], dsem=f"awt{c%2}", dmode="coll")
                p.dma("gpsimd", MEMN_SB[:, c, 0:1], memn_d[c],
                      writes=[f"memn{c}"], dsem="awt0", dmode="coll")
            for c in range(16):
                p.dma(ROT[c % 2], WVB[:, c, :], wv[c], writes=[f"wv{c}"], dsem=f"awt{c%2}", dmode="coll")

            for s in range(ns):
                for g in range(NG):
                    for tt in range(GT):
                        t = g * GT + tt
                        def dstf(c0, tt=tt):
                            return XNF[:, c0:c0 + 8, tt * 128:(tt + 1) * 128]
                        ln_tile(xin, s, t, tt % 2, dstf, f"xnf{tt//4}",
                                SQ_SCR, XN_SCR[tt % 2], f"xnscr{tt%2}",
                                tt % 2)
                    for fq in range(4):
                        for c in range(16):
                            ws = c % 4
                            p.dma(("gpsimd", "act")[ws % 2],
                                  WQ_ST[ws], wq[fq, c],
                                  writes=[f"wqld{ws}"])
                            for fb in range(4):
                                for nh in range(2):
                                    bk = fb * 2 + nh
                                    p.op("pe", lambda h, c=c, fb=fb, nh=nh,
                                         bk=bk, ws=ws: h.matmul(
                                             PS[bk].ap(),
                                             WQ_ST[ws][:, fb * 128:(fb + 1) * 128],
                                             XNF[:, c, nh * 512:(nh + 1) * 512],
                                             start=(c == 0), stop=(c == 15)),
                                         reads=[f"wqld{ws}", f"xnf{nh}"],
                                         writes=[f"ps{bk}"])
                        for fb in range(4):
                            cg = fq * 4 + fb
                            qs = (fq % 2) * 4 + fb
                            for nh in range(2):
                                bk = fb * 2 + nh
                                p.op("dve", lambda h, bk=bk, qs=qs, nh=nh:
                                     h.tensor_copy(
                                         QTMP[:, qs, nh * 512:(nh + 1) * 512],
                                         PS[bk].ap()),
                                     reads=[f"ps{bk}"], writes=[f"qtmp{qs}"])
                            p.dma("sp",
                                  qspill[s, cg, :, g * 1024:(g + 1) * 1024],
                                  QTMP[:, qs, :], reads=[f"qtmp{qs}"],
                                  dsem=f"qsps{qs}")
                            p.op("dve", lambda h, qs=qs, cg=cg:
                                 h.tensor_scalar_min(SIG[:, cg, :],
                                                     QTMP[:, qs, :], 0.0),
                                 reads=[f"qtmp{qs}"], writes=[f"sig{cg}"])
                            p.op("act", lambda h, cg=cg: h.activation(
                                SIG[:, cg, :], SIG[:, cg, :], AF.Exp),
                                reads=[f"sig{cg}"], writes=[f"sig{cg}"])
                            p.op("dve", lambda h, qs=qs, cg=cg:
                                 h.scalar_tensor_tensor(
                                     SIG[:, cg, :], SIG[:, cg, :], -1.0,
                                     QTMP[:, qs, :], ALU.add, ALU.max),
                                 reads=[f"qtmp{qs}"], writes=[f"sig{cg}"])
                    for c in range(16):
                        ws = c % 2
                        p.dma(("gpsimd", "act")[ws], WK_ST[ws], wk[c],
                              writes=[f"wkld{ws}"])
                        for fb in range(4):
                            for nh in range(2):
                                bk = fb * 2 + nh
                                p.op("pe", lambda h, c=c, fb=fb, nh=nh, bk=bk,
                                     ws=ws: h.matmul(
                                         PS[bk].ap(),
                                         WK_ST[ws][:, fb * 128:(fb + 1) * 128],
                                         XNF[:, c, nh * 512:(nh + 1) * 512],
                                         start=(c == 0), stop=(c == 15)),
                                     reads=[f"wkld{ws}", "xnf"],
                                     writes=[f"ps{bk}"])
                    for fb in range(4):
                        for nh in range(2):
                            bk = fb * 2 + nh
                            p.op("dve", lambda h, bk=bk, fb=fb, nh=nh, g=g:
                                 h.tensor_copy(
                                     KC[:, fb, g * 1024 + nh * 512:
                                        g * 1024 + (nh + 1) * 512],
                                     PS[bk].ap()),
                                 reads=[f"ps{bk}"], writes=["kc"])
                    for tt in range(GT):
                        t = g * GT + tt
                        bk = tt % 2
                        for c in range(16):
                            p.op("pe", lambda h, c=c, tt=tt, bk=bk: h.matmul(
                                PS[bk].ap(),
                                XNF[:, c, tt * 128:(tt + 1) * 128],
                                WVB[:, c, :], start=(c == 0), stop=(c == 15)),
                                reads=[f"xnf{tt//4}", f"wv{c}"],
                                writes=[f"ps{bk}"])
                        p.op("dve", lambda h, bk=bk, t=t: h.tensor_copy(
                            VT[:, t, 0:520].rearrange(
                                "p (hh c) -> p hh c", hh=8)[:, :, 0:64],
                            PS[bk].ap().rearrange("p (hh c) -> p hh c", hh=8)),
                            reads=[f"ps{bk}"], writes=[f"vt{t}"])
                        p.op("dve", lambda h, t=t: h.memset(
                            VT[:, t, 0:520].rearrange(
                                "p (hh c) -> p hh c", hh=8)[:, :, 64:65],
                            1.0), writes=[f"vt{t}"])
                    for gq in range(4):
                        for tt in range(GT):
                            t = g * GT + tt
                            bk = tt % 2
                            bkd = 2 + tt % 2
                            for cp in range(4):
                                p.op("pe", lambda h, gq=gq, tt=tt, cp=cp,
                                     bk=bk: h.matmul(
                                         PS[bk].ap(),
                                         SIG[:, 4 * gq + cp,
                                             tt * 128:(tt + 1) * 128],
                                         MEM_SB[:, cp, :],
                                         start=(cp == 0), stop=(cp == 3)),
                                     reads=[f"sig{4*gq+cp}", f"mem{cp}"],
                                     writes=[f"ps{bk}"])
                                p.op("pe", lambda h, gq=gq, tt=tt, cp=cp,
                                     bkd=bkd: h.matmul(
                                         PS[bkd].ap()[:, 0:1],
                                         SIG[:, 4 * gq + cp,
                                             tt * 128:(tt + 1) * 128],
                                         MEMN_SB[:, cp, 0:1],
                                         start=(cp == 0), stop=(cp == 3)),
                                     reads=[f"sig{4*gq+cp}", f"memn{cp}"],
                                     writes=[f"ps{bkd}"])
                            rc = 24 + tt % 2
                            p.op("dve", lambda h, bkd=bkd, rc=rc:
                                 h.reciprocal(st[:, rc:rc + 1],
                                              PS[bkd].ap()[:, 0:1]),
                                 reads=[f"ps{bkd}"], writes=[f"amrec{tt%2}"])
                            amw = AR5.ap()[:, 1024 + (tt % 2) * 512:
                                           1024 + (tt % 2 + 1) * 512]
                            p.op("dve", lambda h, bk=bk, rc=rc, amw=amw:
                                 h.tensor_scalar(amw, PS[bk].ap(),
                                                 st[:, rc:rc + 1], W_REP,
                                                 ALU.mult, ALU.mult),
                                 reads=[f"ps{bk}", f"amrec{tt%2}", "wrep"],
                                 writes=[f"amw{tt%2}"])
                            p.dma("sp",
                                  xout[s, t * 128:(t + 1) * 128,
                                       gq * 512:(gq + 1) * 512],
                                  amw, reads=[f"amw{tt%2}"],
                                  writes=[f"dxo_{t}"], dsem=f"amws{tt%2}")
                p.fence()
                QH = [AR1.ap()[:, 0:512], AR1.ap()[:, 512:1024],
                      AR1.ap()[:, 16384:16896]]
                PBW = [AR1.ap()[:, 1024 + i * 1024:1024 + (i + 1) * 1024]
                       for i in range(3)]
                ADN = AR1.ap()[:, 4096:12288].rearrange(
                    "p (c t) -> p c t", c=16)
                RECB = AR5.ap()[:, 0:512]
                REC1S = [AR5.ap()[:, 512:1024], AR5.ap()[:, 1024:1536]]
                YTC = AR1.ap()[:, 12288:16384].bitcast(F32)
                def make_comb(s, sbk, sub):
                    def emit():
                        t = sbk * 4 + sub
                        xslot = sub % 2
                        xtile = xt_ring[xslot]
                        ptile = xt_ring[2]
                        p.dma("sp", xtile, xin[s, t * 128:(t + 1) * 128, :],
                              writes=[f"xt{xslot}"])
                        p.dma("sp", ptile, xout[s, t * 128:(t + 1) * 128, :],
                              reads=[f"dxo_{t}"], writes=["xt2"])
                        p.op("dve", lambda h: h.tensor_add(YTC, xtile,
                                                           ptile),
                             reads=["xt2", f"xt{xslot}"],
                             writes=["ytc"])
                        for half in range(2):
                            bank = 6 + half
                            for j in range(8):
                                fc = half * 8 + j
                                p.op("pe", lambda h, fc=fc, j=j, bank=bank:
                                     h.transpose(
                                         psb(bank)[:, j * 128:(j + 1) * 128],
                                         ADN[:, fc, sub * 128:(sub + 1) * 128],
                                         IDENT.ap()),
                                     reads=[f"adn{fc}", "IDENT"],
                                     writes=[f"ps{bank}"])
                            p.op("dve", lambda h, bank=bank, half=half:
                                 h.scalar_tensor_tensor(
                                     YTC[:, half * 1024:(half + 1) * 1024],
                                     psb(bank), OMW_REP,
                                     YTC[:, half * 1024:(half + 1) * 1024],
                                     ALU.mult, ALU.add),
                                 reads=[f"ps{bank}", "omwrep"],
                                 writes=["ytc"])
                        p.dma("sp", xout[s, t * 128:(t + 1) * 128, :], YTC,
                              reads=["ytc"], writes=[f"dxo_{t}"],
                              dsem="ytcs")
                    return emit

                self._acomb = []
                for sbk in range(4):
                    LAG = 2

                    def norm_closure(cq, bks):
                        def emit():
                            for h2 in range(2):
                                p.op("dve", lambda h, h2=h2: h.reciprocal(
                                    REC1S[h2][0:1, :],
                                    PS[bks[h2]].ap()[64:65, :]),
                                    reads=[f"ps{bks[h2]}"],
                                    writes=[f"rec1_{h2}"])
                            for h2 in range(2):
                                bk = bks[h2]
                                p.op("pe", lambda h, h2=h2, bk=bk: h.matmul(
                                    PS[bk].ap()[64:128, :],
                                    ONESF.ap()[0:1, 0:64],
                                    REC1S[h2][0:1, :], start=True, stop=True,
                                    tile_position=(0, 64)),
                                    reads=[f"rec1_{h2}", "ONESF"],
                                    writes=[f"ps{bk}"])
                                p.op("act", lambda h, h2=h2, bk=bk: h.copy(
                                    RECB[0:64, :],
                                    PS[bk].ap()[64:128, :]),
                                    reads=[f"ps{bk}"], writes=["recb"])
                                p.op("dve", lambda h, h2=h2, bk=bk:
                                     h.tensor_tensor(
                                         ADN[64 * h2:64 * h2 + 64, cq, :],
                                         PS[bk].ap()[0:64, :],
                                         RECB[0:64, :], ALU.mult),
                                     reads=[f"ps{bk}", "recb"],
                                     writes=[f"adn{cq}"])
                        return emit

                    pend = None
                    exp_hi_par = {}
                    exp_hi_kt = {}

                    def qh_load(pit2, s=s, sbk=sbk):
                        cq2 = 4 * (pit2 // 4) + pit2 % 4
                        qs2 = pit2 % 3
                        p.dma("gpsimd", QH[qs2],
                              qspill[s, cq2, :, sbk * 512:(sbk + 1) * 512],
                              writes=[f"qh{qs2}"])

                    qh_load(0)
                    qh_load(1)
                    for pit in range(16):
                        gq, hp = pit // 4, pit % 4
                        cq = 4 * gq + hp
                        qslot = pit % 3
                        bks = (4, 5) if pit % 2 == 0 else (6, 7)
                        for step in range(16 + LAG):
                            if step == 6 and pend is not None:
                                pend()
                                pend = None
                            if step == 7 and pit + 2 < 16:
                                qh_load(pit + 2)
                            if (pit == 0 and step % 4 == 3 and self._acomb):
                                self._acomb.pop(0)()
                            if step < 16:
                                kt = step
                                xd = ([exp_hi_par[kt % 2]]
                                      if kt % 2 in exp_hi_par else [])
                                for h2 in range(2):
                                    scb = 2 * (kt % 2) + h2
                                    p.op("pe", lambda h, hp=hp, h2=h2, kt=kt,
                                         scb=scb, qslot=qslot: h.matmul(
                                             PS[scb].ap(),
                                             KC[64 * h2:64 * h2 + 64, hp,
                                                kt * 128:(kt + 1) * 128],
                                             QH[qslot][64 * h2:64 * h2 + 64, :],
                                             start=True, stop=True),
                                         reads=["kc", f"qh{qslot}"],
                                         writes=[f"ps{scb}"], deps=xd)
                                pb = kt % 3
                                eid = p.op("act", lambda h, kt=kt, pb=pb:
                                           h.activation(
                                               PBW[pb],
                                               PSD[kt % 2].ap(),
                                               AF.Exp, bias=EXP_SHIFT_A,
                                               scale=SCALE_A),
                                           reads=[f"ps{2*(kt%2)}",
                                                  f"ps{2*(kt%2)+1}"],
                                           writes=[f"pb{pb}"])
                                exp_hi_par[kt % 2] = eid
                                exp_hi_kt[kt] = eid
                            if step >= LAG:
                                kt = step - LAG
                                xd = [exp_hi_kt[kt]]
                                pb = kt % 3
                                for h2 in range(2):
                                    hh = 2 * hp + h2
                                    bk = bks[h2]
                                    p.op("pe", lambda h, kt=kt, hh=hh, pb=pb,
                                         h2=h2, bk=bk: h.matmul(
                                             PS[bk].ap()[0:65, :],
                                             VT[:, kt, 65 * hh:65 * hh + 65],
                                             PBW[pb][:, 512 * h2:
                                                     512 * h2 + 512],
                                             start=(kt == 0),
                                             stop=(kt == 15)),
                                         reads=[f"pb{pb}", f"vt{kt}"],
                                         writes=[f"ps{bk}"], deps=xd)
                        pend = norm_closure(cq, bks)
                    pend()
                    self._acomb.extend(
                        make_comb(s, sbk, sub) for sub in range(4))
                for cb in self._acomb:
                    cb()
                self._acomb = []
                p.fence()

        def emit_ffn(xin, xout, tag):
            WB = AR1.ap().rearrange("p (c f) -> p c f", c=16)
            BB2 = AR2.ap()[:, 0:2048]
            SQ_SCR = AR2.ap()[:, 2048:4096]
            XN_SCR = [AR2.ap()[:, 4096 + i * 2048:4096 + (i + 1) * 2048]
                      for i in range(3)]
            XNT = [AR2.ap()[:, 10240 + i * 2048:10240 + (i + 1) * 2048]
                   .rearrange("p (c t) -> p c t", c=16) for i in range(3)]
            YTS = [AR3.ap()[:, 6144 + i * 2048:6144 + (i + 1) * 2048]
                   for i in range(2)]
            HA = AR4.ap()[:, 4096:5248].rearrange("p (c f) -> p c f", c=16)
            BF = AR4.ap()[:, 5248:7296]
            HG = AR4.ap()[:, 7424:7488]
            HGF = AR4.ap()[:, 7488:7616]
            ROT = ("gpsimd", "act")
            for c in range(16):
                q = ROT[c % 2]
                p.dma(q, WB[:, c, :], wb_d[c], writes=[f"wb{c}"], dsem=f"fwt{c%2}", dmode="coll")
                p.dma(q, HA[:, c, :], ha_d[c], writes=[f"ha{c}"], dsem=f"fwt{c%2}", dmode="coll")
            p.dma("gpsimd", BF[0:64, :], bflat_d, writes=["bflat"], dsem="fwt0", dmode="coll")
            p.dma("gpsimd", BB2[0:1, :], bb_d, writes=["bb"], dsem="fwt0", dmode="coll")

            for s in range(ns):
                def ffn_A(t, s=s):
                    xslot = t % 3
                    xn = XNT[xslot]
                    def dstf(c0, xn=xn):
                        return xn[:, c0:c0 + 8, :]
                    return ln_tile(xin, s, t, t % 3, dstf, f"xnt{xslot}",
                                   SQ_SCR, XN_SCR[t % 3], f"xnscr{t%3}",
                                   t % 2)

                def ffn_B1(t, rxt, s=s):
                    xslot = t % 3
                    xn = XNT[xslot]
                    for c in range(16):
                        p.op("pe", lambda h, c=c, xn=xn: h.matmul(
                            PS[4].ap()[:, 0:72], xn[:, c, :], HA[:, c, :],
                            start=(c == 0), stop=(c == 15)),
                            reads=[f"xnt{xslot}", f"ha{c}"], writes=["ps4"])
                    for c in range(16):
                        for j in range(4):
                            p.op("pe", lambda h, c=c, j=j, xn=xn: h.matmul(
                                PS[j].ap(), xn[:, c, :],
                                WB[:, c, j * 512:(j + 1) * 512],
                                start=(c == 0), stop=False),
                                reads=[f"xnt{xslot}", f"wb{c}"],
                                writes=[f"ps{j}"])
                    for j in range(4):
                        p.op("pe", lambda h, j=j: h.matmul(
                            PS[j].ap(), ONES.ap()[0:1, :],
                            BB2[0:1, j * 512:(j + 1) * 512],
                            start=False, stop=False),
                            reads=["bb", "ONES"], writes=[f"ps{j}"])

                def ffn_B2(t, rxt, s=s):
                    xslot = t % 3
                    xn = XNT[xslot]
                    lg = PS[4].ap()[:, 64:72]
                    p.op("dve", lambda h: h.reduce_max(st[:, 26:27], lg,
                                                       axis=AX.X),
                         reads=["ps4"], writes=["rm1"])
                    p.op("dve", lambda h: h.tensor_scalar_mul(
                        st[:, 27:28], st[:, 26:27], -1.0),
                        reads=["rm1"], writes=["rnm1"])
                    p.op("dve", lambda h: h.tensor_scalar(
                        st[:, 32:40], lg, st[:, 26:27], None, ALU.is_equal),
                        reads=["ps4", "rm1"], writes=["req"])
                    p.op("dve", lambda h: h.scalar_tensor_tensor(
                        st[:, 32:40], st[:, 32:40], -1e30, lg,
                        ALU.mult, ALU.add),
                        reads=["req", "ps4"], writes=["rmsk"])
                    p.op("dve", lambda h: h.reduce_max(st[:, 28:29],
                                                       st[:, 32:40], axis=AX.X),
                         reads=["rmsk"], writes=["rm2"])
                    p.op("act", lambda h: h.activation(st[:, 40:48], lg,
                                                       AF.Exp,
                                                       bias=st[:, 27:28]),
                         reads=["ps4", "rnm1"], writes=["ru"])
                    p.op("dve", lambda h: h.tensor_scalar(
                        st[:, 48:56], lg, st[:, 28:29], None, ALU.is_ge),
                        reads=["ps4", "rm2"], writes=["rkeep"])
                    p.op("dve", lambda h: h.tensor_mul(
                        st[:, 40:48], st[:, 40:48], st[:, 48:56]),
                        reads=["ru", "rkeep"], writes=["rw8"])
                    p.op("dve", lambda h: h.reduce_sum(st[:, 29:30],
                                                       st[:, 40:48], axis=AX.X),
                         reads=["rw8"], writes=["rsg"])
                    p.op("dve", lambda h: h.reciprocal(st[:, 30:31],
                                                       st[:, 29:30]),
                         reads=["rsg"], writes=["rrec"])
                    p.op("dve", lambda h: h.tensor_scalar_mul(
                        st[:, 30:31], st[:, 30:31], 2.0),
                        reads=["rrec"], writes=["rrec"])
                    p.op("dve", lambda h: h.tensor_scalar_mul(
                        st[:, 40:48], st[:, 40:48], st[:, 30:31]),
                        reads=["rw8", "rrec"], writes=["rg8"])
                    p.op("dve", lambda h: h.tensor_tensor(
                        HG.rearrange("p (e r) -> p e r", e=8),
                        PS[4].ap()[:, 0:64].rearrange("p (e r) -> p e r", e=8),
                        st[:, 40:48].unsqueeze(2).broadcast_to([128, 8, 8]),
                        ALU.mult),
                        reads=["ps4", "rg8"], writes=["hg"])
                    p.op("pe", lambda h: h.transpose(psb(5)[0:64, 0:128], HG,
                                                     IDENT.ap()),
                         reads=["hg", "IDENT"], writes=["ps5"])
                    p.op("dve", lambda h: h.tensor_copy(
                        HGF[0:64, :], psb(5)[0:64, 0:128]),
                        reads=["ps5"], writes=["hgf"])
                    for j in range(4):
                        p.op("pe", lambda h, j=j: h.matmul(
                            PS[j].ap(), HGF[0:64, :],
                            BF[0:64, j * 512:(j + 1) * 512],
                            start=False, stop=True),
                            reads=["hgf", "bflat"], writes=[f"ps{j}"])
                    xt = xt_ring[t % 3]
                    yt = YTS[t % 2]
                    for j in range(4):
                        p.op("dve", lambda h, j=j, yt=yt, xt=xt: h.tensor_add(
                            yt[:, j * 512:(j + 1) * 512], PS[j].ap(),
                            xt[:, j * 512:(j + 1) * 512]),
                            reads=[f"ps{j}", rxt], writes=[f"yts{t%2}"])
                    p.dma("sp", xout[s, t * 128:(t + 1) * 128, :], yt,
                          reads=[f"yts{t%2}"], dsem=f"yso{t%2}")

                rxts = {}
                rxts[0] = ffn_A(0)
                rxts[1] = ffn_A(1)
                for t in range(NT):
                    ffn_B1(t, rxts[t])
                    if t + 2 < NT:
                        rxts[t + 2] = ffn_A(t + 2)
                    ffn_B2(t, rxts[t])
            p.fence()

        def emit_attn_b(xin, xout):
            WQA = AR1.ap()[:, 0:24576].rearrange("p (c f) -> p c f", c=16)
            CKV = AR1.ap()[:, 24576:32768].rearrange("p (c t) -> p c t", c=4)
            QAN = AR2.ap()[:, 0:24576].rearrange("p (c t) -> p c t", c=12)
            KPE = AR2.ap()[:, 24576:26624]
            QASC = AR2.ap()[:, 26624:28160]
            CKSC = AR2.ap()[:, 28160:28800]
            AOSC = [AR2.ap()[:, 28800 + i * 512:28800 + (i + 1) * 512]
                    for i in range(2)]
            XNT = [AR4.ap()[:, i * 2048:(i + 1) * 2048].rearrange(
                "p (c t) -> p c t", c=16) for i in range(2)]
            SQ_SCR = AR4.ap()[:, 4096:6144]
            XN_SCRB = AR4.ap()[:, 6144:8192]
            WKVA = AR3.ap()[:, 4096:10240].bitcast(BF16)
            WKVA5 = WKVA[:, 0:8192].rearrange("p (c f) -> p c f", c=16)
            WKVAP = WKVA[:, 8192:9216].rearrange("p (c f) -> p c f", c=16)
            KPSC = WKVA[:, 9216:9280]
            KPSC2 = WKVA[:, 9280:9344]
            LNWQ = st[:, 32:44]
            LNWK = st[:, 44:48]
            p.dma("sp", LNWQ, lnwq_d, writes=["lnwq"], dsem="bw1s", dmode="coll")
            p.dma("sp", LNWK, lnwk_d, writes=["lnwk"], dsem="bw1s", dmode="coll")

            for s in range(ns):
                ROT = ("gpsimd", "act")
                for c in range(16):
                    q = ROT[c % 2]
                    p.dma(q, WQA[:, c, :], wqa[c], writes=[f"wqa{c}"], dsem=f"bw1_{c%2}", dmode="coll")
                    p.dma(ROT[(c + 1) % 2], WKVA5[:, c, :], wkva5[c],
                          writes=[f"wkva{c}"], dsem=f"bw1_{(c+1)%2}", dmode="coll")
                    p.dma(ROT[(c + 1) % 2], WKVAP[:, c, :], wkvap[c],
                          writes=[f"wkvap{c}"], dsem=f"bw1_{(c+1)%2}", dmode="coll")
                def b1_A(t, s=s):
                    xslot = t % 2
                    xn = XNT[xslot]
                    def dstf(c0, xn=xn):
                        return xn[:, c0:c0 + 8, :]
                    ln_tile(xin, s, t, t % 2, dstf, f"xnt{xslot}", SQ_SCR,
                            XN_SCRB, "xnscrB", t % 2)

                def b1_B1(t, s=s):
                    xslot = t % 2
                    xn = XNT[xslot]
                    for c in range(16):
                        for j in range(3):
                            p.op("pe", lambda h, c=c, j=j, xn=xn: h.matmul(
                                PS[j].ap(), xn[:, c, :],
                                WQA[:, c, j * 512:(j + 1) * 512],
                                start=(c == 0), stop=(c == 15)),
                                reads=[f"xnt{xslot}", f"wqa{c}"],
                                writes=[f"ps{j}"])
                        p.op("pe", lambda h, c=c, xn=xn: h.matmul(
                            PS[3].ap(), xn[:, c, :], WKVA5[:, c, :],
                            start=(c == 0), stop=(c == 15)),
                            reads=[f"xnt{xslot}", f"wkva{c}"], writes=["ps3"])
                        p.op("pe", lambda h, c=c, xn=xn: h.matmul(
                            PS[4].ap()[:, 0:64], xn[:, c, :], WKVAP[:, c, :],
                            start=(c == 0), stop=(c == 15)),
                            reads=[f"xnt{xslot}", f"wkvap{c}"], writes=["ps4"])

                def b1_B2(t, s=s):
                    xslot = t % 2
                    xn = XNT[xslot]
                    for j in range(3):
                        p.op("act", lambda h, j=j: h.activation(
                            SQ_SCR[:, j * 512:(j + 1) * 512], PS[j].ap(),
                            AF.Square, accum_out=st[:, 24 + j:25 + j]),
                            reads=[f"ps{j}"], writes=[f"qsq{j}", "SQSCR"])
                    p.op("dve", lambda h: h.tensor_add(
                        st[:, 27:28], st[:, 24:25], st[:, 25:26]),
                        reads=["qsq0", "qsq1"], writes=["qss"])
                    p.op("dve", lambda h: h.tensor_add(
                        st[:, 27:28], st[:, 27:28], st[:, 26:27]),
                        reads=["qss", "qsq2"], writes=["qss"])
                    p.op("act", lambda h: h.activation(
                        st[:, 28:29], st[:, 27:28], AF.Ln, bias=RMS_EPS,
                        scale=1.0 / 1536), reads=["qss"], writes=["qln"])
                    p.op("act", lambda h: h.activation(
                        st[:, 29:30], st[:, 28:29], AF.Exp, scale=-0.5),
                        reads=["qln"], writes=["qrstd"])
                    for j in range(3):
                        p.op("dve", lambda h, j=j: h.tensor_scalar_mul(
                            QASC[:, j * 512:(j + 1) * 512], PS[j].ap(),
                            st[:, 29:30]),
                            reads=[f"ps{j}", "qrstd"], writes=["qasc"])
                    p.op("act", lambda h: h.activation(
                        SQ_SCR[:, 1536:2048], PS[3].ap(), AF.Square,
                        accum_out=st[:, 30:31]), reads=["ps3"],
                        writes=["csq", "SQSCR"])
                    p.op("act", lambda h: h.activation(
                        st[:, 31:32], st[:, 30:31], AF.Ln, bias=RMS_EPS,
                        scale=1.0 / 512), reads=["csq"], writes=["cln"])
                    p.op("act", lambda h: h.activation(
                        st[:, 48:49], st[:, 31:32], AF.Exp, scale=-0.5),
                        reads=["cln"], writes=["crstd"])
                    p.op("dve", lambda h: h.tensor_scalar_mul(
                        CKSC[:, 0:512], PS[3].ap(), st[:, 48:49]),
                        reads=["ps3", "crstd"], writes=["cksc"])
                    p.op("act", lambda h: h.copy(KPSC, PS[4].ap()[:, 0:64]),
                         reads=["ps4"], writes=["kpsc"])
                    cosb = COS.ap()[:, t, :]
                    sinb = SIN.ap()[:, t, :]
                    p.op("dve", lambda h, cosb=cosb: h.tensor_tensor(
                        KPSC2, KPSC, cosb, ALU.mult),
                        reads=["kpsc", "COS"], writes=["kpsc2"])
                    p.op("dve", lambda h, sinb=sinb: h.tensor_tensor(
                        CKSC[:, 512:544], KPSC[:, 32:64], sinb[:, 0:32],
                        ALU.mult), reads=["kpsc", "SIN"], writes=["kpT1"])
                    p.op("dve", lambda h, sinb=sinb: h.tensor_tensor(
                        CKSC[:, 544:576], KPSC[:, 0:32], sinb[:, 32:64],
                        ALU.mult), reads=["kpsc", "SIN"], writes=["kpT2"])
                    p.op("dve", lambda h: h.tensor_sub(
                        KPSC2[:, 0:32], KPSC2[:, 0:32], CKSC[:, 512:544]),
                        reads=["kpT1", "kpsc2"], writes=["kpsc2"])
                    p.op("dve", lambda h: h.tensor_add(
                        KPSC2[:, 32:64], KPSC2[:, 32:64], CKSC[:, 544:576]),
                        reads=["kpT2", "kpsc2"], writes=["kpsc2"])
                    for half in range(2):
                        bank = 6 + half
                        for j in range(8):
                            c = half * 8 + j
                            if c < 12:
                                src = QASC[:, c * 128:(c + 1) * 128]
                                rr = "qasc"
                            elif c < 16:
                                src = CKSC[:, (c - 12) * 128:(c - 11) * 128]
                                rr = "cksc"
                            p.op("pe", lambda h, src=src, j=j, bank=bank:
                                 h.transpose(psb(bank)[:, j * 128:(j + 1) * 128],
                                             src, IDENT.ap()),
                                 reads=[rr, "IDENT"], writes=[f"ps{bank}"])
                        if half == 0:
                            p.op("dve", lambda h, t=t: h.tensor_tensor(
                                QAN[:, 0:8, t * 128:(t + 1) * 128],
                                psb(6).rearrange("p (c u) -> p c u", c=8),
                                LNWQ[:, 0:8].unsqueeze(2).broadcast_to(
                                    [128, 8, 128]), ALU.mult),
                                reads=["ps6", "lnwq"], writes=["qan"])
                        else:
                            p.op("dve", lambda h, t=t: h.tensor_tensor(
                                QAN[:, 8:12, t * 128:(t + 1) * 128],
                                psb(7).rearrange("p (c u) -> p c u", c=8)[:, 0:4, :],
                                LNWQ[:, 8:12].unsqueeze(2).broadcast_to(
                                    [128, 4, 128]), ALU.mult),
                                reads=["ps7", "lnwq"], writes=["qan"])
                            p.op("dve", lambda h, t=t: h.tensor_tensor(
                                CKV[:, 0:4, t * 128:(t + 1) * 128],
                                psb(7).rearrange("p (c u) -> p c u", c=8)[:, 4:8, :],
                                LNWK[:, 0:4].unsqueeze(2).broadcast_to(
                                    [128, 4, 128]), ALU.mult),
                                reads=["ps7", "lnwk"], writes=["ckv"])
                    p.op("pe", lambda h: h.transpose(psb(5)[0:64, 0:128],
                                                     KPSC2, IDENT.ap()),
                         reads=["kpsc2", "IDENT"], writes=["ps5"])
                    p.op("dve", lambda h, t=t: h.tensor_copy(
                        KPE[0:64, t * 128:(t + 1) * 128],
                        psb(5)[0:64, 0:128]),
                        reads=["ps5"], writes=["kpe"])
                    p.op("dve", lambda h, t=t: h.tensor_copy(
                        KPE[64:128, t * 128:(t + 1) * 128],
                        psb(5)[0:64, 0:128]),
                        reads=["ps5"], writes=["kpe"])

                b1_A(0)
                for t in range(NT):
                    b1_B1(t)
                    if t + 1 < NT:
                        b1_A(t + 1)
                    b1_B2(t)
                p.fence()
                if self.debug_taps and s == 0:
                    qantap, ckvtap, kpetap = self._taps
                    for c in range(12):
                        p.dma("sp", qantap[c], QAN[:, c, :], dsem="tapq",
                              dmode="coll")
                    for c in range(4):
                        p.dma("sp", ckvtap[c], CKV[:, c, :], dsem="tapq",
                              dmode="coll")
                    p.dma("sp", kpetap, KPE, dsem="tapq", dmode="coll")
                    p.fence()
                WQBP = AR1.ap()[:, 0:12288].rearrange("p (c f) -> p c f", c=12)
                QPT = AR1.ap()[:, 12288:13312]
                QPR = AR1.ap()[:, 13312:14336]
                TSC = AR1.ap()[:, 14336:14848]
                QPF4 = AR1.ap()[:, 14848:18944].rearrange(
                    "p (c t) -> p c t", c=8)
                for c in range(12):
                    p.dma(("gpsimd", "act")[c % 2], WQBP[:, c, :],
                          wqbp[c], writes=[f"wqbp{c}"], dsem=f"bw2_{c%2}",
                          dmode="coll")
                for t in range(NT):
                    for c in range(12):
                        for j in range(2):
                            p.op("pe", lambda h, c=c, j=j, t=t: h.matmul(
                                PS[j].ap(), QAN[:, c, t * 128:(t + 1) * 128],
                                WQBP[:, c, j * 512:(j + 1) * 512],
                                start=(c == 0), stop=(c == 11)),
                                reads=["qan", f"wqbp{c}"], writes=[f"ps{j}"])
                    for j in range(2):
                        p.op("act", lambda h, j=j: h.copy(
                            QPT[:, j * 512:(j + 1) * 512], PS[j].ap()),
                            reads=[f"ps{j}"], writes=["qpt"])
                    cosb = COS.ap()[:, t, :].unsqueeze(1)
                    sinb = SIN.ap()[:, t, :].unsqueeze(1)
                    qpt3 = QPT.rearrange("p (hh u) -> p hh u", hh=16)
                    qpr3 = QPR.rearrange("p (hh u) -> p hh u", hh=16)
                    tsc3 = TSC.rearrange("p (hh u) -> p hh u", hh=16)
                    p.op("dve", lambda h, cosb=cosb: h.tensor_tensor(
                        qpr3, qpt3, cosb.broadcast_to([128, 16, 64]),
                        ALU.mult), reads=["qpt", "COS"], writes=["qpr"])
                    p.op("dve", lambda h, sinb=sinb: h.tensor_tensor(
                        tsc3[:, :, 0:32], qpt3[:, :, 32:64],
                        sinb.broadcast_to([128, 16, 64])[:, :, 0:32],
                        ALU.mult), reads=["qpt", "SIN"], writes=["qtsc"])
                    p.op("dve", lambda h: h.tensor_sub(
                        qpr3[:, :, 0:32], qpr3[:, :, 0:32], tsc3[:, :, 0:32]),
                        reads=["qtsc", "qpr"], writes=["qpr"])
                    p.op("dve", lambda h, sinb=sinb: h.tensor_tensor(
                        tsc3[:, :, 0:32], qpt3[:, :, 0:32],
                        sinb.broadcast_to([128, 16, 64])[:, :, 32:64],
                        ALU.mult), reads=["qpt", "SIN"], writes=["qtsc"])
                    p.op("dve", lambda h: h.tensor_add(
                        qpr3[:, :, 32:64], qpr3[:, :, 32:64], tsc3[:, :, 0:32]),
                        reads=["qtsc", "qpr"], writes=["qpr"])
                    bank = 5
                    for j in range(8):
                        p.op("pe", lambda h, j=j: h.transpose(
                            psb(bank)[:, j * 128:(j + 1) * 128],
                            QPR[:, j * 128:(j + 1) * 128], IDENT.ap()),
                            reads=["qpr", "IDENT"], writes=[f"ps{bank}"])
                    p.op("dve", lambda h, t=t: h.tensor_copy(
                        QPF4[:, :, (t % 4) * 128:(t % 4 + 1) * 128],
                        psb(bank).rearrange("p (c u) -> p c u", c=8)),
                        reads=[f"ps{bank}"], writes=[f"qpf4_{t%4}"])
                    if t % 4 == 3:
                        p.dma("sp",
                              qpespill[:, :, (t - 3) * 128:(t + 1) * 128]
                              .rearrange("c p u -> p c u"),
                              QPF4[:, :, :],
                              reads=["qpf4_0", "qpf4_1", "qpf4_2", "qpf4_3"],
                              writes=[f"qpesp_{t//4}"], dsem="qpst")
                p.fence()
                WKVBN_SB = AR1.ap()[:, 0:8192].rearrange(
                    "p (hh c u) -> p hh c u", hh=16, c=4)
                WKVBV_SB = AR1.ap()[:, 8192:16384].rearrange(
                    "p (c f) -> p c f", c=4)
                VT4 = AR1.ap()[:, 16384:24576].rearrange(
                    "p (t f) -> p t f", t=16)
                WQBN_ST = [AR4.ap()[:, i * 1536:(i + 1) * 1536].rearrange(
                    "p (c u) -> p c u", c=12) for i in range(2)]
                PBW2 = [AR4.ap()[:, 3072 + i * 1024:3072 + (i + 1) * 1024]
                        for i in range(2)]
                QPEH = [AR4.ap()[:, 5120 + i * 512:5120 + (i + 1) * 512]
                        for i in range(2)]
                QNH = [AR3.ap()[:, i * 1024:(i + 1) * 1024].bitcast(BF16)
                       for i in range(2)]
                KNH = [AR3.ap()[:, 2048 + i * 1024:2048 + (i + 1) * 1024]
                       .bitcast(BF16) for i in range(2)]
                RECB = AR5.ap()[:, 0:512]
                REC1S = [AR5.ap()[:, 512:1024], AR5.ap()[:, 1024:1536]]
                for c in range(4):
                    p.dma(("gpsimd", "act")[c % 2], WKVBV_SB[:, c, :],
                          wkvbv[c], writes=[f"wkvbv{c}"], dsem=f"bw3_{c%2}",
                          dmode="coll")
                for hh in range(16):
                    p.dma(("gpsimd", "act")[hh % 2],
                          WKVBN_SB[:, hh, :, :],
                          wkvbn[hh].rearrange("c p u -> p c u"),
                          writes=[f"wkvbn{hh}"], dsem=f"bw3_{hh%2}",
                          dmode="coll")
                self._bpend = None
                p.dma("gpsimd", WQBN_ST[0], wqbn[0], writes=["wqbnld0"])
                for hg in range(4):
                    for t in range(NT):
                        bk = t % 2
                        for c in range(4):
                            p.op("pe", lambda h, c=c, t=t, bk=bk, hg=hg:
                                 h.matmul(PS[bk].ap(),
                                          CKV[:, c, t * 128:(t + 1) * 128],
                                          WKVBV_SB[:, c,
                                                   hg * 512:(hg + 1) * 512],
                                          start=(c == 0), stop=(c == 3)),
                                 reads=["ckv", f"wkvbv{c}"],
                                 writes=[f"ps{bk}"])
                        p.op("dve", lambda h, bk=bk, t=t: h.tensor_copy(
                            VT4[:, t, :], PS[bk].ap()),
                            reads=[f"ps{bk}"], writes=[f"vt4_{t}"])
                    for hl in range(4):
                        hh = hg * 4 + hl
                        hp = hh // 2
                        h2 = hh % 2
                        ws = hh % 2
                        if hh + 1 < 16:
                            p.dma(("gpsimd", "act")[(hh + 1) % 2],
                                  WQBN_ST[(hh + 1) % 2],
                                  wqbn[hh + 1],
                                  writes=[f"wqbnld{(hh + 1) % 2}"])
                        for tb in range(4):
                            bk = 4 if tb % 2 == 0 else 2
                            for c in range(12):
                                p.op("pe", lambda h, c=c, tb=tb, bk=bk, ws=ws:
                                     h.matmul(
                                         PS[bk].ap(),
                                         WQBN_ST[ws][:, c, :],
                                         QAN[:, c, tb * 512:(tb + 1) * 512],
                                         start=(c == 0), stop=(c == 11)),
                                     reads=[f"wqbnld{ws}", "qan"],
                                     writes=[f"ps{bk}"])
                            p.op("dve", lambda h, bk=bk, tb=tb, ws=ws:
                                 h.tensor_copy(
                                     QNH[ws][:, tb * 512:(tb + 1) * 512],
                                     PS[bk].ap()),
                                 reads=[f"ps{bk}"], writes=[f"qnh{ws}"])
                        for tb in range(4):
                            bk = 4 if tb % 2 == 0 else 2
                            for c in range(4):
                                p.op("pe", lambda h, c=c, tb=tb, bk=bk, hh=hh:
                                     h.matmul(
                                         PS[bk].ap(),
                                         WKVBN_SB[:, hh, c, :],
                                         CKV[:, c, tb * 512:(tb + 1) * 512],
                                         start=(c == 0), stop=(c == 3)),
                                     reads=[f"wkvbn{hh}", "ckv"],
                                     writes=[f"ps{bk}"])
                            p.op("dve", lambda h, bk=bk, tb=tb, ws=ws:
                                 h.tensor_copy(
                                     KNH[ws][:, tb * 512:(tb + 1) * 512],
                                     PS[bk].ap()),
                                 reads=[f"ps{bk}"], writes=[f"knh{ws}"])
                        LAGB = 1
                        for sbk in range(4):
                            qslot = sbk % 2
                            obk = 6 if sbk % 2 == 0 else 7
                            rsb = 5
                            p.dma("sp", QPEH[qslot][0:64, :],
                                  qpespill[hp, 64 * h2:64 * h2 + 64,
                                           sbk * 512:(sbk + 1) * 512],
                                  reads=[f"qpesp_{sbk}"],
                                  writes=[f"qpeh{qslot}"])
                            p.dma("sp", QPEH[qslot][64:128, :],
                                  qpespill[hp, 64 * h2:64 * h2 + 64,
                                           sbk * 512:(sbk + 1) * 512],
                                  reads=[f"qpesp_{sbk}"],
                                  writes=[f"qpeh{qslot}"])
                            nkt = (sbk + 1) * 4
                            exp_last = {}
                            for step in range(nkt // 2 + LAGB):
                                if step < nkt // 2:
                                    k0 = 2 * step
                                    p2 = (k0 // 2) % 2
                                    xd = ([exp_last[k0 - 4]]
                                          if k0 - 4 in exp_last else [])
                                    for dk in range(2):
                                        kt = k0 + dk
                                        scb = 2 * p2 + dk
                                        p.op("pe", lambda h, kt=kt, scb=scb,
                                             ws=ws, sbk=sbk: h.matmul(
                                                 PS[scb].ap(),
                                                 KNH[ws][:, kt * 128:(kt + 1) * 128],
                                                 QNH[ws][:, sbk * 512:(sbk + 1) * 512],
                                                 start=True, stop=False),
                                             reads=[f"knh{ws}", f"qnh{ws}"],
                                             writes=[f"ps{scb}"], deps=xd)
                                    for dk in range(2):
                                        kt = k0 + dk
                                        scb = 2 * p2 + dk
                                        p.op("pe", lambda h, kt=kt, scb=scb,
                                             dk=dk, qslot=qslot: h.matmul(
                                                 PS[scb].ap(),
                                                 KPE[64 * dk:64 * dk + 64,
                                                     kt * 128:(kt + 1) * 128],
                                                 QPEH[qslot][64 * dk:64 * dk + 64, :],
                                                 start=False, stop=True),
                                             reads=["kpe", f"qpeh{qslot}"],
                                             writes=[f"ps{scb}"])
                                    pb = (k0 // 2) % 2
                                    eid = p.op("act", lambda h, p2=p2, pb=pb:
                                               h.activation(
                                                   PBW2[pb],
                                                   PSD[p2].ap(),
                                                   AF.Exp, scale=SCALE_B),
                                               reads=[f"ps{2 * p2}",
                                                      f"ps{2 * p2 + 1}"],
                                               writes=[f"pb{pb}"])
                                    exp_last[k0] = eid
                                    exp_last[k0 + 1] = eid
                                    for dk in range(2):
                                        kt = k0 + dk
                                        if kt >= nkt - 4:
                                            base = sbk * 512 - kt * 128
                                            p.op("gpsimd",
                                                 lambda h, pb=pb, dk=dk,
                                                 base=base:
                                                 h.affine_select(
                                                     PBW2[pb][:, 512 * dk:
                                                              512 * dk + 512],
                                                     PBW2[pb][:, 512 * dk:
                                                              512 * dk + 512],
                                                     [[1, 512]],
                                                     ALU.is_ge, 0.0,
                                                     base=base,
                                                     channel_multiplier=-1),
                                                 reads=[f"pb{pb}"],
                                                 writes=[f"pb{pb}"])
                                if step == 0 and self._bpend is not None:
                                    self._bpend[0]()
                                if step == 2 and self._bpend is not None:
                                    self._bpend[1]()
                                    self._bpend = None
                                if step >= LAGB:
                                    k0 = 2 * (step - LAGB)
                                    xd = [exp_last[k0 + 1]]
                                    pb = (k0 // 2) % 2
                                    for dk in range(2):
                                        kt = k0 + dk
                                        p.op("pe", lambda h, kt=kt, pb=pb,
                                             dk=dk, hl=hl, nkt=nkt, obk=obk:
                                             h.matmul(
                                                 PS[obk].ap(),
                                                 VT4[:, kt,
                                                     hl * 128:(hl + 1) * 128],
                                                 PBW2[pb][:, 512 * dk:
                                                          512 * dk + 512],
                                                 start=(kt == 0),
                                                 stop=(kt == nkt - 1)),
                                             reads=[f"pb{pb}", f"vt4_{kt}"],
                                             writes=[f"ps{obk}"], deps=xd)
                                        p.op("pe", lambda h, kt=kt, pb=pb,
                                             dk=dk, nkt=nkt, rsb=rsb:
                                             h.matmul(
                                                 PS[rsb].ap()[0:1, :],
                                                 ONES.ap()[:, 0:1],
                                                 PBW2[pb][:, 512 * dk:
                                                          512 * dk + 512],
                                                 start=(kt == 0),
                                                 stop=(kt == nkt - 1)),
                                             reads=[f"pb{pb}", "ONES"],
                                             writes=[f"ps{rsb}"])

                            def bnorm(hh=hh, sbk=sbk, rsb=rsb, obk=obk,
                                      rslt=sbk % 2, asl=sbk % 2):
                                def emit1():
                                    p.op("act", lambda h: h.activation(
                                        REC1S[rslt][0:1, :],
                                        PS[rsb].ap()[0:1, :], AF.Ln),
                                        reads=[f"ps{rsb}"],
                                        writes=[f"rec1_{rslt}"])
                                    p.op("act", lambda h: h.activation(
                                        REC1S[rslt][0:1, :],
                                        REC1S[rslt][0:1, :], AF.Exp,
                                        scale=-1.0),
                                        reads=[f"rec1_{rslt}"],
                                        writes=[f"rec1_{rslt}"])
                                def emit2():
                                    p.op("pe", lambda h: h.matmul(
                                        PS[4].ap(), ONESF.ap()[0:1, :],
                                        REC1S[rslt][0:1, :],
                                        start=True, stop=True),
                                        reads=[f"rec1_{rslt}", "ONESF"],
                                        writes=["ps4"])
                                    p.op("act", lambda h: h.copy(
                                        RECB, PS[4].ap()),
                                        reads=["ps4"], writes=["recb"])
                                    p.op("dve", lambda h: h.tensor_tensor(
                                        AOSC[asl], PS[obk].ap(), RECB,
                                        ALU.mult),
                                        reads=[f"ps{obk}", "recb"],
                                        writes=[f"aosc{asl}"])
                                    p.dma("sp",
                                          aofspill[hh, :,
                                                   sbk * 512:(sbk + 1) * 512],
                                          AOSC[asl], reads=[f"aosc{asl}"],
                                          writes=[f"aofsp_{hh}_{sbk}"],
                                          dsem=f"aos{asl}")
                                return (emit1, emit2)
                            self._bpend = bnorm()
                if self._bpend is not None:
                    self._bpend[0]()
                    self._bpend[1]()
                    self._bpend = None
                p.fence()
                WO_SB = AR1.ap().rearrange("p (c f) -> p c f", c=16)
                AOF_SB = AR2.ap()[:, 0:16384].rearrange(
                    "p (r c t) -> p r c t", r=2, c=16)
                for c in range(16):
                    p.dma(("gpsimd", "act")[c % 2], WO_SB[:, c, :],
                          wo_d[c], writes=[f"wo{c}"], dsem=f"bwo{c%2}",
                          dmode="coll")
                for sbk in range(4):
                    rr = sbk % 2
                    for c in range(16):
                        p.dma("sp", AOF_SB[:, rr, c, :],
                              aofspill[c, :, sbk * 512:(sbk + 1) * 512],
                              reads=[f"aofsp_{c}_{sbk}"],
                              writes=[f"aofsb{rr}_{c}"],
                              dsem=f"aofl{rr}", dmode="coll")
                    for sub in range(4):
                        t = sbk * 4 + sub
                        xslot = t % 3
                        xtile = xt_ring[xslot]
                        ytile = AR3.ap()[:, 6144 + (t % 2) * 2048:
                                         6144 + (t % 2 + 1) * 2048]
                        p.dma("sp", xtile, xin[s, t * 128:(t + 1) * 128, :],
                              writes=[f"xt{xslot}"])
                        for c in range(16):
                            for j in range(4):
                                p.op("pe", lambda h, c=c, j=j, sub=sub,
                                     rr=rr: h.matmul(
                                         PS[j].ap(),
                                         AOF_SB[:, rr, c,
                                                sub * 128:(sub + 1) * 128],
                                         WO_SB[:, c, j * 512:(j + 1) * 512],
                                         start=(c == 0), stop=(c == 15)),
                                     reads=[f"aofsb{rr}_{c}", f"wo{c}"],
                                     writes=[f"ps{j}"])
                        for j in range(4):
                            p.op("dve", lambda h, j=j, xtile=xtile,
                                 ytile=ytile: h.tensor_add(
                                     ytile[:, j * 512:(j + 1) * 512],
                                     PS[j].ap(),
                                     xtile[:, j * 512:(j + 1) * 512]),
                                 reads=[f"ps{j}", f"xt{xslot}"],
                                 writes=[f"yts{t%2}"])
                        p.dma("sp", xout[s, t * 128:(t + 1) * 128, :], ytile,
                              reads=[f"yts{t%2}"], dsem=f"yso{t%2}")
                p.fence()

        subs = list(self.subs)
        chain_in = {0: x_in, 1: xA, 2: xB, 3: xA}
        chain_out = {0: xA, 1: xB, 2: xA, 3: y_out}
        for i in subs:
            xi = x_in if i == subs[0] else chain_in[i]
            if i == subs[-1] and i != 0:
                xo = y_out
            else:
                xo = chain_out[i]
            if i == 0:
                emit_attn_a(xi, xo)
            elif i == 2:
                emit_attn_b(xi, xo)
            else:
                emit_ffn(xi, xo, f"ffn{i}")
        if subs[-1] == 0:
            for s in range(ns):
                for t in range(NT):
                    xt = xt_ring[t % 2]
                    p.dma("sp", xt, xA[s, t * 128:(t + 1) * 128, :],
                          writes=[f"cp{t%2}"])
                    p.dma("sp", y_out[s, t * 128:(t + 1) * 128, :], xt,
                          reads=[f"cp{t%2}"], writes=[f"cpd{t%2}"],
                          dsem=f"cps{t%2}")

        p.emit(nc)
        ctx.close()



def host_layouts(I):
    f32 = np.float32
    out = {}
    out["wq"] = np.ascontiguousarray(
        I["L0_attn_a_Wq"].T.reshape(16, 128, 4, 512).transpose(2, 0, 1, 3))
    out["wk"] = np.ascontiguousarray(I["L0_attn_a_Wk"].T.reshape(16, 128, 512))
    out["wv"] = np.ascontiguousarray(I["L0_attn_a_Wv"].T.reshape(16, 128, 512))
    out["mem"] = np.ascontiguousarray(
        I["L0_attn_a_memory"].reshape(4, 128, 512))
    out["memn"] = np.ascontiguousarray(
        I["L0_attn_a_memory_norm"].reshape(4, 128, 1))
    out["memw"] = I["L0_attn_a_memory_weight"].reshape(1, 1).astype(f32)
    out["wqa"] = np.ascontiguousarray(I["L0_attn_b_Wqa"].reshape(16, 128, 1536))
    out["lnwq"] = np.ascontiguousarray(
        I["L0_attn_b_q_a_ln_w"].reshape(12, 128).T)
    Wqb = I["L0_attn_b_Wqb"].reshape(12, 128, 16, 192)
    out["wqbn"] = np.ascontiguousarray(Wqb[:, :, :, :128].transpose(2, 1, 0, 3))
    out["wqbp"] = np.ascontiguousarray(
        Wqb[:, :, :, 128:].reshape(12, 128, 16 * 64))
    Wkva = I["L0_attn_b_Wkva"].reshape(16, 128, 576)
    out["wkva5"] = np.ascontiguousarray(Wkva[:, :, 0:512])
    out["wkvap"] = np.ascontiguousarray(Wkva[:, :, 512:576])
    out["lnwk"] = np.ascontiguousarray(
        I["L0_attn_b_kv_a_ln_w"].reshape(4, 128).T)
    Wkvb = I["L0_attn_b_Wkvb"].reshape(4, 128, 16, 256)
    out["wkvbn"] = np.ascontiguousarray(Wkvb[:, :, :, :128].transpose(2, 0, 1, 3))
    out["wkvbv"] = np.ascontiguousarray(
        Wkvb[:, :, :, 128:].reshape(4, 128, 16 * 128))
    out["wo"] = np.ascontiguousarray(I["L0_attn_b_Wo"].reshape(16, 128, 2048))
    out["wb"] = np.ascontiguousarray(I["L0_ffn_W_base"].T.reshape(16, 128, 2048))
    ha = np.concatenate([
        I["L0_ffn_A_w"].transpose(2, 0, 1).reshape(2048, 64),
        I["L0_ffn_W_router"].T], axis=1)
    out["ha"] = np.ascontiguousarray(ha.reshape(16, 128, 72))
    out["bb"] = I["L0_ffn_b_base"].reshape(1, 2048).astype(f32)
    out["bflat"] = np.ascontiguousarray(
        I["L0_ffn_B_w"].transpose(0, 2, 1).reshape(64, 2048))
    inv = 1.0 / 10000.0 ** (np.arange(0, 64, 2, dtype=np.float64) / 64)
    tpos = np.arange(S, dtype=np.float64)
    fr = np.outer(tpos, inv)
    emb = np.concatenate([fr, fr], axis=1)
    out["cos"] = np.ascontiguousarray(np.cos(emb).astype(f32).reshape(16, 128, 64))
    out["sin"] = np.ascontiguousarray(np.sin(emb).astype(f32).reshape(16, 128, 64))
    bf16_keys = {"wq", "wk", "wv", "mem", "wqa", "wqbn", "wqbp", "wkva5",
                 "wkvap", "wkvbn", "wkvbv", "wo", "wb", "ha", "bb", "bflat"}
    for k in out:
        dt = ml_dtypes.bfloat16 if k in bf16_keys else f32
        out[k] = np.ascontiguousarray(out[k].astype(f32).astype(dt))
    return out


_BUILDER_CACHE = {}


def get_builder(n_seq=B_LOC, subs=(0, 1, 2, 3)):
    key = (n_seq, tuple(subs))
    if key not in _BUILDER_CACHE:
        _BUILDER_CACHE[key] = Builder(n_seq, subs)
    return _BUILDER_CACHE[key]


def kernel(**inputs):
    w = host_layouts(inputs)
    x = np.ascontiguousarray(inputs["L0_attn_a_x"], dtype=np.float32)
    b = get_builder()
    in_maps = []
    for core in range(N_CORES):
        m = dict(w)
        m["x_in"] = np.ascontiguousarray(x[core * B_LOC:(core + 1) * B_LOC])
        in_maps.append(m)
    res = run_bass_kernel_spmd(b.nc, in_maps, core_ids=list(range(N_CORES)))
    return np.concatenate([r["y"] for r in res.results], axis=0)
